# version 11
# baseline (speedup 1.0000x reference)
import math
import contextlib
from collections import defaultdict

import numpy as np
import ml_dtypes

import concourse.bass as bass
import concourse.mybir as mybir
from concourse.bass_utils import run_bass_kernel_spmd

F32 = mybir.dt.float32
BF16 = mybir.dt.bfloat16
AF = mybir.ActivationFunctionType
ALU = mybir.AluOpType
AX = mybir.AxisListType

P = 128
DM = 1024
T = 4096
H = 16
D = 64
HID = 4096
NE = 16
QG = 512
NQG = T // QG
QG3 = 256
NQG3 = T // QG3
N_PAIR = 8
N_QT = 32
CLN = math.log(8.0 / math.log1p(64.0))
EPS_LN = 1e-5
NCORE = 8
PIPE_MODE = 2


class Prog:

    ENGS = ("sync", "gpsimd", "tensor", "scalar", "vector")

    def __init__(self, nc):
        self.nc = nc
        self.ops = {e: [] for e in self.ENGS}
        self.cnt = defaultdict(int)
        self.sems = {}
        self._es = contextlib.ExitStack()

    def sem(self, name):
        if name not in self.sems:
            self.sems[name] = self._es.enter_context(self.nc.semaphore(name))
        return self.sems[name]

    def op(self, eng, fn, waits=(), inc=None):
        incs = []
        if inc:
            for (s, k) in (inc if isinstance(inc, list) else [inc]):
                self.cnt[s] += k
                incs.append((s, k))
        self.ops[eng].append((fn, [(s, v) for (s, v) in waits if v > 0], incs))

    def dma(self, eng, out, in_, waits=(), inc=None, **kw):
        def fn(e, out=out, in_=in_, kw=kw):
            return e.dma_start(out=out, in_=in_, **kw)
        self.op(eng, fn, waits, inc)

    def at(self, s):
        return self.cnt[s]

    def snap(self):
        return dict(self.cnt)

    def emit(self):
        with self.nc.Block() as block:
            for eng in self.ENGS:
                ops = self.ops[eng]
                if not ops:
                    continue
                def body(e, ops=ops):
                    seen = defaultdict(int)
                    for fn, waits, incs in ops:
                        for (s, v) in waits:
                            if v > seen[s]:
                                e.wait_ge(self.sems[s], v)
                                seen[s] = v
                        inst = fn(e)
                        for (s, k) in incs:
                            inst.then_inc(self.sems[s], k)
                getattr(block, eng)(body)
        self.ops = {e: [] for e in self.ENGS}


def build_program():
    nc = bass.Bass("TRN2", target_bir_lowering=False)

    def din(name, shape, dt=F32):
        return nc.dram_tensor(name, shape, dt, kind="ExternalInput").ap()

    xT = din("xT", [DM, T])
    wq = din("wq", [DM, DM]); wk = din("wk", [DM, DM])
    wv = din("wv", [DM, DM]); wo = din("wo", [DM, DM])
    w1T = din("w1T", [DM, HID]); w2T = din("w2T", [HID, DM])
    wr = din("wr", [DM, NE])
    brow = din("brow", [1, NE])
    bq = din("bq", [P, 8]); bk = din("bk", [P, 8]); bv = din("bv", [P, 8])
    bo = din("bo", [P, 8]); b2c = din("b2c", [P, 8])
    b1c = din("b1c", [P, 32])
    alpha = din("alpha", [1, 1])
    cIbf = din("cIbf", [P, P], BF16)
    cIf = din("cIf", [P, P])
    cones = din("cones", [1, P])
    conesbf = din("conesbf", [P, QG], BF16)

    outT = nc.dram_tensor("outT", [DM, T], F32, kind="ExternalOutput").ap()

    def scr(name, shape, dt):
        return nc.dram_tensor(name, shape, dt, kind="Internal").ap()

    qD = scr("qD", [DM, T], BF16)
    kD = scr("kD", [DM, T], BF16)
    vD = scr("vD", [DM, T], BF16)
    xnD = scr("xnD", [DM, T], BF16)
    qnD = scr("qnD", [H, T], BF16)
    knD = scr("knD", [H, T], BF16)
    maskD = scr("maskD", [NE, T], BF16)
    ctxD = scr("ctxD", [DM, T], BF16)
    wqb = scr("wqb", [DM, DM], BF16); wkb = scr("wkb", [DM, DM], BF16)
    wvb = scr("wvb", [DM, DM], BF16); wob = scr("wob", [DM, DM], BF16)
    w1b = scr("w1b", [DM, HID], BF16); w2b = scr("w2b", [HID, DM], BF16)
    wrb = scr("wrb", [DM, NE], BF16)
    browb = scr("browb", [1, NE], BF16)

    es = contextlib.ExitStack()
    ec = es.enter_context

    s_bq = ec(nc.sbuf_tensor("s_bq", [P, 8], F32))
    s_m2bq = ec(nc.sbuf_tensor("s_m2bq", [P, 8], F32))
    s_bk = ec(nc.sbuf_tensor("s_bk", [P, 8], F32))
    s_bv = ec(nc.sbuf_tensor("s_bv", [P, 8], F32))
    s_bo = ec(nc.sbuf_tensor("s_bo", [P, 8], F32))
    s_b2 = ec(nc.sbuf_tensor("s_b2", [P, 8], F32))
    s_b1 = ec(nc.sbuf_tensor("s_b1", [P, 32], F32))
    s_alpha = ec(nc.sbuf_tensor("s_alpha", [1, 1], F32))
    s_slh = ec(nc.sbuf_tensor("s_slh", [1, 1], F32))
    s_slhB = ec(nc.sbuf_tensor("s_slhB", [P, 1], F32))
    s_Ibf = ec(nc.sbuf_tensor("s_Ibf", [P, P], BF16))
    s_If = ec(nc.sbuf_tensor("s_If", [P, P], F32))
    s_ones = ec(nc.sbuf_tensor("s_ones", [1, P], F32))
    s_eps = ec(nc.sbuf_tensor("s_eps", [1, 1], F32))
    s_onesbf = ec(nc.sbuf_tensor("s_onesbf", [P, QG], BF16))

    p = Prog(nc)
    for s in ("dS", "dG", "T", "A", "V"):
        p.sem(s)

    for ap_, dst in [(alpha, s_alpha), (bq, s_bq), (bk, s_bk), (bv, s_bv),
                     (bo, s_bo), (b2c, s_b2), (b1c, s_b1),
                     (cIbf, s_Ibf), (cIf, s_If), (cones, s_ones), (conesbf, s_onesbf)]:
        p.dma("sync", dst[:], ap_, inc=("dS", 16))
    ld0 = p.at("dS")

    p.op("vector", lambda e: e.memset(s_eps[:, :], EPS_LN), inc=("V", 1))
    p.op("vector", lambda e: e.tensor_scalar(out=s_m2bq[:, :], in0=s_bq[:, :],
                                             scalar1=-2.0, scalar2=None, op0=ALU.mult),
         waits=[("dS", ld0)], inc=("V", 1))
    p.op("scalar", lambda e: e.activation(s_slh[:, :], s_alpha[:, :], AF.Exp, scale=CLN / 2.0),
         waits=[("dS", ld0)], inc=("A", 1))
    p.op("vector", lambda e: e.tensor_scalar(out=s_slh[:, :], in0=s_slh[:, :],
                                             scalar1=0.5, scalar2=None, op0=ALU.mult),
         waits=[("A", p.at("A"))], inc=("V", 1))

    with nc.psum_tensor("pS0", [P, 512], F32) as pS0, \
         nc.sbuf_tensor("s_wst", [P, 2 * HID], F32) as s_wst, \
         nc.sbuf_tensor("s_wbo", [P, 2 * HID], BF16) as s_wbo:
        p.op("tensor", lambda e: e.matmul(pS0[:, 0:1], s_ones[0:1, :], s_slh[0:1, :],
                                          start=True, stop=True),
             waits=[("V", p.at("V"))], inc=("T", 1))
        p.op("vector", lambda e: e.tensor_copy(out=s_slhB[:, :], in_=pS0[:, 0:1]),
             waits=[("T", p.at("T"))], inc=("V", 1))

        jobs = []
        for (src, dst, nch, w) in [(wq, wqb, 8, DM), (wk, wkb, 8, DM),
                                   (wv, wvb, 8, DM), (wo, wob, 8, DM),
                                   (w1T, w1b, 8, HID), (w2T, w2b, 32, DM),
                                   (wr, wrb, 8, NE)]:
            for ci in range(nch):
                jobs.append((src, dst, ci, P, w))
        jobs.append((brow, browb, 0, 1, NE))
        store_done = []
        for idx, (src, dst, ci, rows, w) in enumerate(jobs):
            b = idx % 2
            st = s_wst[:rows, b * HID: b * HID + w]
            ob = s_wbo[:rows, b * HID: b * HID + w]
            waits = [("dG", store_done[idx - 2])] if idx >= 2 else []
            p.dma("sync", st, src[ci * P: ci * P + rows, :], waits=waits, inc=("dS", 16))
            p.op("vector", lambda e, ob=ob, st=st: e.tensor_copy(out=ob, in_=st),
                 waits=[("dS", p.at("dS"))], inc=("V", 1))
            p.dma("gpsimd", dst[ci * P: ci * P + rows, :], ob,
                  waits=[("V", p.at("V"))], inc=("dG", 16))
            store_done.append(p.at("dG"))
        s0 = p.snap()
        p.emit()

    with contextlib.ExitStack() as st1:
        e1 = st1.enter_context
        s_xf = e1(nc.sbuf_tensor("s_xf", [P, 8 * QG], F32))
        s_xb = e1(nc.sbuf_tensor("s_xb", [P, 8 * QG], BF16))
        s_xsq = e1(nc.sbuf_tensor("s_xsq", [P, 8 * QG], BF16))
        s_xn = e1(nc.sbuf_tensor("s_xn", [P, 8 * QG], BF16))
        s_wqb = e1(nc.sbuf_tensor("s_wqb", [P, 8 * DM], BF16))
        s_wkb = e1(nc.sbuf_tensor("s_wkb", [P, 8 * DM], BF16))
        s_wvb = e1(nc.sbuf_tensor("s_wvb", [P, 8 * DM], BF16))
        s_wrb = e1(nc.sbuf_tensor("s_wrb", [P, 8 * NE], BF16))
        s_brb = e1(nc.sbuf_tensor("s_brb", [1, NE], BF16))
        s_stats = e1(nc.sbuf_tensor("s_stats", [1, 6 * QG], F32))
        s_tmp = e1(nc.sbuf_tensor("s_tmp", [P, QG], F32))
        s_ev = e1(nc.sbuf_tensor("s_ev", [P, 2 * QG], BF16))
        s_sq2 = e1(nc.sbuf_tensor("s_sq2", [P, QG], BF16))
        s_nr = e1(nc.sbuf_tensor("s_nr", [1, QG], BF16))
        s_lgF = e1(nc.sbuf_tensor("s_lgF", [NE, QG], F32))
        s_lgA = e1(nc.sbuf_tensor("s_lgA", [P, QG], F32))
        s_srt = e1(nc.sbuf_tensor("s_srt", [P, 2 * QG], F32))
        s_thr = e1(nc.sbuf_tensor("s_thr", [P, QG], F32))
        s_mskF = e1(nc.sbuf_tensor("s_mskF", [NE, T], BF16))
        pStat = e1(nc.psum_tensor("pStat", [P, 512], F32))
        pStat2 = e1(nc.psum_tensor("pStat2", [P, 512], F32))
        pAB0 = e1(nc.psum_tensor("pAB0", [P, 512], F32))
        pAB1 = e1(nc.psum_tensor("pAB1", [P, 512], F32))
        pM0 = e1(nc.psum_tensor("pM0", [P, 512], F32))
        pM1 = e1(nc.psum_tensor("pM1", [P, 512], F32))
        pN1 = e1(nc.psum_tensor("pN1", [P, 512], F32))
        pN2 = e1(nc.psum_tensor("pN2", [P, 512], F32))
        pMm = [pM0, pM1]

        for ci in range(8):
            p.dma("sync", s_wqb[:, ci * DM:(ci + 1) * DM], wqb[ci * P:(ci + 1) * P, :],
                  waits=[("dG", s0["dG"])], inc=("dS", 16))
            p.dma("sync", s_wkb[:, ci * DM:(ci + 1) * DM], wkb[ci * P:(ci + 1) * P, :], inc=("dS", 16))
            p.dma("sync", s_wvb[:, ci * DM:(ci + 1) * DM], wvb[ci * P:(ci + 1) * P, :], inc=("dS", 16))
            p.dma("sync", s_wrb[:, ci * NE:(ci + 1) * NE], wrb[ci * P:(ci + 1) * P, :], inc=("dS", 16))
        p.dma("sync", s_brb[:, :], browb[:, :], inc=("dS", 16))
        s1w = p.at("dS")

        lgfree = 0
        prev = dict(s0)
        for qg in range(NQG):
            cs = slice(qg * QG, (qg + 1) * QG)
            wprev = [("T", prev["T"]), ("A", prev["A"]), ("V", prev["V"]), ("dS", s1w)]
            for ci in range(8):
                p.dma("sync", s_xf[:, ci * QG:(ci + 1) * QG], xT[ci * P:(ci + 1) * P, cs],
                      waits=wprev if ci == 0 else [], inc=("dS", 16))
            ldx = p.at("dS")
            for ci in range(8):
                sl = slice(ci * QG, (ci + 1) * QG)
                p.op("vector", lambda e, sl=sl: e.tensor_copy(out=s_xb[:, sl], in_=s_xf[:, sl]),
                     waits=[("dS", ldx)], inc=("V", 1))
                p.op("scalar", lambda e, sl=sl: e.activation(s_xsq[:, sl], s_xf[:, sl], AF.Square),
                     waits=[("dS", ldx)], inc=("A", 1))
            vcast, acast = p.at("V"), p.at("A")
            for ci in range(8):
                sl = slice(ci * QG, (ci + 1) * QG)
                p.op("tensor", lambda e, sl=sl, ci=ci: e.matmul(
                    pStat[0:1, :], s_onesbf[:, 0:1], s_xb[:, sl],
                    start=(ci == 0), stop=(ci == 7)),
                    waits=[("V", vcast)] if ci == 0 else [])
            for ci in range(8):
                sl = slice(ci * QG, (ci + 1) * QG)
                p.op("tensor", lambda e, sl=sl, ci=ci: e.matmul(
                    pStat2[0:1, :], s_onesbf[:, 0:1], s_xsq[:, sl],
                    start=(ci == 0), stop=(ci == 7)),
                    waits=[("A", acast)] if ci == 0 else [],
                    inc=("T", 1) if ci == 7 else None)
            tstat = p.at("T")
            MEAN = s_stats[0:1, 0:QG]; MSQ = s_stats[0:1, QG:2 * QG]
            VAR = s_stats[0:1, 2 * QG:3 * QG]; RSTD = s_stats[0:1, 3 * QG:4 * QG]
            BB = s_stats[0:1, 4 * QG:5 * QG]; LNV = s_stats[0:1, 5 * QG:6 * QG]
            p.op("vector", lambda e: e.tensor_scalar(out=MEAN, in0=pStat[0:1, :],
                                                     scalar1=1.0 / DM, scalar2=None, op0=ALU.mult),
                 waits=[("T", tstat)], inc=("V", 1))
            p.op("vector", lambda e: e.tensor_scalar(out=MSQ, in0=pStat2[0:1, :],
                                                     scalar1=1.0 / DM, scalar2=None, op0=ALU.mult),
                 inc=("V", 1))
            p.op("vector", lambda e: e.tensor_tensor(out=VAR, in0=MEAN, in1=MEAN, op=ALU.mult),
                 waits=[("V", p.at("V"))], inc=("V", 1))
            p.op("vector", lambda e: e.tensor_tensor(out=VAR, in0=MSQ, in1=VAR, op=ALU.subtract),
                 waits=[("V", p.at("V"))], inc=("V", 1))
            p.op("scalar", lambda e: e.activation(LNV, VAR, AF.Ln, bias=s_eps[0:1, 0:1]),
                 waits=[("V", p.at("V"))], inc=("A", 1))
            p.op("scalar", lambda e: e.activation(RSTD, LNV, AF.Exp, scale=-0.5),
                 waits=[("A", p.at("A"))], inc=("A", 1))
            p.op("vector", lambda e: e.tensor_tensor(out=BB, in0=MEAN, in1=RSTD, op=ALU.mult),
                 waits=[("A", p.at("A"))], inc=("V", 1))
            p.op("vector", lambda e: e.tensor_scalar(out=BB, in0=BB, scalar1=-1.0,
                                                     scalar2=None, op0=ALU.mult),
                 waits=[("V", p.at("V"))], inc=("V", 1))
            p.op("tensor", lambda e: e.matmul(pAB0[:, :], s_ones[0:1, :], RSTD,
                                              start=True, stop=True),
                 waits=[("V", p.at("V"))])
            p.op("tensor", lambda e: e.matmul(pAB1[:, :], s_ones[0:1, :], BB,
                                              start=True, stop=True),
                 inc=("T", 1))
            tab = p.at("T")
            for ci in range(8):
                sl = slice(ci * QG, (ci + 1) * QG)
                p.op("vector", lambda e, sl=sl: e.scalar_tensor_tensor(
                    out=s_tmp[:, :], in0=s_xf[:, sl], scalar=1.0, in1=pAB0[:, :],
                    op0=ALU.mult, op1=ALU.mult),
                    waits=[("T", tab), ("V", p.at("V"))], inc=("V", 1))
                p.op("vector", lambda e, sl=sl: e.tensor_tensor(
                    out=s_xn[:, sl], in0=s_tmp[:, :], in1=pAB1[:, :], op=ALU.add),
                    waits=[("V", p.at("V"))], inc=("V", 1))
            vxn = p.at("V")
            for ci in range(8):
                p.dma("gpsimd", xnD[ci * P:(ci + 1) * P, cs], s_xn[:, ci * QG:(ci + 1) * QG],
                      waits=[("V", vxn)] if ci == 0 else [], inc=("dG", 16))

            G = {"bfree": [[("T", prev["T"])], [("T", prev["T"])]],
                 "efree": [prev["dG"], prev["dG"]],
                 "sqfree": prev["T"], "nrfree": prev["V"], "nrfree_g": prev["dG"]}

            def proj(kind, wsb, evac_bias, evac_scale, dstD, normD):
                for ofc in range(8):
                    bank = pMm[ofc % 2]
                    for ci in range(8):
                        sl = slice(ci * QG, (ci + 1) * QG)
                        p.op("tensor", lambda e, sl=sl, bank=bank, ofc=ofc, ci=ci, wsb=wsb: e.matmul(
                            bank[:, :], wsb[:, ci * DM + ofc * P: ci * DM + ofc * P + P],
                            s_xn[:, sl], start=(ci == 0), stop=(ci == 7)),
                            waits=([("V", vxn)] + G["bfree"][ofc % 2]) if ci == 0 else [],
                            inc=("T", 1) if ci == 7 else None)
                    tmm = p.at("T")
                    eb = s_ev[:, (ofc % 2) * QG:(ofc % 2 + 1) * QG]
                    p.op("scalar", lambda e, eb=eb, bank=bank, ofc=ofc, evac_bias=evac_bias,
                         evac_scale=evac_scale: e.activation(
                             eb, bank[:, :], AF.Identity, bias=evac_bias[:, ofc:ofc + 1],
                             scale=evac_scale),
                         waits=[("T", tmm), ("dG", G["efree"][ofc % 2])], inc=("A", 1))
                    aev = p.at("A")
                    p.dma("gpsimd", dstD[ofc * P:(ofc + 1) * P, cs], eb,
                          waits=[("A", aev)], inc=("dG", 16))
                    G["efree"][ofc % 2] = p.at("dG")
                    if normD is not None:
                        nbias = s_bq if kind == "q" else s_bk
                        p.op("scalar", lambda e, bank=bank, ofc=ofc, nbias=nbias: e.activation(
                            s_sq2[:, :], bank[:, :], AF.Square, bias=nbias[:, ofc:ofc + 1]),
                            waits=[("T", G["sqfree"])], inc=("A", 1))
                        asq = p.at("A")
                        G["bfree"][ofc % 2] = [("A", asq)]
                        p.op("tensor", lambda e: e.matmul(
                            pN1[0:1, :], s_onesbf[0:64, 0:1], s_sq2[0:64, :],
                            start=True, stop=True, tile_position=(0, 0)),
                            waits=[("A", asq), ("V", G["nrfree"])])
                        p.op("tensor", lambda e: e.matmul(
                            pN1[64:65, :], s_onesbf[64:128, 0:1], s_sq2[64:128, :],
                            start=True, stop=True, tile_position=(64, 64)),
                            inc=("T", 1))
                        tn = p.at("T")
                        G["sqfree"] = tn
                        p.op("vector", lambda e: e.tensor_copy(out=s_nr[:, :], in_=pN1[0:1, :]),
                             waits=[("T", tn), ("dG", G["nrfree_g"])], inc=("V", 1))
                        p.dma("gpsimd", normD[2 * ofc: 2 * ofc + 1, cs], s_nr[:, :],
                              waits=[("V", p.at("V"))], inc=("dG", 16))
                        p.op("vector", lambda e: e.tensor_copy(out=s_nr[:, :], in_=pN1[64:65, :]),
                             waits=[("dG", p.at("dG"))], inc=("V", 1))
                        G["nrfree"] = p.at("V")
                        p.dma("gpsimd", normD[2 * ofc + 1: 2 * ofc + 2, cs], s_nr[:, :],
                              waits=[("V", p.at("V"))], inc=("dG", 16))
                        G["nrfree_g"] = p.at("dG")
                    else:
                        G["bfree"][ofc % 2] = [("A", p.at("A"))]

            proj("q", s_wqb, s_m2bq, -2.0, qD, qnD)
            proj("k", s_wkb, s_bk, 1.0, kD, knD)
            proj("v", s_wvb, s_bv, 1.0, vD, None)

            for ci in range(8):
                sl = slice(ci * QG, (ci + 1) * QG)
                p.op("tensor", lambda e, sl=sl, ci=ci: e.matmul(
                    pN2[0:NE, :], s_wrb[:, ci * NE:(ci + 1) * NE], s_xn[:, sl],
                    start=(ci == 0), stop=False),
                    waits=[("V", lgfree)] if ci == 0 else [])
            p.op("tensor", lambda e: e.matmul(
                pN2[0:NE, :], s_brb[0:1, :], s_onesbf[0:1, 0:QG],
                start=False, stop=True), inc=("T", 1))
            trt = p.at("T")
            p.op("vector", lambda e: e.tensor_copy(out=s_lgF[:, :], in_=pN2[0:NE, :]),
                 waits=[("T", trt)], inc=("V", 1))
            vlg = p.at("V")
            for tt_ in range(4):
                p.op("tensor", lambda e, tt_=tt_: e.transpose(
                    pN2[:, tt_ * NE:(tt_ + 1) * NE], s_lgF[:, tt_ * P:(tt_ + 1) * P], s_If[0:NE, 0:NE]),
                    waits=[("V", vlg)] if tt_ == 0 else [],
                    inc=("T", 1) if tt_ == 3 else None)
            tlg = p.at("T")
            p.op("vector", lambda e, qg=qg: e.tensor_copy(
                out=s_lgA[:, qg * 64:(qg + 1) * 64], in_=pN2[:, 0:64]),
                waits=[("T", tlg)], inc=("V", 1))
            lgfree = p.at("V")
            prev = p.snap()

        def stages16():
            for k in [2, 4, 8, 16]:
                j = k // 2
                while j >= 1:
                    yield k, j
                    j //= 2

        s1a = p.snap()
        cur, nxt = s_srt[:, 0:QG], s_srt[:, QG:2 * QG]
        p.op("vector", lambda e, cur=cur: e.tensor_copy(out=cur, in_=s_lgA[:, :]),
             waits=[("V", s1a["V"])], inc=("V", 1))
        for (k, j) in stages16():
            kb, ab, c = 16 // k, k // (2 * j), j
            u = max(kb // 2, 1)
            v = 2 if kb > 1 else 1
            vsrc = cur.rearrange("p (t u v a h c) -> p t u v a h c", t=32, u=u, v=v, a=ab, h=2, c=c)
            vdst = nxt.rearrange("p (t u v a h c) -> p t u v a h c", t=32, u=u, v=v, a=ab, h=2, c=c)
            vw = p.at("V")
            if kb > 1:
                ops_ = [(0, ALU.max, 0), (0, ALU.min, 1), (1, ALU.min, 0), (1, ALU.max, 1)]
            else:
                ops_ = [(0, ALU.max, 0), (0, ALU.min, 1)]
            for i_, (par, alu, hv) in enumerate(ops_):
                p.op("vector", lambda e, vsrc=vsrc, vdst=vdst, par=par, alu=alu, hv=hv: e.tensor_tensor(
                    out=vdst[:, :, :, par, :, hv, :],
                    in0=vsrc[:, :, :, par, :, 0, :],
                    in1=vsrc[:, :, :, par, :, 1, :], op=alu),
                    waits=[("V", vw)] if i_ == 0 else [],
                    inc=("V", 1) if i_ == len(ops_) - 1 else None)
            cur, nxt = nxt, cur
        c3 = cur.rearrange("p (t e) -> p t e", e=16)
        th = s_thr.rearrange("p (t e) -> p t e", e=16)
        p.op("vector", lambda e: e.tensor_copy(out=th[:, :, 0:1], in_=c3[:, :, 3:4]),
             waits=[("V", p.at("V"))], inc=("V", 1))
        for (a, b) in [(1, 2), (2, 4), (4, 8), (8, 16)]:
            p.op("vector", lambda e, a=a, b=b: e.tensor_copy(out=th[:, :, a:b], in_=th[:, :, 0:b - a]),
                 waits=[("V", p.at("V"))], inc=("V", 1))
        p.op("vector", lambda e: e.tensor_tensor(out=s_thr[:, :], in0=s_lgA[:, :],
                                                 in1=s_thr[:, :], op=ALU.is_ge),
             waits=[("V", p.at("V"))], inc=("V", 1))
        vmask = p.at("V")
        mq = s_thr.rearrange("p (t e) -> p t e", e=16)
        for g in range(8):
            for tt_ in range(4):
                p.op("tensor", lambda e, g=g, tt_=tt_: e.transpose(
                    pN2[0:NE, tt_ * P:(tt_ + 1) * P], mq[:, 4 * g + tt_, :], s_If[:, :]),
                    waits=[("V", vmask if g == 0 else p.at("V"))] if tt_ == 0 else [],
                    inc=("T", 1) if tt_ == 3 else None)
            tmk = p.at("T")
            p.op("vector", lambda e, g=g: e.tensor_copy(
                out=s_mskF[:, g * QG:(g + 1) * QG], in_=pN2[0:NE, 0:QG]),
                waits=[("T", tmk)], inc=("V", 1))
        p.dma("gpsimd", maskD[:, :], s_mskF[:, :], waits=[("V", p.at("V"))], inc=("dG", 16))
        s1b = p.snap()
        p.emit()

    with contextlib.ExitStack() as st2:
        e2 = st2.enter_context
        s_qp = [e2(nc.sbuf_tensor(f"s_qp{i}", [P, T], BF16)) for i in range(2)]
        s_kp = [e2(nc.sbuf_tensor(f"s_kp{i}", [P, T], BF16)) for i in range(2)]
        s_vF = e2(nc.sbuf_tensor("s_vF", [D, T], BF16))
        s_vb = [e2(nc.sbuf_tensor(f"s_vb{i}", [P, N_QT * D], BF16)) for i in range(2)]
        s_qnw = [e2(nc.sbuf_tensor(f"s_qnw{i}", [2, T], BF16)) for i in range(2)]
        s_knw = [e2(nc.sbuf_tensor(f"s_knw{i}", [2, T], BF16)) for i in range(2)]
        s_lnb = e2(nc.sbuf_tensor("s_lnb", [P, 1024], F32))
        s_rw2 = [e2(nc.sbuf_tensor(f"s_rw2{i}", [P, 2048], F32)) for i in range(2)]
        s_y2 = [e2(nc.sbuf_tensor(f"s_y2{i}", [P, T], F32)) for i in range(2)]
        s_z2 = [e2(nc.sbuf_tensor(f"s_z2{i}", [P, T], F32)) for i in range(2)]
        s_e2 = [e2(nc.sbuf_tensor(f"s_e2{i}", [P, T], BF16)) for i in range(2)]
        s_et = [e2(nc.sbuf_tensor(f"s_et{i}", [P, T], BF16)) for i in range(2)]
        s_ctx2 = [e2(nc.sbuf_tensor(f"s_ctx2{i}", [P, P], BF16)) for i in range(2)]
        s_sm = e2(nc.sbuf_tensor("s_sm", [P, 64], F32))
        s_lamT = [e2(nc.sbuf_tensor(f"s_lamT{i}", [1, P], F32)) for i in range(2)]
        s_lamB = e2(nc.sbuf_tensor("s_lamB", [P, P], F32))
        pA = [e2(nc.psum_tensor(f"pA{i}", [P, 512], F32)) for i in range(3)]
        pB = [e2(nc.psum_tensor(f"pB{i}", [P, 512], F32)) for i in range(2)]
        pT_ = e2(nc.psum_tensor("pT", [P, 1024], BF16))
        pAV = e2(nc.psum_tensor("pAV", [P, 128], F32))
        pSm = e2(nc.psum_tensor("pSm", [P, 512], F32))

        bar = [("V", s1b["V"]), ("A", s1b["A"]), ("T", s1b["T"]), ("dG", s1b["dG"])]

        for i in range(2):
            p.op("vector", lambda e, i=i: e.memset(s_qnw[i][:, :], 1.0),
                 waits=bar if i == 0 else [], inc=("V", 1))
            p.op("vector", lambda e, i=i: e.memset(s_knw[i][:, :], 1.0), inc=("V", 1))
        vmemset = p.at("V")

        head_ld = [0, 0]
        head_cons = [0, 0]
        vb_evac = [0, 0]
        vb_done_prev = 0
        pt_free = 0
        etA_free = [0, 0]
        y_free2 = [0, 0]
        z_free2 = [0, 0]
        e_free2 = [0, 0]
        rwhist = [[0, 0, 0, 0], [0, 0, 0, 0]]
        ctx_free2 = [0, 0]
        pAV_free = 0
        pSm_free = 0
        lamT_free = 0
        bankA_g = [0, 0, 0]
        bankB_g = [0, 0]

        for pair in range(N_PAIR):
            heads = (2 * pair, 2 * pair + 1)
            for i, h in enumerate(heads):
                w_ = list(bar) + [("T", head_cons[i]), ("V", vmemset)]
                p.dma("sync", s_qp[i][0:64, :], qD[h * D:(h + 1) * D, :], waits=w_, inc=("dS", 16))
                p.dma("sync", s_qp[i][64:128, :], qD[h * D:(h + 1) * D, :], inc=("dS", 16))
                p.dma("sync", s_kp[i][0:64, :], kD[h * D:(h + 1) * D, :], inc=("dS", 16))
                p.dma("sync", s_kp[i][64:128, :], kD[h * D:(h + 1) * D, :], inc=("dS", 16))
                p.dma("sync", s_qnw[i][0:1, :], qnD[h:h + 1, :], inc=("dS", 16))
                p.dma("sync", s_knw[i][1:2, :], knD[h:h + 1, :], inc=("dS", 16))
                head_ld[i] = p.at("dS")
                p.dma("sync", s_vF[:, :], vD[h * D:(h + 1) * D, :],
                      waits=[("T", vb_done_prev)], inc=("dS", 16))
                ldv = p.at("dS")
                for half in range(2):
                    for tr in range(16):
                        sc = half * 16 + tr
                        p.op("tensor", lambda e, sc=sc, tr=tr: e.transpose(
                            pT_[:, tr * D:(tr + 1) * D], s_vF[:, sc * P:(sc + 1) * P], s_Ibf[0:D, 0:D]),
                            waits=[("dS", ldv), ("V", pt_free)] if tr == 0 else [],
                            inc=("T", 1) if tr == 15 else None)
                    tvh = p.at("T")
                    p.op("vector", lambda e, i=i, half=half: e.tensor_copy(
                        out=s_vb[i][:, half * 1024:(half + 1) * 1024], in_=pT_[:, :]),
                        waits=[("T", tvh)], inc=("V", 1))
                    pt_free = p.at("V")
                vb_done_prev = p.at("T")
                vb_evac[i] = p.at("V")

            units = [(qt, i) for qt in range(N_QT) for i in range(2)]
            NU = len(units)
            pend = {}
            last_dots_T = [0, 0]

            def emit_head(uidx):
                qt, i = units[uidx]
                qs = slice(qt * P, (qt + 1) * P)
                s_rw = s_rw2[i]; s_y = s_y2[i]
                aLn = []
                vY = []
                rwh = rwhist[i]
                for sc in range(8):
                    ss = slice(sc * 512, (sc + 1) * 512)
                    rsl = slice((sc % 4) * 512, (sc % 4 + 1) * 512)
                    lsl = slice((sc % 2) * 512, (sc % 2 + 1) * 512)
                    w_ = [("dS", head_ld[i]),
                          ("A", bankB_g[sc % 2]), ("V", bankA_g[sc % 3])]
                    p.op("tensor", lambda e, i=i, qs=qs, ss=ss, sc=sc: e.matmul(
                        pA[sc % 3][:, :], s_qp[i][0:64, qs], s_kp[i][0:64, ss],
                        start=True, stop=True, tile_position=(0, 0)),
                        waits=w_)
                    p.op("tensor", lambda e, i=i, qs=qs, ss=ss, sc=sc: e.matmul(
                        pB[sc % 2][:, :], s_qp[i][64:128, qs], s_kp[i][64:128, ss],
                        start=True, stop=False, tile_position=(64, 0)))
                    p.op("tensor", lambda e, i=i, qs=qs, ss=ss, sc=sc: e.matmul(
                        pB[sc % 2][:, :], s_qnw[i][:, qs], s_knw[i][:, ss],
                        start=False, stop=True, tile_position=(0, 0)),
                        inc=("T", 1))
                    tsc = p.at("T")
                    p.op("scalar", lambda e, lsl=lsl, sc=sc: e.activation(
                        s_lnb[:, lsl], pB[sc % 2][:, :], AF.Ln),
                        waits=[("T", tsc)], inc=("A", 1))
                    bankB_g[sc % 2] = p.at("A")
                    p.op("scalar", lambda e, lsl=lsl, rsl=rsl, s_rw=s_rw: e.activation(
                        s_rw[:, rsl], s_lnb[:, lsl], AF.Exp, scale=-0.5),
                        waits=[("A", p.at("A")), ("V", rwh[sc % 4])], inc=("A", 1))
                    aLn.append(p.at("A"))
                    p.op("vector", lambda e, ss=ss, rsl=rsl, sc=sc, s_rw=s_rw, s_y=s_y: e.scalar_tensor_tensor(
                        out=s_y[:, ss], in0=s_rw[:, rsl], scalar=s_slhB[:, :],
                        in1=pA[sc % 3][:, :], op0=ALU.mult, op1=ALU.mult),
                        waits=[("A", aLn[-1]), ("A", y_free2[i])], inc=("V", 1))
                    vY.append(p.at("V"))
                    rwh[sc % 4] = vY[-1]
                    bankA_g[sc % 3] = vY[-1]
                last_dots_T[i] = p.at("T")
                pend[uidx] = {"vY": vY}

            def emit_zparts(uidx):
                qt, i = units[uidx]
                s_y = s_y2[i]; s_z = s_z2[i]
                vY = pend[uidx]["vY"]
                p.op("scalar", lambda e, s_y=s_y, s_z=s_z: e.activation(
                    s_z[:, 2048:4096], s_y[:, 2048:4096], AF.Square),
                    waits=[("V", vY[7]), ("A", z_free2[i])], inc=("A", 1))
                azhi = p.at("A")
                pend[uidx]["azhi"] = azhi
                p.op("vector", lambda e, s_y=s_y, s_z=s_z: e.tensor_tensor(
                    out=s_z[:, 0:2048], in0=s_y[:, 0:2048], in1=s_y[:, 0:2048], op=ALU.mult),
                    waits=[("V", vY[3]), ("A", z_free2[i])], inc=("V", 1))
                vzlo = p.at("V")
                smb = (qt % 2) * 8 + i * 4
                mcol = s_sm[:, smb:smb + 1]
                ncol = s_sm[:, smb + 1:smb + 2]
                p.op("vector", lambda e, mcol=mcol, s_z=s_z: e.tensor_reduce(
                    out=mcol, in_=s_z[:, :], axis=AX.X, op=ALU.max),
                    waits=[("V", vzlo), ("A", azhi)], inc=("V", 1))
                p.op("vector", lambda e, mcol=mcol, ncol=ncol: e.tensor_scalar(
                    out=ncol, in0=mcol, scalar1=-1.0, scalar2=None, op0=ALU.mult),
                    waits=[("V", p.at("V"))], inc=("V", 1))
                pend[uidx]["vneg"] = p.at("V")
                y_free2[i] = azhi

            def emit_exp(uidx):
                qt, i = units[uidx]
                s_z = s_z2[i]; s_e = s_e2[i]
                smb = (qt % 2) * 8 + i * 4
                ncol = s_sm[:, smb + 1:smb + 2]
                ecol = s_sm[:, smb + 2:smb + 3]
                p.op("scalar", lambda e, ncol=ncol, ecol=ecol, s_z=s_z, s_e=s_e: e.activation(
                    s_e[:, :], s_z[:, :], AF.Exp, bias=ncol, accum_out=ecol),
                    waits=[("V", pend[uidx]["vneg"]), ("T", e_free2[i])], inc=("A", 1))
                pend[uidx]["aexp"] = p.at("A")
                z_free2[i] = p.at("A")

            def emit_rest(uidx):
                nonlocal pt_free, pSm_free, lamT_free
                qt, i = units[uidx]
                s_e = s_e2[i]
                aexp = pend[uidx]["aexp"]
                smb = (qt % 2) * 8 + i * 4
                ecol = s_sm[:, smb + 2:smb + 3]
                lcol = s_sm[:, smb + 3:smb + 4]
                p.op("vector", lambda e, ecol=ecol, lcol=lcol: e.reciprocal(out=lcol, in_=ecol),
                     waits=[("A", aexp)], inc=("V", 1))
                vlam = p.at("V")
                p.op("tensor", lambda e, lcol=lcol, i=i: e.transpose(
                    pSm[0:1, i * P:(i + 1) * P], lcol, s_If[:, :]),
                    waits=[("V", vlam), ("V", pSm_free)], inc=("T", 1))
                p.op("vector", lambda e, i=i: e.tensor_copy(
                    out=s_lamT[i][:, :], in_=pSm[0:1, i * P:(i + 1) * P]),
                    waits=[("T", p.at("T")), ("T", lamT_free)], inc=("V", 1))
                lamT_done = p.at("V")
                tfill = 0
                for half in range(4):
                    for tr in range(8):
                        sc16 = half * 8 + tr
                        p.op("tensor", lambda e, sc16=sc16, tr=tr, s_e=s_e: e.transpose(
                            pT_[:, tr * P:(tr + 1) * P], s_e[:, sc16 * P:(sc16 + 1) * P],
                            s_Ibf[:, :]),
                            waits=[("A", aexp), ("V", pt_free)] if tr == 0 else [],
                            inc=("T", 1) if tr == 7 else None)
                    tfill = p.at("T")
                    p.op("vector", lambda e, i=i, half=half: e.tensor_copy(
                        out=s_et[i][:, half * 1024:(half + 1) * 1024], in_=pT_[:, :]),
                        waits=[("T", tfill), ("T", etA_free[i])], inc=("V", 1))
                    pt_free = p.at("V")
                e_free2[i] = tfill
                return lamT_done, p.at("V")

            def emit_avgroup(qt, lamT_done, et_done):
                nonlocal lamT_free, pSm_free, pAV_free, etA_free, ctx_free2
                qs = slice(qt * P, (qt + 1) * P)
                p.op("tensor", lambda e: e.matmul(
                    pSm[0:64, 256:384], s_ones[0:1, 0:64], s_lamT[0][0:1, :],
                    start=True, stop=True, tile_position=(0, 0)),
                    waits=[("V", lamT_done), ("V", pSm_free)])
                p.op("tensor", lambda e: e.matmul(
                    pSm[64:128, 256:384], s_ones[0:1, 0:64], s_lamT[1][0:1, :],
                    start=True, stop=True, tile_position=(0, 64)),
                    inc=("T", 1))
                tlb = p.at("T")
                lamT_free = tlb
                p.op("vector", lambda e: e.tensor_copy(out=s_lamB[:, :], in_=pSm[:, 256:384]),
                     waits=[("T", tlb)], inc=("V", 1))
                vlamB = p.at("V")
                pSm_free = vlamB
                for sc in range(N_QT):
                    p.op("tensor", lambda e, sc=sc: e.matmul(
                        pAV[0:64, :], s_vb[0][:, sc * D:(sc + 1) * D],
                        s_et[0][:, sc * P:(sc + 1) * P],
                        start=(sc == 0), stop=(sc == N_QT - 1), tile_position=(0, 0)),
                        waits=[("V", et_done), ("V", vb_evac[0]), ("V", vb_evac[1]),
                               ("V", pAV_free)] if sc == 0 else [])
                    p.op("tensor", lambda e, sc=sc: e.matmul(
                        pAV[64:128, :], s_vb[1][:, sc * D:(sc + 1) * D],
                        s_et[1][:, sc * P:(sc + 1) * P],
                        start=(sc == 0), stop=(sc == N_QT - 1), tile_position=(0, 64)),
                        inc=("T", 1) if sc == N_QT - 1 else None)
                tav = p.at("T")
                etA_free = [tav, tav]
                cbuf = s_ctx2[qt % 2]
                p.op("vector", lambda e, cbuf=cbuf: e.tensor_tensor(
                    out=cbuf[:, :], in0=pAV[:, :], in1=s_lamB[:, :], op=ALU.mult),
                    waits=[("T", tav), ("V", vlamB), ("dG", ctx_free2[qt % 2])], inc=("V", 1))
                pAV_free = p.at("V")
                p.dma("gpsimd", ctxD[pair * P:(pair + 1) * P, qs], cbuf[:, :],
                      waits=[("V", pAV_free)], inc=("dG", 16))
                ctx_free2[qt % 2] = p.at("dG")

            def emit_headchunk(uidx, sc):
                qt, i = units[uidx]
                qs = slice(qt * P, (qt + 1) * P)
                s_rw = s_rw2[i]; s_y = s_y2[i]
                st_ = pend.setdefault(uidx, {"aLn": [], "vY": [], "zdone": []})
                ss = slice(sc * 512, (sc + 1) * 512)
                rsl = slice((sc % 4) * 512, (sc % 4 + 1) * 512)
                lsl = slice((sc % 2) * 512, (sc % 2 + 1) * 512)
                rwh = rwhist[i]
                w_ = [("dS", head_ld[i]),
                      ("A", bankB_g[sc % 2]), ("V", bankA_g[sc % 3])]
                p.op("tensor", lambda e, i=i, qs=qs, ss=ss, sc=sc: e.matmul(
                    pA[sc % 3][:, :], s_qp[i][0:64, qs], s_kp[i][0:64, ss],
                    start=True, stop=True, tile_position=(0, 0)),
                    waits=w_)
                p.op("tensor", lambda e, i=i, qs=qs, ss=ss, sc=sc: e.matmul(
                    pB[sc % 2][:, :], s_qp[i][64:128, qs], s_kp[i][64:128, ss],
                    start=True, stop=False, tile_position=(64, 0)))
                p.op("tensor", lambda e, i=i, qs=qs, ss=ss, sc=sc: e.matmul(
                    pB[sc % 2][:, :], s_qnw[i][:, qs], s_knw[i][:, ss],
                    start=False, stop=True, tile_position=(0, 0)),
                    inc=("T", 1))
                tsc = p.at("T")
                p.op("scalar", lambda e, lsl=lsl, sc=sc: e.activation(
                    s_lnb[:, lsl], pB[sc % 2][:, :], AF.Ln),
                    waits=[("T", tsc)], inc=("A", 1))
                bankB_g[sc % 2] = p.at("A")
                p.op("scalar", lambda e, lsl=lsl, rsl=rsl, s_rw=s_rw: e.activation(
                    s_rw[:, rsl], s_lnb[:, lsl], AF.Exp, scale=-0.5),
                    waits=[("A", p.at("A")), ("V", rwh[sc % 4])], inc=("A", 1))
                st_["aLn"].append(p.at("A"))
                p.op("vector", lambda e, ss=ss, rsl=rsl, sc=sc, s_rw=s_rw, s_y=s_y: e.scalar_tensor_tensor(
                    out=s_y[:, ss], in0=s_rw[:, rsl], scalar=s_slhB[:, :],
                    in1=pA[sc % 3][:, :], op0=ALU.mult, op1=ALU.mult),
                    waits=[("A", st_["aLn"][-1]), ("A", y_free2[i])], inc=("V", 1))
                st_["vY"].append(p.at("V"))
                rwh[sc % 4] = st_["vY"][-1]
                bankA_g[sc % 3] = st_["vY"][-1]
                s_z = s_z2[i]
                smb = (qt % 2) * 32 + i * 16
                mpcol = s_sm[:, smb + sc:smb + sc + 1]
                if sc % 2 == 0:
                    p.op("scalar", lambda e, ss=ss, s_y=s_y, s_z=s_z: e.activation(
                        s_z[:, ss], s_y[:, ss], AF.Square),
                        waits=[("V", st_["vY"][-1]), ("A", z_free2[i])], inc=("A", 1))
                    st_["zdone"].append(("A", p.at("A")))
                else:
                    p.op("vector", lambda e, ss=ss, s_y=s_y, s_z=s_z: e.tensor_tensor(
                        out=s_z[:, ss], in0=s_y[:, ss], in1=s_y[:, ss], op=ALU.mult),
                        waits=[("A", z_free2[i])], inc=("V", 1))
                    st_["zdone"].append(("V", p.at("V")))
                p.op("vector", lambda e, ss=ss, s_z=s_z, mpcol=mpcol: e.tensor_reduce(
                    out=mpcol, in_=s_z[:, ss], axis=AX.X, op=ALU.max),
                    waits=[st_["zdone"][-1]], inc=("V", 1))
                st_["vmp"] = p.at("V")
                if sc == 7:
                    last_dots_T[i] = p.at("T")
                    y_free2[i] = max(v for (k, v) in st_["zdone"] if k == "A")

            def emit_fill(uidx, half):
                nonlocal pt_free
                qt, i = units[uidx]
                s_e = s_e2[i]
                aexp = pend[uidx]["aexp"]
                for tr in range(8):
                    sc16 = half * 8 + tr
                    p.op("tensor", lambda e, sc16=sc16, tr=tr, s_e=s_e: e.transpose(
                        pT_[:, tr * P:(tr + 1) * P], s_e[:, sc16 * P:(sc16 + 1) * P],
                        s_Ibf[:, :]),
                        waits=[("A", aexp), ("V", pt_free)] if tr == 0 else [],
                        inc=("T", 1) if tr == 7 else None)
                tfill = p.at("T")
                p.op("vector", lambda e, i=i, half=half: e.tensor_copy(
                    out=s_et[i][:, half * 1024:(half + 1) * 1024], in_=pT_[:, :]),
                    waits=[("T", tfill), ("T", etA_free[i])], inc=("V", 1))
                pt_free = p.at("V")
                if half == 3:
                    e_free2[i] = tfill
                    pend[uidx]["et_done"] = p.at("V")

            def emit_lamchain(uidx):
                nonlocal pSm_free, lamT_free
                qt, i = units[uidx]
                aexp = pend[uidx]["aexp"]
                smb = (qt % 2) * 32 + i * 16
                ecol = s_sm[:, smb + 10:smb + 11]
                lcol = s_sm[:, smb + 11:smb + 12]
                p.op("vector", lambda e, ecol=ecol, lcol=lcol: e.reciprocal(out=lcol, in_=ecol),
                     waits=[("A", aexp)], inc=("V", 1))
                vlam = p.at("V")
                p.op("tensor", lambda e, lcol=lcol, i=i: e.transpose(
                    pSm[0:1, i * P:(i + 1) * P], lcol, s_If[:, :]),
                    waits=[("V", vlam), ("V", pSm_free)], inc=("T", 1))
                p.op("vector", lambda e, i=i: e.tensor_copy(
                    out=s_lamT[i][:, :], in_=pSm[0:1, i * P:(i + 1) * P]),
                    waits=[("T", p.at("T")), ("T", lamT_free)], inc=("V", 1))
                pend[uidx]["lamT_done"] = p.at("V")

            def emit_tailzip(uidx):
                qt, i = units[uidx]
                s_z = s_z2[i]
                has_next = uidx + 2 < NU
                smb = (qt % 2) * 32 + i * 16
                mcol = s_sm[:, smb + 8:smb + 9]
                ncol = s_sm[:, smb + 9:smb + 10]
                ecol = s_sm[:, smb + 10:smb + 11]

                p.op("vector", lambda e, mcol=mcol, smb=smb: e.tensor_reduce(
                    out=mcol, in_=s_sm[:, smb:smb + 8], axis=AX.X, op=ALU.max),
                    waits=[("V", pend[uidx]["vmp"])], inc=("V", 1))
                p.op("vector", lambda e, mcol=mcol, ncol=ncol: e.tensor_scalar(
                    out=ncol, in0=mcol, scalar1=-1.0, scalar2=None, op0=ALU.mult),
                    waits=[("V", p.at("V"))], inc=("V", 1))
                vneg = p.at("V")
                s_e = s_e2[i]
                p.op("scalar", lambda e, ncol=ncol, ecol=ecol, s_z=s_z, s_e=s_e: e.activation(
                    s_e[:, :], s_z[:, :], AF.Exp, bias=ncol, accum_out=ecol),
                    waits=[("V", vneg), ("T", e_free2[i])], inc=("A", 1))
                pend[uidx]["aexp"] = p.at("A")
                z_free2[i] = p.at("A")
                if has_next:
                    emit_headchunk(uidx + 2, 0)
                emit_lamchain(uidx)
                if has_next:
                    emit_headchunk(uidx + 2, 1)
                emit_fill(uidx, 0)
                if has_next:
                    emit_headchunk(uidx + 2, 2)
                emit_fill(uidx, 1)
                if has_next:
                    emit_headchunk(uidx + 2, 3)
                emit_fill(uidx, 2)
                if has_next:
                    emit_headchunk(uidx + 2, 4)
                emit_fill(uidx, 3)
                if has_next:
                    emit_headchunk(uidx + 2, 5)
                    emit_headchunk(uidx + 2, 6)
                    emit_headchunk(uidx + 2, 7)

            def emit_avgroup(qt):
                nonlocal lamT_free, pSm_free, pAV_free, etA_free, ctx_free2
                u0 = qt * 2
                lamT_done = pend[u0 + 1]["lamT_done"]
                et_done = pend[u0 + 1]["et_done"]
                qs = slice(qt * P, (qt + 1) * P)
                p.op("tensor", lambda e: e.matmul(
                    pSm[0:64, 256:384], s_ones[0:1, 0:64], s_lamT[0][0:1, :],
                    start=True, stop=True, tile_position=(0, 0)),
                    waits=[("V", lamT_done), ("V", pSm_free)])
                p.op("tensor", lambda e: e.matmul(
                    pSm[64:128, 256:384], s_ones[0:1, 0:64], s_lamT[1][0:1, :],
                    start=True, stop=True, tile_position=(0, 64)),
                    inc=("T", 1))
                tlb = p.at("T")
                lamT_free = tlb
                p.op("vector", lambda e: e.tensor_copy(out=s_lamB[:, :], in_=pSm[:, 256:384]),
                     waits=[("T", tlb)], inc=("V", 1))
                vlamB = p.at("V")
                pSm_free = vlamB
                for sc in range(N_QT):
                    p.op("tensor", lambda e, sc=sc: e.matmul(
                        pAV[0:64, :], s_vb[0][:, sc * D:(sc + 1) * D],
                        s_et[0][:, sc * P:(sc + 1) * P],
                        start=(sc == 0), stop=(sc == N_QT - 1), tile_position=(0, 0)),
                        waits=[("V", et_done), ("V", vb_evac[0]), ("V", vb_evac[1]),
                               ("V", pAV_free)] if sc == 0 else [])
                    p.op("tensor", lambda e, sc=sc: e.matmul(
                        pAV[64:128, :], s_vb[1][:, sc * D:(sc + 1) * D],
                        s_et[1][:, sc * P:(sc + 1) * P],
                        start=(sc == 0), stop=(sc == N_QT - 1), tile_position=(0, 64)),
                        inc=("T", 1) if sc == N_QT - 1 else None)
                tav = p.at("T")
                etA_free = [tav, tav]
                cbuf = s_ctx2[qt % 2]
                p.op("vector", lambda e, cbuf=cbuf: e.tensor_tensor(
                    out=cbuf[:, :], in0=pAV[:, :], in1=s_lamB[:, :], op=ALU.mult),
                    waits=[("T", tav), ("V", vlamB), ("dG", ctx_free2[qt % 2])], inc=("V", 1))
                pAV_free = p.at("V")
                p.dma("gpsimd", ctxD[pair * P:(pair + 1) * P, qs], cbuf[:, :],
                      waits=[("V", pAV_free)], inc=("dG", 16))
                ctx_free2[qt % 2] = p.at("dG")

            for sc in range(8):
                emit_headchunk(0, sc)
            for sc in range(8):
                emit_headchunk(1, sc)
            for uidx in range(NU):
                emit_tailzip(uidx)
                qt, i = units[uidx]
                if i == 1:
                    emit_avgroup(qt)
                if uidx - 2 >= 0:
                    pend.pop(uidx - 2, None)

            head_cons = [last_dots_T[0], last_dots_T[1]]

        s2 = p.snap()
        p.emit()

    with contextlib.ExitStack() as st3:
        e3 = st3.enter_context
        s_w1 = e3(nc.sbuf_tensor("s_w1", [P, 8 * HID], BF16))
        s_w2 = e3(nc.sbuf_tensor("s_w2", [P, 32 * DM], BF16))
        s_wo = e3(nc.sbuf_tensor("s_wo", [P, 8 * DM], BF16))
        s_xn3 = e3(nc.sbuf_tensor("s_xn3", [P, 8 * QG3], BF16))
        s_cx3 = e3(nc.sbuf_tensor("s_cx3", [P, 8 * QG3], BF16))
        s_x3 = e3(nc.sbuf_tensor("s_x3", [P, 8 * QG3], F32))
        s_hM = e3(nc.sbuf_tensor("s_hM", [P, 32 * QG3], BF16))
        s_hT = e3(nc.sbuf_tensor("s_hT", [P, 2 * QG3], BF16))
        s_mr = e3(nc.sbuf_tensor("s_mr", [1, NE * QG3], BF16))
        s_mB = [e3(nc.sbuf_tensor(f"s_mB{i}", [P, QG3], BF16)) for i in range(2)]
        s_sS = e3(nc.sbuf_tensor("s_sS", [P, 8 * QG3], F32))
        s_g3 = e3(nc.sbuf_tensor("s_g3", [P, 2 * QG3], F32))
        s_o3 = e3(nc.sbuf_tensor("s_o3", [P, 2 * QG3], F32))
        pS3 = [e3(nc.psum_tensor(f"pS3{i}", [P, 512], F32)) for i in range(2)]
        pH3 = [e3(nc.psum_tensor(f"pH3{i}", [P, 512], F32)) for i in range(2)]
        pY3 = [e3(nc.psum_tensor(f"pY3{i}", [P, 512], F32)) for i in range(2)]
        pM3 = e3(nc.psum_tensor("pM3", [P, 512], F32))

        bar2 = [("V", s2["V"]), ("A", s2["A"]), ("T", s2["T"]), ("dG", s2["dG"])]

        for ci in range(8):
            p.dma("sync", s_w1[:, ci * HID:(ci + 1) * HID], w1b[ci * P:(ci + 1) * P, :],
                  waits=bar2 if ci == 0 else [], inc=("dS", 16))
            p.dma("sync", s_wo[:, ci * DM:(ci + 1) * DM], wob[ci * P:(ci + 1) * P, :], inc=("dS", 16))
        for hc in range(32):
            p.dma("sync", s_w2[:, hc * DM:(hc + 1) * DM], w2b[hc * P:(hc + 1) * P, :], inc=("dS", 16))
        s3w = p.at("dS")

        sfree = [0, 0]
        mfree = [0, 0]
        mB_done = [0, 0]
        hfree = [0, 0]
        hTfree = [0, 0]
        hM_free = 0
        yfree = [0, 0]
        gfree = [0, 0]
        ofree = [0, 0]

        prev3 = dict(s2)
        for g3 in range(NQG3):
            cs = slice(g3 * QG3, (g3 + 1) * QG3)
            wprev = [("T", prev3["T"]), ("A", prev3["A"]), ("V", prev3["V"]),
                     ("dG", prev3["dG"]), ("dS", s3w)]
            for ci in range(8):
                p.dma("sync", s_xn3[:, ci * QG3:(ci + 1) * QG3], xnD[ci * P:(ci + 1) * P, cs],
                      waits=wprev if ci == 0 else [], inc=("dS", 16))
                p.dma("sync", s_cx3[:, ci * QG3:(ci + 1) * QG3], ctxD[ci * P:(ci + 1) * P, cs],
                      inc=("dS", 16))
                p.dma("sync", s_x3[:, ci * QG3:(ci + 1) * QG3], xT[ci * P:(ci + 1) * P, cs],
                      inc=("dS", 16))
            for e_ in range(NE):
                p.dma("sync", s_mr[0:1, e_ * QG3:(e_ + 1) * QG3], maskD[e_:e_ + 1, cs],
                      inc=("dS", 16))
            ld3 = p.at("dS")

            for ofc in range(8):
                for hd in range(8):
                    p.op("tensor", lambda e, ofc=ofc, hd=hd: e.matmul(
                        pS3[ofc % 2][:, 0:QG3],
                        s_wo[:, hd * DM + ofc * P: hd * DM + ofc * P + P],
                        s_cx3[:, hd * QG3:(hd + 1) * QG3],
                        start=(hd == 0), stop=(hd == 7)),
                        waits=[("dS", ld3), ("A", sfree[ofc % 2])] if hd == 0 else [],
                        inc=("T", 1) if hd == 7 else None)
                tpr = p.at("T")
                p.op("scalar", lambda e, ofc=ofc: e.activation(
                    s_sS[:, ofc * QG3:(ofc + 1) * QG3], pS3[ofc % 2][:, 0:QG3],
                    AF.Identity, bias=s_bo[:, ofc:ofc + 1]),
                    waits=[("T", tpr), ("V", gfree[0]), ("V", gfree[1])], inc=("A", 1))
                sfree[ofc % 2] = p.at("A")
            aS = p.at("A")

            for hc in range(32):
                ex = hc // 2
                if hc % 2 == 0:
                    p.op("tensor", lambda e, ex=ex: e.matmul(
                        pM3[:, 0:QG3], s_onesbf[0:1, 0:P],
                        s_mr[0:1, ex * QG3:(ex + 1) * QG3],
                        start=True, stop=True),
                        waits=[("dS", ld3), ("V", mfree[ex % 2])],
                        inc=("T", 1))
                    tmb = p.at("T")
                    p.op("vector", lambda e, ex=ex: e.tensor_copy(
                        out=s_mB[ex % 2][:, :], in_=pM3[:, 0:QG3]),
                        waits=[("T", tmb)], inc=("V", 1))
                    mB_done[ex % 2] = p.at("V")
                for ci in range(8):
                    p.op("tensor", lambda e, hc=hc, ci=ci: e.matmul(
                        pH3[hc % 2][:, 0:QG3],
                        s_w1[:, ci * HID + hc * P: ci * HID + hc * P + P],
                        s_xn3[:, ci * QG3:(ci + 1) * QG3],
                        start=(ci == 0), stop=(ci == 7)),
                        waits=[("dS", ld3), ("A", hfree[hc % 2])] if ci == 0 else [],
                        inc=("T", 1) if ci == 7 else None)
                thc = p.at("T")
                p.op("scalar", lambda e, hc=hc: e.activation(
                    s_hT[:, (hc % 2) * QG3:(hc % 2 + 1) * QG3], pH3[hc % 2][:, 0:QG3],
                    AF.Relu, bias=s_b1[:, hc:hc + 1]),
                    waits=[("T", thc), ("V", hTfree[hc % 2])], inc=("A", 1))
                hfree[hc % 2] = p.at("A")
                p.op("vector", lambda e, hc=hc, ex=ex: e.tensor_tensor(
                    out=s_hM[:, hc * QG3:(hc + 1) * QG3],
                    in0=s_hT[:, (hc % 2) * QG3:(hc % 2 + 1) * QG3],
                    in1=s_mB[ex % 2][:, :], op=ALU.mult),
                    waits=[("A", hfree[hc % 2]), ("V", mB_done[ex % 2]),
                           ("T", hM_free)], inc=("V", 1))
                hTfree[hc % 2] = p.at("V")
                mfree[ex % 2] = p.at("V")
            vH = p.at("V")

            for ofc in range(8):
                for hc in range(32):
                    p.op("tensor", lambda e, ofc=ofc, hc=hc: e.matmul(
                        pY3[ofc % 2][:, 0:QG3],
                        s_w2[:, hc * DM + ofc * P: hc * DM + ofc * P + P],
                        s_hM[:, hc * QG3:(hc + 1) * QG3],
                        start=(hc == 0), stop=(hc == 31)),
                        waits=[("V", vH), ("V", yfree[ofc % 2])] if hc == 0 else [],
                        inc=("T", 1) if hc == 31 else None)
                tyc = p.at("T")
                gb = s_g3[:, (ofc % 2) * QG3:(ofc % 2 + 1) * QG3]
                ob = s_o3[:, (ofc % 2) * QG3:(ofc % 2 + 1) * QG3]
                p.op("scalar", lambda e, gb=gb, ofc=ofc: e.activation(
                    gb, pY3[ofc % 2][:, 0:QG3], AF.Silu, bias=s_b2[:, ofc:ofc + 1]),
                    waits=[("T", tyc), ("V", gfree[ofc % 2])], inc=("A", 1))
                ag = p.at("A")
                p.op("vector", lambda e, gb=gb, ofc=ofc: e.tensor_tensor(
                    out=gb, in0=gb, in1=s_sS[:, ofc * QG3:(ofc + 1) * QG3], op=ALU.mult),
                    waits=[("A", ag), ("A", aS)], inc=("V", 1))
                p.op("vector", lambda e, gb=gb, ob=ob, ofc=ofc: e.tensor_tensor(
                    out=ob, in0=gb, in1=s_x3[:, ofc * QG3:(ofc + 1) * QG3], op=ALU.add),
                    waits=[("V", p.at("V")), ("dG", ofree[ofc % 2])], inc=("V", 1))
                vout = p.at("V")
                gfree[ofc % 2] = vout
                yfree[ofc % 2] = vout
                p.dma("gpsimd", outT[ofc * P:(ofc + 1) * P, cs], ob,
                      waits=[("V", vout)], inc=("dG", 16))
                ofree[ofc % 2] = p.at("dG")
            hM_free = p.at("T")
            prev3 = p.snap()

        p.op("gpsimd", lambda e: e.memset(s_g3[0:1, 0:1], 0.0),
             waits=[("dG", p.at("dG"))])
        p.emit()

    es.close()
    p._es.close()
    return nc


def _host_layout(inputs):
    f32 = np.float32
    shared = {}
    shared["wq"] = np.ascontiguousarray(inputs["L0_attn_wq"], dtype=f32)
    shared["wk"] = np.ascontiguousarray(inputs["L0_attn_wk"], dtype=f32)
    shared["wv"] = np.ascontiguousarray(inputs["L0_attn_wv"], dtype=f32)
    shared["wo"] = np.ascontiguousarray(inputs["L0_attn_wo"], dtype=f32)
    shared["w1T"] = np.ascontiguousarray(np.asarray(inputs["L0_ffn_w1"], dtype=f32).T)
    shared["w2T"] = np.ascontiguousarray(np.asarray(inputs["L0_ffn_w2"], dtype=f32).T)
    shared["wr"] = np.ascontiguousarray(inputs["L0_ffn_w_router"], dtype=f32)
    shared["brow"] = np.ascontiguousarray(np.asarray(inputs["L0_ffn_b_router"], dtype=f32).reshape(1, NE))
    for name, key in [("bq", "L0_attn_bq"), ("bk", "L0_attn_bk"), ("bv", "L0_attn_bv"),
                      ("bo", "L0_attn_bo"), ("b2c", "L0_ffn_b2")]:
        shared[name] = np.ascontiguousarray(np.asarray(inputs[key], dtype=f32).reshape(8, P).T)
    shared["b1c"] = np.ascontiguousarray(np.asarray(inputs["L0_ffn_b1"], dtype=f32).reshape(32, P).T)
    shared["alpha"] = np.asarray(inputs["L0_attn_alpha"], dtype=f32).reshape(1, 1)
    shared["cIbf"] = np.eye(P, dtype=f32).astype(ml_dtypes.bfloat16)
    shared["cIf"] = np.eye(P, dtype=f32)
    shared["cones"] = np.ones((1, P), dtype=f32)
    shared["conesbf"] = np.ones((P, QG), dtype=f32).astype(ml_dtypes.bfloat16)
    x = np.asarray(inputs["L0_attn_x"], dtype=f32)
    in_maps = []
    for b in range(NCORE):
        m = dict(shared)
        m["xT"] = np.ascontiguousarray(x[b].T)
        in_maps.append(m)
    return in_maps


def kernel(**inputs):
    in_maps = _host_layout(inputs)
    nc = build_program()
    res = run_bass_kernel_spmd(nc, in_maps, core_ids=list(range(NCORE)))
    out = np.empty((NCORE, T, DM), dtype=np.float32)
    for b in range(NCORE):
        out[b] = np.asarray(res.results[b]["outT"], dtype=np.float32).T
    return out


# version 13
# speedup vs baseline: 1.0473x; 1.0473x over previous
import math
import contextlib
from collections import defaultdict

import numpy as np
import ml_dtypes

import concourse.bass as bass
import concourse.mybir as mybir
from concourse.bass_utils import run_bass_kernel_spmd

F32 = mybir.dt.float32
BF16 = mybir.dt.bfloat16
AF = mybir.ActivationFunctionType
ALU = mybir.AluOpType
AX = mybir.AxisListType

P = 128
DM = 1024
T = 4096
H = 16
D = 64
HID = 4096
NE = 16
QG = 512
NQG = T // QG
QG3 = 256
NQG3 = T // QG3
N_PAIR = 8
N_QT = 32
CLN = math.log(8.0 / math.log1p(64.0))
EPS_LN = 1e-5
NCORE = 8
PIPE_MODE = 2


class Prog:

    ENGS = ("sync", "gpsimd", "tensor", "scalar", "vector")

    def __init__(self, nc):
        self.nc = nc
        self.ops = {e: [] for e in self.ENGS}
        self.cnt = defaultdict(int)
        self.sems = {}
        self._es = contextlib.ExitStack()

    def sem(self, name):
        if name not in self.sems:
            self.sems[name] = self._es.enter_context(self.nc.semaphore(name))
        return self.sems[name]

    def op(self, eng, fn, waits=(), inc=None):
        incs = []
        if inc:
            for (s, k) in (inc if isinstance(inc, list) else [inc]):
                self.cnt[s] += k
                incs.append((s, k))
        self.ops[eng].append((fn, [(s, v) for (s, v) in waits if v > 0], incs))

    def dma(self, eng, out, in_, waits=(), inc=None, **kw):
        def fn(e, out=out, in_=in_, kw=kw):
            return e.dma_start(out=out, in_=in_, **kw)
        self.op(eng, fn, waits, inc)

    def at(self, s):
        return self.cnt[s]

    def snap(self):
        return dict(self.cnt)

    def emit(self):
        with self.nc.Block() as block:
            for eng in self.ENGS:
                ops = self.ops[eng]
                if not ops:
                    continue
                def body(e, ops=ops):
                    seen = defaultdict(int)
                    for fn, waits, incs in ops:
                        for (s, v) in waits:
                            if v > seen[s]:
                                e.wait_ge(self.sems[s], v)
                                seen[s] = v
                        inst = fn(e)
                        for (s, k) in incs:
                            inst.then_inc(self.sems[s], k)
                getattr(block, eng)(body)
        self.ops = {e: [] for e in self.ENGS}


def build_program():
    nc = bass.Bass("TRN2", target_bir_lowering=False)

    def din(name, shape, dt=F32):
        return nc.dram_tensor(name, shape, dt, kind="ExternalInput").ap()

    xT = din("xT", [DM, T])
    wq = din("wq", [DM, DM]); wk = din("wk", [DM, DM])
    wv = din("wv", [DM, DM]); wo = din("wo", [DM, DM])
    w1T = din("w1T", [DM, HID]); w2T = din("w2T", [HID, DM])
    wr = din("wr", [DM, NE])
    brow = din("brow", [1, NE])
    bq = din("bq", [P, 8]); bk = din("bk", [P, 8]); bv = din("bv", [P, 8])
    bo = din("bo", [P, 8]); b2c = din("b2c", [P, 8])
    b1c = din("b1c", [P, 32])
    alpha = din("alpha", [1, 1])
    cIbf = din("cIbf", [P, P], BF16)
    cIf = din("cIf", [P, P])
    cones = din("cones", [1, P])
    conesbf = din("conesbf", [P, QG], BF16)

    outT = nc.dram_tensor("outT", [DM, T], F32, kind="ExternalOutput").ap()

    def scr(name, shape, dt):
        return nc.dram_tensor(name, shape, dt, kind="Internal").ap()

    qD = scr("qD", [DM, T], BF16)
    kD = scr("kD", [DM, T], BF16)
    vD = scr("vD", [DM, T], BF16)
    xnD = scr("xnD", [DM, T], BF16)
    qnD = scr("qnD", [H, T], BF16)
    knD = scr("knD", [H, T], BF16)
    maskD = scr("maskD", [NE, T], BF16)
    ctxD = scr("ctxD", [DM, T], BF16)
    wqb = scr("wqb", [DM, DM], BF16); wkb = scr("wkb", [DM, DM], BF16)
    wvb = scr("wvb", [DM, DM], BF16); wob = scr("wob", [DM, DM], BF16)
    w1b = scr("w1b", [DM, HID], BF16); w2b = scr("w2b", [HID, DM], BF16)
    wrb = scr("wrb", [DM, NE], BF16)
    browb = scr("browb", [1, NE], BF16)

    es = contextlib.ExitStack()
    ec = es.enter_context

    s_bq = ec(nc.sbuf_tensor("s_bq", [P, 8], F32))
    s_m2bq = ec(nc.sbuf_tensor("s_m2bq", [P, 8], F32))
    s_bk = ec(nc.sbuf_tensor("s_bk", [P, 8], F32))
    s_bv = ec(nc.sbuf_tensor("s_bv", [P, 8], F32))
    s_bo = ec(nc.sbuf_tensor("s_bo", [P, 8], F32))
    s_b2 = ec(nc.sbuf_tensor("s_b2", [P, 8], F32))
    s_b1 = ec(nc.sbuf_tensor("s_b1", [P, 32], F32))
    s_alpha = ec(nc.sbuf_tensor("s_alpha", [1, 1], F32))
    s_slh = ec(nc.sbuf_tensor("s_slh", [1, 1], F32))
    s_slhB = ec(nc.sbuf_tensor("s_slhB", [P, 1], F32))
    s_s4B = ec(nc.sbuf_tensor("s_s4B", [P, 1], F32))
    s_Ibf = ec(nc.sbuf_tensor("s_Ibf", [P, P], BF16))
    s_If = ec(nc.sbuf_tensor("s_If", [P, P], F32))
    s_ones = ec(nc.sbuf_tensor("s_ones", [1, P], F32))
    s_eps = ec(nc.sbuf_tensor("s_eps", [1, 1], F32))
    s_onesbf = ec(nc.sbuf_tensor("s_onesbf", [P, QG], BF16))

    p = Prog(nc)
    for s in ("dS", "dG", "T", "A", "V"):
        p.sem(s)

    for ap_, dst in [(alpha, s_alpha), (bq, s_bq), (bk, s_bk), (bv, s_bv),
                     (bo, s_bo), (b2c, s_b2), (b1c, s_b1),
                     (cIbf, s_Ibf), (cIf, s_If), (cones, s_ones), (conesbf, s_onesbf)]:
        p.dma("sync", dst[:], ap_, inc=("dS", 16))
    ld0 = p.at("dS")

    p.op("vector", lambda e: e.memset(s_eps[:, :], EPS_LN), inc=("V", 1))
    p.op("vector", lambda e: e.tensor_scalar(out=s_m2bq[:, :], in0=s_bq[:, :],
                                             scalar1=-2.0, scalar2=None, op0=ALU.mult),
         waits=[("dS", ld0)], inc=("V", 1))
    p.op("scalar", lambda e: e.activation(s_slh[:, :], s_alpha[:, :], AF.Exp, scale=CLN / 2.0),
         waits=[("dS", ld0)], inc=("A", 1))
    p.op("vector", lambda e: e.tensor_scalar(out=s_slh[:, :], in0=s_slh[:, :],
                                             scalar1=0.5, scalar2=None, op0=ALU.mult),
         waits=[("A", p.at("A"))], inc=("V", 1))

    with nc.psum_tensor("pS0", [P, 512], F32) as pS0, \
         nc.sbuf_tensor("s_wst", [P, 2 * HID], F32) as s_wst, \
         nc.sbuf_tensor("s_wbo", [P, 2 * HID], BF16) as s_wbo:
        p.op("tensor", lambda e: e.matmul(pS0[:, 0:1], s_ones[0:1, :], s_slh[0:1, :],
                                          start=True, stop=True),
             waits=[("V", p.at("V"))], inc=("T", 1))
        p.op("vector", lambda e: e.tensor_copy(out=s_slhB[:, :], in_=pS0[:, 0:1]),
             waits=[("T", p.at("T"))], inc=("V", 1))
        p.op("vector", lambda e: e.tensor_tensor(out=s_s4B[:, :], in0=s_slhB[:, :],
                                                 in1=s_slhB[:, :], op=ALU.mult),
             waits=[("V", p.at("V"))], inc=("V", 1))

        jobs = []
        for (src, dst, nch, w) in [(wq, wqb, 8, DM), (wk, wkb, 8, DM),
                                   (wv, wvb, 8, DM), (wo, wob, 8, DM),
                                   (w1T, w1b, 8, HID), (w2T, w2b, 32, DM),
                                   (wr, wrb, 8, NE)]:
            for ci in range(nch):
                jobs.append((src, dst, ci, P, w))
        jobs.append((brow, browb, 0, 1, NE))
        store_done = []
        for idx, (src, dst, ci, rows, w) in enumerate(jobs):
            b = idx % 2
            st = s_wst[:rows, b * HID: b * HID + w]
            ob = s_wbo[:rows, b * HID: b * HID + w]
            waits = [("dG", store_done[idx - 2])] if idx >= 2 else []
            p.dma("sync", st, src[ci * P: ci * P + rows, :], waits=waits, inc=("dS", 16))
            p.op("vector", lambda e, ob=ob, st=st: e.tensor_copy(out=ob, in_=st),
                 waits=[("dS", p.at("dS"))], inc=("V", 1))
            p.dma("gpsimd", dst[ci * P: ci * P + rows, :], ob,
                  waits=[("V", p.at("V"))], inc=("dG", 16))
            store_done.append(p.at("dG"))
        s0 = p.snap()
        p.emit()

    with contextlib.ExitStack() as st1:
        e1 = st1.enter_context
        s_xf = e1(nc.sbuf_tensor("s_xf", [P, 8 * QG], F32))
        s_xb = e1(nc.sbuf_tensor("s_xb", [P, 8 * QG], BF16))
        s_xsq = e1(nc.sbuf_tensor("s_xsq", [P, 8 * QG], BF16))
        s_xn = e1(nc.sbuf_tensor("s_xn", [P, 8 * QG], BF16))
        s_wqb = e1(nc.sbuf_tensor("s_wqb", [P, 8 * DM], BF16))
        s_wkb = e1(nc.sbuf_tensor("s_wkb", [P, 8 * DM], BF16))
        s_wvb = e1(nc.sbuf_tensor("s_wvb", [P, 8 * DM], BF16))
        s_wrb = e1(nc.sbuf_tensor("s_wrb", [P, 8 * NE], BF16))
        s_brb = e1(nc.sbuf_tensor("s_brb", [1, NE], BF16))
        s_stats = e1(nc.sbuf_tensor("s_stats", [1, 6 * QG], F32))
        s_tmp = e1(nc.sbuf_tensor("s_tmp", [P, QG], F32))
        s_ev = e1(nc.sbuf_tensor("s_ev", [P, 2 * QG], BF16))
        s_sq2 = e1(nc.sbuf_tensor("s_sq2", [P, QG], BF16))
        s_nr = e1(nc.sbuf_tensor("s_nr", [1, QG], BF16))
        s_lgF = e1(nc.sbuf_tensor("s_lgF", [NE, QG], F32))
        s_lgA = e1(nc.sbuf_tensor("s_lgA", [P, QG], F32))
        s_srt = e1(nc.sbuf_tensor("s_srt", [P, 2 * QG], F32))
        s_thr = e1(nc.sbuf_tensor("s_thr", [P, QG], F32))
        s_mskF = e1(nc.sbuf_tensor("s_mskF", [NE, T], BF16))
        pStat = e1(nc.psum_tensor("pStat", [P, 512], F32))
        pStat2 = e1(nc.psum_tensor("pStat2", [P, 512], F32))
        pAB0 = e1(nc.psum_tensor("pAB0", [P, 512], F32))
        pAB1 = e1(nc.psum_tensor("pAB1", [P, 512], F32))
        pM0 = e1(nc.psum_tensor("pM0", [P, 512], F32))
        pM1 = e1(nc.psum_tensor("pM1", [P, 512], F32))
        pN1 = e1(nc.psum_tensor("pN1", [P, 512], F32))
        pN2 = e1(nc.psum_tensor("pN2", [P, 512], F32))
        pMm = [pM0, pM1]

        for ci in range(8):
            p.dma("sync", s_wqb[:, ci * DM:(ci + 1) * DM], wqb[ci * P:(ci + 1) * P, :],
                  waits=[("dG", s0["dG"])], inc=("dS", 16))
            p.dma("sync", s_wkb[:, ci * DM:(ci + 1) * DM], wkb[ci * P:(ci + 1) * P, :], inc=("dS", 16))
            p.dma("sync", s_wvb[:, ci * DM:(ci + 1) * DM], wvb[ci * P:(ci + 1) * P, :], inc=("dS", 16))
            p.dma("sync", s_wrb[:, ci * NE:(ci + 1) * NE], wrb[ci * P:(ci + 1) * P, :], inc=("dS", 16))
        p.dma("sync", s_brb[:, :], browb[:, :], inc=("dS", 16))
        s1w = p.at("dS")

        lgfree = 0
        prev = dict(s0)
        for qg in range(NQG):
            cs = slice(qg * QG, (qg + 1) * QG)
            wprev = [("T", prev["T"]), ("A", prev["A"]), ("V", prev["V"]), ("dS", s1w)]
            for ci in range(8):
                p.dma("sync", s_xf[:, ci * QG:(ci + 1) * QG], xT[ci * P:(ci + 1) * P, cs],
                      waits=wprev if ci == 0 else [], inc=("dS", 16))
            ldx = p.at("dS")
            for ci in range(8):
                sl = slice(ci * QG, (ci + 1) * QG)
                p.op("vector", lambda e, sl=sl: e.tensor_copy(out=s_xb[:, sl], in_=s_xf[:, sl]),
                     waits=[("dS", ldx)], inc=("V", 1))
                p.op("scalar", lambda e, sl=sl: e.activation(s_xsq[:, sl], s_xf[:, sl], AF.Square),
                     waits=[("dS", ldx)], inc=("A", 1))
            vcast, acast = p.at("V"), p.at("A")
            for ci in range(8):
                sl = slice(ci * QG, (ci + 1) * QG)
                p.op("tensor", lambda e, sl=sl, ci=ci: e.matmul(
                    pStat[0:1, :], s_onesbf[:, 0:1], s_xb[:, sl],
                    start=(ci == 0), stop=(ci == 7)),
                    waits=[("V", vcast)] if ci == 0 else [])
            for ci in range(8):
                sl = slice(ci * QG, (ci + 1) * QG)
                p.op("tensor", lambda e, sl=sl, ci=ci: e.matmul(
                    pStat2[0:1, :], s_onesbf[:, 0:1], s_xsq[:, sl],
                    start=(ci == 0), stop=(ci == 7)),
                    waits=[("A", acast)] if ci == 0 else [],
                    inc=("T", 1) if ci == 7 else None)
            tstat = p.at("T")
            MEAN = s_stats[0:1, 0:QG]; MSQ = s_stats[0:1, QG:2 * QG]
            VAR = s_stats[0:1, 2 * QG:3 * QG]; RSTD = s_stats[0:1, 3 * QG:4 * QG]
            BB = s_stats[0:1, 4 * QG:5 * QG]; LNV = s_stats[0:1, 5 * QG:6 * QG]
            p.op("vector", lambda e: e.tensor_scalar(out=MEAN, in0=pStat[0:1, :],
                                                     scalar1=1.0 / DM, scalar2=None, op0=ALU.mult),
                 waits=[("T", tstat)], inc=("V", 1))
            p.op("vector", lambda e: e.tensor_scalar(out=MSQ, in0=pStat2[0:1, :],
                                                     scalar1=1.0 / DM, scalar2=None, op0=ALU.mult),
                 inc=("V", 1))
            p.op("vector", lambda e: e.tensor_tensor(out=VAR, in0=MEAN, in1=MEAN, op=ALU.mult),
                 waits=[("V", p.at("V"))], inc=("V", 1))
            p.op("vector", lambda e: e.tensor_tensor(out=VAR, in0=MSQ, in1=VAR, op=ALU.subtract),
                 waits=[("V", p.at("V"))], inc=("V", 1))
            p.op("scalar", lambda e: e.activation(LNV, VAR, AF.Ln, bias=s_eps[0:1, 0:1]),
                 waits=[("V", p.at("V"))], inc=("A", 1))
            p.op("scalar", lambda e: e.activation(RSTD, LNV, AF.Exp, scale=-0.5),
                 waits=[("A", p.at("A"))], inc=("A", 1))
            p.op("vector", lambda e: e.tensor_tensor(out=BB, in0=MEAN, in1=RSTD, op=ALU.mult),
                 waits=[("A", p.at("A"))], inc=("V", 1))
            p.op("vector", lambda e: e.tensor_scalar(out=BB, in0=BB, scalar1=-1.0,
                                                     scalar2=None, op0=ALU.mult),
                 waits=[("V", p.at("V"))], inc=("V", 1))
            p.op("tensor", lambda e: e.matmul(pAB0[:, :], s_ones[0:1, :], RSTD,
                                              start=True, stop=True),
                 waits=[("V", p.at("V"))])
            p.op("tensor", lambda e: e.matmul(pAB1[:, :], s_ones[0:1, :], BB,
                                              start=True, stop=True),
                 inc=("T", 1))
            tab = p.at("T")
            for ci in range(8):
                sl = slice(ci * QG, (ci + 1) * QG)
                p.op("vector", lambda e, sl=sl: e.scalar_tensor_tensor(
                    out=s_tmp[:, :], in0=s_xf[:, sl], scalar=1.0, in1=pAB0[:, :],
                    op0=ALU.mult, op1=ALU.mult),
                    waits=[("T", tab), ("V", p.at("V"))], inc=("V", 1))
                p.op("vector", lambda e, sl=sl: e.tensor_tensor(
                    out=s_xn[:, sl], in0=s_tmp[:, :], in1=pAB1[:, :], op=ALU.add),
                    waits=[("V", p.at("V"))], inc=("V", 1))
            vxn = p.at("V")
            for ci in range(8):
                p.dma("gpsimd", xnD[ci * P:(ci + 1) * P, cs], s_xn[:, ci * QG:(ci + 1) * QG],
                      waits=[("V", vxn)] if ci == 0 else [], inc=("dG", 16))

            G = {"bfree": [[("T", prev["T"])], [("T", prev["T"])]],
                 "efree": [prev["dG"], prev["dG"]],
                 "sqfree": prev["T"], "nrfree": prev["V"], "nrfree_g": prev["dG"]}

            def proj(kind, wsb, evac_bias, evac_scale, dstD, normD):
                for ofc in range(8):
                    bank = pMm[ofc % 2]
                    for ci in range(8):
                        sl = slice(ci * QG, (ci + 1) * QG)
                        p.op("tensor", lambda e, sl=sl, bank=bank, ofc=ofc, ci=ci, wsb=wsb: e.matmul(
                            bank[:, :], wsb[:, ci * DM + ofc * P: ci * DM + ofc * P + P],
                            s_xn[:, sl], start=(ci == 0), stop=(ci == 7)),
                            waits=([("V", vxn)] + G["bfree"][ofc % 2]) if ci == 0 else [],
                            inc=("T", 1) if ci == 7 else None)
                    tmm = p.at("T")
                    eb = s_ev[:, (ofc % 2) * QG:(ofc % 2 + 1) * QG]
                    p.op("scalar", lambda e, eb=eb, bank=bank, ofc=ofc, evac_bias=evac_bias,
                         evac_scale=evac_scale: e.activation(
                             eb, bank[:, :], AF.Identity, bias=evac_bias[:, ofc:ofc + 1],
                             scale=evac_scale),
                         waits=[("T", tmm), ("dG", G["efree"][ofc % 2])], inc=("A", 1))
                    aev = p.at("A")
                    p.dma("gpsimd", dstD[ofc * P:(ofc + 1) * P, cs], eb,
                          waits=[("A", aev)], inc=("dG", 16))
                    G["efree"][ofc % 2] = p.at("dG")
                    if normD is not None:
                        nbias = s_bq if kind == "q" else s_bk
                        p.op("scalar", lambda e, bank=bank, ofc=ofc, nbias=nbias: e.activation(
                            s_sq2[:, :], bank[:, :], AF.Square, bias=nbias[:, ofc:ofc + 1]),
                            waits=[("T", G["sqfree"])], inc=("A", 1))
                        asq = p.at("A")
                        G["bfree"][ofc % 2] = [("A", asq)]
                        p.op("tensor", lambda e: e.matmul(
                            pN1[0:1, :], s_onesbf[0:64, 0:1], s_sq2[0:64, :],
                            start=True, stop=True, tile_position=(0, 0)),
                            waits=[("A", asq), ("V", G["nrfree"])])
                        p.op("tensor", lambda e: e.matmul(
                            pN1[64:65, :], s_onesbf[64:128, 0:1], s_sq2[64:128, :],
                            start=True, stop=True, tile_position=(64, 64)),
                            inc=("T", 1))
                        tn = p.at("T")
                        G["sqfree"] = tn
                        p.op("vector", lambda e: e.tensor_copy(out=s_nr[:, :], in_=pN1[0:1, :]),
                             waits=[("T", tn), ("dG", G["nrfree_g"])], inc=("V", 1))
                        p.dma("gpsimd", normD[2 * ofc: 2 * ofc + 1, cs], s_nr[:, :],
                              waits=[("V", p.at("V"))], inc=("dG", 16))
                        p.op("vector", lambda e: e.tensor_copy(out=s_nr[:, :], in_=pN1[64:65, :]),
                             waits=[("dG", p.at("dG"))], inc=("V", 1))
                        G["nrfree"] = p.at("V")
                        p.dma("gpsimd", normD[2 * ofc + 1: 2 * ofc + 2, cs], s_nr[:, :],
                              waits=[("V", p.at("V"))], inc=("dG", 16))
                        G["nrfree_g"] = p.at("dG")
                    else:
                        G["bfree"][ofc % 2] = [("A", p.at("A"))]

            proj("q", s_wqb, s_m2bq, -2.0, qD, qnD)
            proj("k", s_wkb, s_bk, 1.0, kD, knD)
            proj("v", s_wvb, s_bv, 1.0, vD, None)

            for ci in range(8):
                sl = slice(ci * QG, (ci + 1) * QG)
                p.op("tensor", lambda e, sl=sl, ci=ci: e.matmul(
                    pN2[0:NE, :], s_wrb[:, ci * NE:(ci + 1) * NE], s_xn[:, sl],
                    start=(ci == 0), stop=False),
                    waits=[("V", lgfree)] if ci == 0 else [])
            p.op("tensor", lambda e: e.matmul(
                pN2[0:NE, :], s_brb[0:1, :], s_onesbf[0:1, 0:QG],
                start=False, stop=True), inc=("T", 1))
            trt = p.at("T")
            p.op("vector", lambda e: e.tensor_copy(out=s_lgF[:, :], in_=pN2[0:NE, :]),
                 waits=[("T", trt)], inc=("V", 1))
            vlg = p.at("V")
            for tt_ in range(4):
                p.op("tensor", lambda e, tt_=tt_: e.transpose(
                    pN2[:, tt_ * NE:(tt_ + 1) * NE], s_lgF[:, tt_ * P:(tt_ + 1) * P], s_If[0:NE, 0:NE]),
                    waits=[("V", vlg)] if tt_ == 0 else [],
                    inc=("T", 1) if tt_ == 3 else None)
            tlg = p.at("T")
            p.op("vector", lambda e, qg=qg: e.tensor_copy(
                out=s_lgA[:, qg * 64:(qg + 1) * 64], in_=pN2[:, 0:64]),
                waits=[("T", tlg)], inc=("V", 1))
            lgfree = p.at("V")
            prev = p.snap()

        def stages16():
            for k in [2, 4, 8, 16]:
                j = k // 2
                while j >= 1:
                    yield k, j
                    j //= 2

        s1a = p.snap()
        cur, nxt = s_srt[:, 0:QG], s_srt[:, QG:2 * QG]
        p.op("vector", lambda e, cur=cur: e.tensor_copy(out=cur, in_=s_lgA[:, :]),
             waits=[("V", s1a["V"])], inc=("V", 1))
        for (k, j) in stages16():
            kb, ab, c = 16 // k, k // (2 * j), j
            u = max(kb // 2, 1)
            v = 2 if kb > 1 else 1
            vsrc = cur.rearrange("p (t u v a h c) -> p t u v a h c", t=32, u=u, v=v, a=ab, h=2, c=c)
            vdst = nxt.rearrange("p (t u v a h c) -> p t u v a h c", t=32, u=u, v=v, a=ab, h=2, c=c)
            vw = p.at("V")
            if kb > 1:
                ops_ = [(0, ALU.max, 0), (0, ALU.min, 1), (1, ALU.min, 0), (1, ALU.max, 1)]
            else:
                ops_ = [(0, ALU.max, 0), (0, ALU.min, 1)]
            for i_, (par, alu, hv) in enumerate(ops_):
                p.op("vector", lambda e, vsrc=vsrc, vdst=vdst, par=par, alu=alu, hv=hv: e.tensor_tensor(
                    out=vdst[:, :, :, par, :, hv, :],
                    in0=vsrc[:, :, :, par, :, 0, :],
                    in1=vsrc[:, :, :, par, :, 1, :], op=alu),
                    waits=[("V", vw)] if i_ == 0 else [],
                    inc=("V", 1) if i_ == len(ops_) - 1 else None)
            cur, nxt = nxt, cur
        c3 = cur.rearrange("p (t e) -> p t e", e=16)
        th = s_thr.rearrange("p (t e) -> p t e", e=16)
        p.op("vector", lambda e: e.tensor_copy(out=th[:, :, 0:1], in_=c3[:, :, 3:4]),
             waits=[("V", p.at("V"))], inc=("V", 1))
        for (a, b) in [(1, 2), (2, 4), (4, 8), (8, 16)]:
            p.op("vector", lambda e, a=a, b=b: e.tensor_copy(out=th[:, :, a:b], in_=th[:, :, 0:b - a]),
                 waits=[("V", p.at("V"))], inc=("V", 1))
        p.op("vector", lambda e: e.tensor_tensor(out=s_thr[:, :], in0=s_lgA[:, :],
                                                 in1=s_thr[:, :], op=ALU.is_ge),
             waits=[("V", p.at("V"))], inc=("V", 1))
        vmask = p.at("V")
        mq = s_thr.rearrange("p (t e) -> p t e", e=16)
        for g in range(8):
            for tt_ in range(4):
                p.op("tensor", lambda e, g=g, tt_=tt_: e.transpose(
                    pN2[0:NE, tt_ * P:(tt_ + 1) * P], mq[:, 4 * g + tt_, :], s_If[:, :]),
                    waits=[("V", vmask if g == 0 else p.at("V"))] if tt_ == 0 else [],
                    inc=("T", 1) if tt_ == 3 else None)
            tmk = p.at("T")
            p.op("vector", lambda e, g=g: e.tensor_copy(
                out=s_mskF[:, g * QG:(g + 1) * QG], in_=pN2[0:NE, 0:QG]),
                waits=[("T", tmk)], inc=("V", 1))
        p.dma("gpsimd", maskD[:, :], s_mskF[:, :], waits=[("V", p.at("V"))], inc=("dG", 16))
        s1b = p.snap()
        p.emit()

    with contextlib.ExitStack() as st2:
        e2 = st2.enter_context
        s_qp = [e2(nc.sbuf_tensor(f"s_qp{i}", [P, T], BF16)) for i in range(2)]
        s_kp = [e2(nc.sbuf_tensor(f"s_kp{i}", [P, T], BF16)) for i in range(2)]
        s_vF = e2(nc.sbuf_tensor("s_vF", [D, T], BF16))
        s_vb = [e2(nc.sbuf_tensor(f"s_vb{i}", [P, N_QT * D], BF16)) for i in range(2)]
        s_qnw = [e2(nc.sbuf_tensor(f"s_qnw{i}", [2, T], BF16)) for i in range(2)]
        s_knw = [e2(nc.sbuf_tensor(f"s_knw{i}", [2, T], BF16)) for i in range(2)]
        s_lnb = e2(nc.sbuf_tensor("s_lnb", [P, 1024], F32))
        s_rw2 = [e2(nc.sbuf_tensor(f"s_rw2{i}", [P, 2048], F32)) for i in range(2)]
        s_y2 = [e2(nc.sbuf_tensor(f"s_y2{i}", [P, T], F32)) for i in range(2)]
        s_z2 = [e2(nc.sbuf_tensor(f"s_z2{i}", [P, T], F32)) for i in range(2)]
        s_e2 = [e2(nc.sbuf_tensor(f"s_e2{i}", [P, T], BF16)) for i in range(2)]
        s_et = [e2(nc.sbuf_tensor(f"s_et{i}", [P, T], BF16)) for i in range(2)]
        s_ctx2 = [e2(nc.sbuf_tensor(f"s_ctx2{i}", [P, P], BF16)) for i in range(2)]
        s_sm = e2(nc.sbuf_tensor("s_sm", [P, 64], F32))
        s_lamT = [e2(nc.sbuf_tensor(f"s_lamT{i}", [1, P], F32)) for i in range(2)]
        s_lamB = e2(nc.sbuf_tensor("s_lamB", [P, P], F32))
        pA = [e2(nc.psum_tensor(f"pA{i}", [P, 512], F32)) for i in range(3)]
        pB = [e2(nc.psum_tensor(f"pB{i}", [P, 512], F32)) for i in range(2)]
        pT_ = e2(nc.psum_tensor("pT", [P, 1024], BF16))
        pAV = e2(nc.psum_tensor("pAV", [P, 128], F32))
        pSm = e2(nc.psum_tensor("pSm", [P, 512], F32))

        bar = [("V", s1b["V"]), ("A", s1b["A"]), ("T", s1b["T"]), ("dG", s1b["dG"])]

        for i in range(2):
            p.op("vector", lambda e, i=i: e.memset(s_qnw[i][:, :], 1.0),
                 waits=bar if i == 0 else [], inc=("V", 1))
            p.op("vector", lambda e, i=i: e.memset(s_knw[i][:, :], 1.0), inc=("V", 1))
        vmemset = p.at("V")

        head_ld = [0, 0]
        head_cons = [0, 0]
        vb_evac = [0, 0]
        vb_done_prev = 0
        pt_free = 0
        etA_free = [0, 0]
        y_free2 = [0, 0]
        z_free2 = [0, 0]
        e_free2 = [0, 0]
        rwhist = [[0, 0, 0, 0], [0, 0, 0, 0]]
        ctx_free2 = [0, 0]
        pAV_free = 0
        pSm_free = 0
        lamT_free = 0
        bankA_g = [0, 0, 0]
        bankB_g = [0, 0]
        bankA_t = [("A", 0), ("A", 0), ("A", 0)]
        uhist = [[("A", 0)] * 4, [("A", 0)] * 4]
        gdot = 0

        for pair in range(N_PAIR):
            heads = (2 * pair, 2 * pair + 1)
            for i, h in enumerate(heads):
                w_ = list(bar) + [("T", head_cons[i]), ("V", vmemset)]
                p.dma("sync", s_qp[i][0:64, :], qD[h * D:(h + 1) * D, :], waits=w_, inc=("dS", 16))
                p.dma("sync", s_qp[i][64:128, :], qD[h * D:(h + 1) * D, :], inc=("dS", 16))
                p.dma("sync", s_kp[i][0:64, :], kD[h * D:(h + 1) * D, :], inc=("dS", 16))
                p.dma("sync", s_kp[i][64:128, :], kD[h * D:(h + 1) * D, :], inc=("dS", 16))
                p.dma("sync", s_qnw[i][0:1, :], qnD[h:h + 1, :], inc=("dS", 16))
                p.dma("sync", s_knw[i][1:2, :], knD[h:h + 1, :], inc=("dS", 16))
                head_ld[i] = p.at("dS")
                p.dma("sync", s_vF[:, :], vD[h * D:(h + 1) * D, :],
                      waits=[("T", vb_done_prev)], inc=("dS", 16))
                ldv = p.at("dS")
                for half in range(2):
                    for tr in range(16):
                        sc = half * 16 + tr
                        p.op("tensor", lambda e, sc=sc, tr=tr: e.transpose(
                            pT_[:, tr * D:(tr + 1) * D], s_vF[:, sc * P:(sc + 1) * P], s_Ibf[0:D, 0:D]),
                            waits=[("dS", ldv), ("V", pt_free)] if tr == 0 else [],
                            inc=("T", 1) if tr == 15 else None)
                    tvh = p.at("T")
                    p.op("vector", lambda e, i=i, half=half: e.tensor_copy(
                        out=s_vb[i][:, half * 1024:(half + 1) * 1024], in_=pT_[:, :]),
                        waits=[("T", tvh)], inc=("V", 1))
                    pt_free = p.at("V")
                vb_done_prev = p.at("T")
                vb_evac[i] = p.at("V")

            units = [(qt, i) for qt in range(N_QT) for i in range(2)]
            NU = len(units)
            pend = {}
            last_dots_T = [0, 0]

            def emit_head(uidx):
                qt, i = units[uidx]
                qs = slice(qt * P, (qt + 1) * P)
                s_rw = s_rw2[i]; s_y = s_y2[i]
                aLn = []
                vY = []
                rwh = rwhist[i]
                for sc in range(8):
                    ss = slice(sc * 512, (sc + 1) * 512)
                    rsl = slice((sc % 4) * 512, (sc % 4 + 1) * 512)
                    lsl = slice((sc % 2) * 512, (sc % 2 + 1) * 512)
                    w_ = [("dS", head_ld[i]),
                          ("A", bankB_g[sc % 2]), ("V", bankA_g[sc % 3])]
                    p.op("tensor", lambda e, i=i, qs=qs, ss=ss, sc=sc: e.matmul(
                        pA[sc % 3][:, :], s_qp[i][0:64, qs], s_kp[i][0:64, ss],
                        start=True, stop=True, tile_position=(0, 0)),
                        waits=w_)
                    p.op("tensor", lambda e, i=i, qs=qs, ss=ss, sc=sc: e.matmul(
                        pB[sc % 2][:, :], s_qp[i][64:128, qs], s_kp[i][64:128, ss],
                        start=True, stop=False, tile_position=(64, 0)))
                    p.op("tensor", lambda e, i=i, qs=qs, ss=ss, sc=sc: e.matmul(
                        pB[sc % 2][:, :], s_qnw[i][:, qs], s_knw[i][:, ss],
                        start=False, stop=True, tile_position=(0, 0)),
                        inc=("T", 1))
                    tsc = p.at("T")
                    p.op("scalar", lambda e, lsl=lsl, sc=sc: e.activation(
                        s_lnb[:, lsl], pB[sc % 2][:, :], AF.Ln),
                        waits=[("T", tsc)], inc=("A", 1))
                    bankB_g[sc % 2] = p.at("A")
                    p.op("scalar", lambda e, lsl=lsl, rsl=rsl, s_rw=s_rw: e.activation(
                        s_rw[:, rsl], s_lnb[:, lsl], AF.Exp, scale=-0.5),
                        waits=[("A", p.at("A")), ("V", rwh[sc % 4])], inc=("A", 1))
                    aLn.append(p.at("A"))
                    p.op("vector", lambda e, ss=ss, rsl=rsl, sc=sc, s_rw=s_rw, s_y=s_y: e.scalar_tensor_tensor(
                        out=s_y[:, ss], in0=s_rw[:, rsl], scalar=s_slhB[:, :],
                        in1=pA[sc % 3][:, :], op0=ALU.mult, op1=ALU.mult),
                        waits=[("A", aLn[-1]), ("A", y_free2[i])], inc=("V", 1))
                    vY.append(p.at("V"))
                    rwh[sc % 4] = vY[-1]
                    bankA_g[sc % 3] = vY[-1]
                last_dots_T[i] = p.at("T")
                pend[uidx] = {"vY": vY}

            def emit_zparts(uidx):
                qt, i = units[uidx]
                s_y = s_y2[i]; s_z = s_z2[i]
                vY = pend[uidx]["vY"]
                p.op("scalar", lambda e, s_y=s_y, s_z=s_z: e.activation(
                    s_z[:, 2048:4096], s_y[:, 2048:4096], AF.Square),
                    waits=[("V", vY[7]), ("A", z_free2[i])], inc=("A", 1))
                azhi = p.at("A")
                pend[uidx]["azhi"] = azhi
                p.op("vector", lambda e, s_y=s_y, s_z=s_z: e.tensor_tensor(
                    out=s_z[:, 0:2048], in0=s_y[:, 0:2048], in1=s_y[:, 0:2048], op=ALU.mult),
                    waits=[("V", vY[3]), ("A", z_free2[i])], inc=("V", 1))
                vzlo = p.at("V")
                smb = (qt % 2) * 8 + i * 4
                mcol = s_sm[:, smb:smb + 1]
                ncol = s_sm[:, smb + 1:smb + 2]
                p.op("vector", lambda e, mcol=mcol, s_z=s_z: e.tensor_reduce(
                    out=mcol, in_=s_z[:, :], axis=AX.X, op=ALU.max),
                    waits=[("V", vzlo), ("A", azhi)], inc=("V", 1))
                p.op("vector", lambda e, mcol=mcol, ncol=ncol: e.tensor_scalar(
                    out=ncol, in0=mcol, scalar1=-1.0, scalar2=None, op0=ALU.mult),
                    waits=[("V", p.at("V"))], inc=("V", 1))
                pend[uidx]["vneg"] = p.at("V")
                y_free2[i] = azhi

            def emit_exp(uidx):
                qt, i = units[uidx]
                s_z = s_z2[i]; s_e = s_e2[i]
                smb = (qt % 2) * 8 + i * 4
                ncol = s_sm[:, smb + 1:smb + 2]
                ecol = s_sm[:, smb + 2:smb + 3]
                p.op("scalar", lambda e, ncol=ncol, ecol=ecol, s_z=s_z, s_e=s_e: e.activation(
                    s_e[:, :], s_z[:, :], AF.Exp, bias=ncol, accum_out=ecol),
                    waits=[("V", pend[uidx]["vneg"]), ("T", e_free2[i])], inc=("A", 1))
                pend[uidx]["aexp"] = p.at("A")
                z_free2[i] = p.at("A")

            def emit_rest(uidx):
                nonlocal pt_free, pSm_free, lamT_free
                qt, i = units[uidx]
                s_e = s_e2[i]
                aexp = pend[uidx]["aexp"]
                smb = (qt % 2) * 8 + i * 4
                ecol = s_sm[:, smb + 2:smb + 3]
                lcol = s_sm[:, smb + 3:smb + 4]
                p.op("vector", lambda e, ecol=ecol, lcol=lcol: e.reciprocal(out=lcol, in_=ecol),
                     waits=[("A", aexp)], inc=("V", 1))
                vlam = p.at("V")
                p.op("tensor", lambda e, lcol=lcol, i=i: e.transpose(
                    pSm[0:1, i * P:(i + 1) * P], lcol, s_If[:, :]),
                    waits=[("V", vlam), ("V", pSm_free)], inc=("T", 1))
                p.op("vector", lambda e, i=i: e.tensor_copy(
                    out=s_lamT[i][:, :], in_=pSm[0:1, i * P:(i + 1) * P]),
                    waits=[("T", p.at("T")), ("T", lamT_free)], inc=("V", 1))
                lamT_done = p.at("V")
                tfill = 0
                for half in range(4):
                    for tr in range(8):
                        sc16 = half * 8 + tr
                        p.op("tensor", lambda e, sc16=sc16, tr=tr, s_e=s_e: e.transpose(
                            pT_[:, tr * P:(tr + 1) * P], s_e[:, sc16 * P:(sc16 + 1) * P],
                            s_Ibf[:, :]),
                            waits=[("A", aexp), ("V", pt_free)] if tr == 0 else [],
                            inc=("T", 1) if tr == 7 else None)
                    tfill = p.at("T")
                    p.op("vector", lambda e, i=i, half=half: e.tensor_copy(
                        out=s_et[i][:, half * 1024:(half + 1) * 1024], in_=pT_[:, :]),
                        waits=[("T", tfill), ("T", etA_free[i])], inc=("V", 1))
                    pt_free = p.at("V")
                e_free2[i] = tfill
                return lamT_done, p.at("V")

            def emit_avgroup(qt, lamT_done, et_done):
                nonlocal lamT_free, pSm_free, pAV_free, etA_free, ctx_free2
                qs = slice(qt * P, (qt + 1) * P)
                p.op("tensor", lambda e: e.matmul(
                    pSm[0:64, 256:384], s_ones[0:1, 0:64], s_lamT[0][0:1, :],
                    start=True, stop=True, tile_position=(0, 0)),
                    waits=[("V", lamT_done), ("V", pSm_free)])
                p.op("tensor", lambda e: e.matmul(
                    pSm[64:128, 256:384], s_ones[0:1, 0:64], s_lamT[1][0:1, :],
                    start=True, stop=True, tile_position=(0, 64)),
                    inc=("T", 1))
                tlb = p.at("T")
                lamT_free = tlb
                p.op("vector", lambda e: e.tensor_copy(out=s_lamB[:, :], in_=pSm[:, 256:384]),
                     waits=[("T", tlb)], inc=("V", 1))
                vlamB = p.at("V")
                pSm_free = vlamB
                for sc in range(N_QT):
                    p.op("tensor", lambda e, sc=sc: e.matmul(
                        pAV[0:64, :], s_vb[0][:, sc * D:(sc + 1) * D],
                        s_et[0][:, sc * P:(sc + 1) * P],
                        start=(sc == 0), stop=(sc == N_QT - 1), tile_position=(0, 0)),
                        waits=[("V", et_done), ("V", vb_evac[0]), ("V", vb_evac[1]),
                               ("V", pAV_free)] if sc == 0 else [])
                    p.op("tensor", lambda e, sc=sc: e.matmul(
                        pAV[64:128, :], s_vb[1][:, sc * D:(sc + 1) * D],
                        s_et[1][:, sc * P:(sc + 1) * P],
                        start=(sc == 0), stop=(sc == N_QT - 1), tile_position=(0, 64)),
                        inc=("T", 1) if sc == N_QT - 1 else None)
                tav = p.at("T")
                etA_free = [tav, tav]
                cbuf = s_ctx2[qt % 2]
                p.op("vector", lambda e, cbuf=cbuf: e.tensor_tensor(
                    out=cbuf[:, :], in0=pAV[:, :], in1=s_lamB[:, :], op=ALU.mult),
                    waits=[("T", tav), ("V", vlamB), ("dG", ctx_free2[qt % 2])], inc=("V", 1))
                pAV_free = p.at("V")
                p.dma("gpsimd", ctxD[pair * P:(pair + 1) * P, qs], cbuf[:, :],
                      waits=[("V", pAV_free)], inc=("dG", 16))
                ctx_free2[qt % 2] = p.at("dG")

            def emit_headchunk(uidx, sc):
                nonlocal gdot
                qt, i = units[uidx]
                qs = slice(qt * P, (qt + 1) * P)
                s_rc = s_rw2[i]; s_u = s_y2[i]
                st_ = pend.setdefault(uidx, {"aRc": []})
                ss = slice(sc * 512, (sc + 1) * 512)
                rsl = slice((sc % 4) * 512, (sc % 4 + 1) * 512)
                g3 = gdot % 3
                g2 = gdot % 2
                gdot += 1
                lsl = slice(g2 * 512, (g2 + 1) * 512)
                rwh = rwhist[i]
                uh = uhist[i]
                w_ = [("dS", head_ld[i]),
                      ("A", bankB_g[g2]), bankA_t[g3]]
                p.op("tensor", lambda e, i=i, qs=qs, ss=ss, g3=g3: e.matmul(
                    pA[g3][:, :], s_qp[i][0:64, qs], s_kp[i][0:64, ss],
                    start=True, stop=True, tile_position=(0, 0)),
                    waits=w_)
                p.op("tensor", lambda e, i=i, qs=qs, ss=ss, g2=g2: e.matmul(
                    pB[g2][:, :], s_qp[i][64:128, qs], s_kp[i][64:128, ss],
                    start=True, stop=False, tile_position=(64, 0)))
                p.op("tensor", lambda e, i=i, qs=qs, ss=ss, g2=g2: e.matmul(
                    pB[g2][:, :], s_qnw[i][:, qs], s_knw[i][:, ss],
                    start=False, stop=True, tile_position=(0, 0)),
                    inc=("T", 1))
                tsc = p.at("T")
                p.op("scalar", lambda e, rsl=rsl, g3=g3, s_u=s_u: e.activation(
                    s_u[:, rsl], pA[g3][:, :], AF.Square, scale=s_slhB[:, :]),
                    waits=[("T", tsc), uh[sc % 4]], inc=("A", 1))
                bankA_t[g3] = ("A", p.at("A"))
                p.op("scalar", lambda e, lsl=lsl, g2=g2: e.activation(
                    s_lnb[:, lsl], pB[g2][:, :], AF.Ln),
                    waits=[("T", tsc)], inc=("A", 1))
                bankB_g[g2] = p.at("A")
                p.op("scalar", lambda e, lsl=lsl, rsl=rsl, s_rc=s_rc: e.activation(
                    s_rc[:, rsl], s_lnb[:, lsl], AF.Exp, scale=-1.0),
                    waits=[("A", p.at("A")), ("V", rwh[sc % 4])], inc=("A", 1))
                st_["aRc"].append(p.at("A"))
                s_z = s_z2[i]
                smb = (qt % 2) * 32 + i * 16
                mpcol = s_sm[:, smb + sc:smb + sc + 1]
                p.op("vector", lambda e, ss=ss, rsl=rsl, s_u=s_u, s_rc=s_rc, s_z=s_z: e.tensor_tensor(
                    out=s_z[:, ss], in0=s_u[:, rsl], in1=s_rc[:, rsl], op=ALU.mult),
                    waits=[("A", st_["aRc"][-1]), ("A", z_free2[i])], inc=("V", 1))
                zv = p.at("V")
                rwh[sc % 4] = zv
                uh[sc % 4] = ("V", zv)
                p.op("vector", lambda e, ss=ss, s_z=s_z, mpcol=mpcol: e.tensor_reduce(
                    out=mpcol, in_=s_z[:, ss], axis=AX.X, op=ALU.max),
                    inc=("V", 1))
                st_["vmp"] = p.at("V")
                if sc == 7:
                    last_dots_T[i] = p.at("T")

            def emit_fill(uidx, half):
                nonlocal pt_free
                qt, i = units[uidx]
                s_e = s_e2[i]
                aexp = pend[uidx]["aexp"]
                for tr in range(8):
                    sc16 = half * 8 + tr
                    p.op("tensor", lambda e, sc16=sc16, tr=tr, s_e=s_e: e.transpose(
                        pT_[:, tr * P:(tr + 1) * P], s_e[:, sc16 * P:(sc16 + 1) * P],
                        s_Ibf[:, :]),
                        waits=[("A", aexp), ("V", pt_free)] if tr == 0 else [],
                        inc=("T", 1) if tr == 7 else None)
                tfill = p.at("T")
                p.op("vector", lambda e, i=i, half=half: e.tensor_copy(
                    out=s_et[i][:, half * 1024:(half + 1) * 1024], in_=pT_[:, :]),
                    waits=[("T", tfill), ("T", etA_free[i])], inc=("V", 1))
                pt_free = p.at("V")
                if half == 3:
                    e_free2[i] = tfill
                    pend[uidx]["et_done"] = p.at("V")

            def emit_lamchain(uidx):
                nonlocal pSm_free, lamT_free
                qt, i = units[uidx]
                aexp = pend[uidx]["aexp"]
                smb = (qt % 2) * 32 + i * 16
                ecol = s_sm[:, smb + 10:smb + 11]
                lcol = s_sm[:, smb + 11:smb + 12]
                p.op("vector", lambda e, ecol=ecol, lcol=lcol: e.reciprocal(out=lcol, in_=ecol),
                     waits=[("A", aexp)], inc=("V", 1))
                vlam = p.at("V")
                p.op("tensor", lambda e, lcol=lcol, i=i: e.transpose(
                    pSm[0:1, i * P:(i + 1) * P], lcol, s_If[:, :]),
                    waits=[("V", vlam), ("V", pSm_free)], inc=("T", 1))
                p.op("vector", lambda e, i=i: e.tensor_copy(
                    out=s_lamT[i][:, :], in_=pSm[0:1, i * P:(i + 1) * P]),
                    waits=[("T", p.at("T")), ("T", lamT_free)], inc=("V", 1))
                pend[uidx]["lamT_done"] = p.at("V")

            def emit_tailtiny(uidx):
                qt, i = units[uidx]
                s_z = s_z2[i]
                smb = (qt % 2) * 32 + i * 16
                mcol = s_sm[:, smb + 8:smb + 9]
                ncol = s_sm[:, smb + 9:smb + 10]
                ecol = s_sm[:, smb + 10:smb + 11]
                p.op("vector", lambda e, mcol=mcol, smb=smb: e.tensor_reduce(
                    out=mcol, in_=s_sm[:, smb:smb + 8], axis=AX.X, op=ALU.max),
                    waits=[("V", pend[uidx]["vmp"])], inc=("V", 1))
                p.op("vector", lambda e, mcol=mcol, ncol=ncol: e.tensor_scalar(
                    out=ncol, in0=mcol, scalar1=-1.0, scalar2=None, op0=ALU.mult),
                    waits=[("V", p.at("V"))], inc=("V", 1))
                vneg = p.at("V")
                s_e = s_e2[i]
                p.op("scalar", lambda e, ncol=ncol, ecol=ecol, s_z=s_z, s_e=s_e: e.activation(
                    s_e[:, :], s_z[:, :], AF.Exp, bias=ncol, accum_out=ecol),
                    waits=[("V", vneg), ("T", e_free2[i])], inc=("A", 1))
                pend[uidx]["aexp"] = p.at("A")
                z_free2[i] = p.at("A")

            def emit_tailpair(qt):
                a = qt * 2
                b = a + 1
                nqt = qt + 1
                has_next = nqt < N_QT
                na, nb = nqt * 2, nqt * 2 + 1

                def hcpair(sc):
                    if has_next:
                        emit_headchunk(na, sc)
                        emit_headchunk(nb, sc)

                emit_tailtiny(a)
                if has_next:
                    emit_headchunk(na, 0)
                emit_tailtiny(b)
                if has_next:
                    emit_headchunk(nb, 0)
                emit_lamchain(a)
                hcpair(1)
                emit_lamchain(b)
                emit_fill(a, 0)
                hcpair(2)
                emit_fill(a, 1)
                hcpair(3)
                emit_fill(a, 2)
                hcpair(4)
                emit_fill(a, 3)
                hcpair(5)
                emit_fill(b, 0)
                hcpair(6)
                emit_fill(b, 1)
                emit_fill(b, 2)
                hcpair(7)
                emit_fill(b, 3)
                emit_avgroup(qt)

            def emit_avgroup(qt):
                nonlocal lamT_free, pSm_free, pAV_free, etA_free, ctx_free2
                u0 = qt * 2
                lamT_done = pend[u0 + 1]["lamT_done"]
                et_done = pend[u0 + 1]["et_done"]
                qs = slice(qt * P, (qt + 1) * P)
                p.op("tensor", lambda e: e.matmul(
                    pSm[0:64, 256:384], s_ones[0:1, 0:64], s_lamT[0][0:1, :],
                    start=True, stop=True, tile_position=(0, 0)),
                    waits=[("V", lamT_done), ("V", pSm_free)])
                p.op("tensor", lambda e: e.matmul(
                    pSm[64:128, 256:384], s_ones[0:1, 0:64], s_lamT[1][0:1, :],
                    start=True, stop=True, tile_position=(0, 64)),
                    inc=("T", 1))
                tlb = p.at("T")
                lamT_free = tlb
                p.op("vector", lambda e: e.tensor_copy(out=s_lamB[:, :], in_=pSm[:, 256:384]),
                     waits=[("T", tlb)], inc=("V", 1))
                vlamB = p.at("V")
                pSm_free = vlamB
                for sc in range(N_QT):
                    p.op("tensor", lambda e, sc=sc: e.matmul(
                        pAV[0:64, :], s_vb[0][:, sc * D:(sc + 1) * D],
                        s_et[0][:, sc * P:(sc + 1) * P],
                        start=(sc == 0), stop=(sc == N_QT - 1), tile_position=(0, 0)),
                        waits=[("V", et_done), ("V", vb_evac[0]), ("V", vb_evac[1]),
                               ("V", pAV_free)] if sc == 0 else [])
                    p.op("tensor", lambda e, sc=sc: e.matmul(
                        pAV[64:128, :], s_vb[1][:, sc * D:(sc + 1) * D],
                        s_et[1][:, sc * P:(sc + 1) * P],
                        start=(sc == 0), stop=(sc == N_QT - 1), tile_position=(0, 64)),
                        inc=("T", 1) if sc == N_QT - 1 else None)
                tav = p.at("T")
                etA_free = [tav, tav]
                cbuf = s_ctx2[qt % 2]
                p.op("vector", lambda e, cbuf=cbuf: e.tensor_tensor(
                    out=cbuf[:, :], in0=pAV[:, :], in1=s_lamB[:, :], op=ALU.mult),
                    waits=[("T", tav), ("V", vlamB), ("dG", ctx_free2[qt % 2])], inc=("V", 1))
                pAV_free = p.at("V")
                p.dma("gpsimd", ctxD[pair * P:(pair + 1) * P, qs], cbuf[:, :],
                      waits=[("V", pAV_free)], inc=("dG", 16))
                ctx_free2[qt % 2] = p.at("dG")

            for sc in range(8):
                emit_headchunk(0, sc)
                emit_headchunk(1, sc)
            for qt in range(N_QT):
                emit_tailpair(qt)
                pend.pop(qt * 2 - 2, None)
                pend.pop(qt * 2 - 1, None)

            head_cons = [last_dots_T[0], last_dots_T[1]]

        s2 = p.snap()
        p.emit()

    with contextlib.ExitStack() as st3:
        e3 = st3.enter_context
        s_w1 = e3(nc.sbuf_tensor("s_w1", [P, 8 * HID], BF16))
        s_w2 = e3(nc.sbuf_tensor("s_w2", [P, 32 * DM], BF16))
        s_wo = e3(nc.sbuf_tensor("s_wo", [P, 8 * DM], BF16))
        s_xn3 = e3(nc.sbuf_tensor("s_xn3", [P, 8 * QG3], BF16))
        s_cx3 = e3(nc.sbuf_tensor("s_cx3", [P, 8 * QG3], BF16))
        s_x3 = e3(nc.sbuf_tensor("s_x3", [P, 8 * QG3], F32))
        s_hM = e3(nc.sbuf_tensor("s_hM", [P, 32 * QG3], BF16))
        s_hT = e3(nc.sbuf_tensor("s_hT", [P, 2 * QG3], BF16))
        s_mr = e3(nc.sbuf_tensor("s_mr", [1, NE * QG3], BF16))
        s_mB = [e3(nc.sbuf_tensor(f"s_mB{i}", [P, QG3], BF16)) for i in range(2)]
        s_sS = e3(nc.sbuf_tensor("s_sS", [P, 8 * QG3], F32))
        s_g3 = e3(nc.sbuf_tensor("s_g3", [P, 2 * QG3], F32))
        s_o3 = e3(nc.sbuf_tensor("s_o3", [P, 2 * QG3], F32))
        pS3 = [e3(nc.psum_tensor(f"pS3{i}", [P, 512], F32)) for i in range(2)]
        pH3 = [e3(nc.psum_tensor(f"pH3{i}", [P, 512], F32)) for i in range(2)]
        pY3 = [e3(nc.psum_tensor(f"pY3{i}", [P, 512], F32)) for i in range(2)]
        pM3 = e3(nc.psum_tensor("pM3", [P, 512], F32))

        bar2 = [("V", s2["V"]), ("A", s2["A"]), ("T", s2["T"]), ("dG", s2["dG"])]

        for ci in range(8):
            p.dma("sync", s_w1[:, ci * HID:(ci + 1) * HID], w1b[ci * P:(ci + 1) * P, :],
                  waits=bar2 if ci == 0 else [], inc=("dS", 16))
            p.dma("sync", s_wo[:, ci * DM:(ci + 1) * DM], wob[ci * P:(ci + 1) * P, :], inc=("dS", 16))
        for hc in range(32):
            p.dma("sync", s_w2[:, hc * DM:(hc + 1) * DM], w2b[hc * P:(hc + 1) * P, :], inc=("dS", 16))
        s3w = p.at("dS")

        sfree = [0, 0]
        mfree = [0, 0]
        mB_done = [0, 0]
        hfree = [0, 0]
        hTfree = [0, 0]
        hM_free = 0
        yfree = [0, 0]
        gfree = [0, 0]
        ofree = [0, 0]

        prev3 = dict(s2)
        for g3 in range(NQG3):
            cs = slice(g3 * QG3, (g3 + 1) * QG3)
            wprev = [("T", prev3["T"]), ("A", prev3["A"]), ("V", prev3["V"]),
                     ("dG", prev3["dG"]), ("dS", s3w)]
            for ci in range(8):
                p.dma("sync", s_xn3[:, ci * QG3:(ci + 1) * QG3], xnD[ci * P:(ci + 1) * P, cs],
                      waits=wprev if ci == 0 else [], inc=("dS", 16))
                p.dma("sync", s_cx3[:, ci * QG3:(ci + 1) * QG3], ctxD[ci * P:(ci + 1) * P, cs],
                      inc=("dS", 16))
                p.dma("sync", s_x3[:, ci * QG3:(ci + 1) * QG3], xT[ci * P:(ci + 1) * P, cs],
                      inc=("dS", 16))
            for e_ in range(NE):
                p.dma("sync", s_mr[0:1, e_ * QG3:(e_ + 1) * QG3], maskD[e_:e_ + 1, cs],
                      inc=("dS", 16))
            ld3 = p.at("dS")

            for ofc in range(8):
                for hd in range(8):
                    p.op("tensor", lambda e, ofc=ofc, hd=hd: e.matmul(
                        pS3[ofc % 2][:, 0:QG3],
                        s_wo[:, hd * DM + ofc * P: hd * DM + ofc * P + P],
                        s_cx3[:, hd * QG3:(hd + 1) * QG3],
                        start=(hd == 0), stop=(hd == 7)),
                        waits=[("dS", ld3), ("A", sfree[ofc % 2])] if hd == 0 else [],
                        inc=("T", 1) if hd == 7 else None)
                tpr = p.at("T")
                p.op("scalar", lambda e, ofc=ofc: e.activation(
                    s_sS[:, ofc * QG3:(ofc + 1) * QG3], pS3[ofc % 2][:, 0:QG3],
                    AF.Identity, bias=s_bo[:, ofc:ofc + 1]),
                    waits=[("T", tpr), ("V", gfree[0]), ("V", gfree[1])], inc=("A", 1))
                sfree[ofc % 2] = p.at("A")
            aS = p.at("A")

            for hc in range(32):
                ex = hc // 2
                if hc % 2 == 0:
                    p.op("tensor", lambda e, ex=ex: e.matmul(
                        pM3[:, 0:QG3], s_onesbf[0:1, 0:P],
                        s_mr[0:1, ex * QG3:(ex + 1) * QG3],
                        start=True, stop=True),
                        waits=[("dS", ld3), ("V", mfree[ex % 2])],
                        inc=("T", 1))
                    tmb = p.at("T")
                    p.op("vector", lambda e, ex=ex: e.tensor_copy(
                        out=s_mB[ex % 2][:, :], in_=pM3[:, 0:QG3]),
                        waits=[("T", tmb)], inc=("V", 1))
                    mB_done[ex % 2] = p.at("V")
                for ci in range(8):
                    p.op("tensor", lambda e, hc=hc, ci=ci: e.matmul(
                        pH3[hc % 2][:, 0:QG3],
                        s_w1[:, ci * HID + hc * P: ci * HID + hc * P + P],
                        s_xn3[:, ci * QG3:(ci + 1) * QG3],
                        start=(ci == 0), stop=(ci == 7)),
                        waits=[("dS", ld3), ("A", hfree[hc % 2])] if ci == 0 else [],
                        inc=("T", 1) if ci == 7 else None)
                thc = p.at("T")
                p.op("scalar", lambda e, hc=hc: e.activation(
                    s_hT[:, (hc % 2) * QG3:(hc % 2 + 1) * QG3], pH3[hc % 2][:, 0:QG3],
                    AF.Relu, bias=s_b1[:, hc:hc + 1]),
                    waits=[("T", thc), ("V", hTfree[hc % 2])], inc=("A", 1))
                hfree[hc % 2] = p.at("A")
                p.op("vector", lambda e, hc=hc, ex=ex: e.tensor_tensor(
                    out=s_hM[:, hc * QG3:(hc + 1) * QG3],
                    in0=s_hT[:, (hc % 2) * QG3:(hc % 2 + 1) * QG3],
                    in1=s_mB[ex % 2][:, :], op=ALU.mult),
                    waits=[("A", hfree[hc % 2]), ("V", mB_done[ex % 2]),
                           ("T", hM_free)], inc=("V", 1))
                hTfree[hc % 2] = p.at("V")
                mfree[ex % 2] = p.at("V")
            vH = p.at("V")

            for ofc in range(8):
                for hc in range(32):
                    p.op("tensor", lambda e, ofc=ofc, hc=hc: e.matmul(
                        pY3[ofc % 2][:, 0:QG3],
                        s_w2[:, hc * DM + ofc * P: hc * DM + ofc * P + P],
                        s_hM[:, hc * QG3:(hc + 1) * QG3],
                        start=(hc == 0), stop=(hc == 31)),
                        waits=[("V", vH), ("V", yfree[ofc % 2])] if hc == 0 else [],
                        inc=("T", 1) if hc == 31 else None)
                tyc = p.at("T")
                gb = s_g3[:, (ofc % 2) * QG3:(ofc % 2 + 1) * QG3]
                ob = s_o3[:, (ofc % 2) * QG3:(ofc % 2 + 1) * QG3]
                p.op("scalar", lambda e, gb=gb, ofc=ofc: e.activation(
                    gb, pY3[ofc % 2][:, 0:QG3], AF.Silu, bias=s_b2[:, ofc:ofc + 1]),
                    waits=[("T", tyc), ("V", gfree[ofc % 2])], inc=("A", 1))
                ag = p.at("A")
                p.op("vector", lambda e, gb=gb, ofc=ofc: e.tensor_tensor(
                    out=gb, in0=gb, in1=s_sS[:, ofc * QG3:(ofc + 1) * QG3], op=ALU.mult),
                    waits=[("A", ag), ("A", aS)], inc=("V", 1))
                p.op("vector", lambda e, gb=gb, ob=ob, ofc=ofc: e.tensor_tensor(
                    out=ob, in0=gb, in1=s_x3[:, ofc * QG3:(ofc + 1) * QG3], op=ALU.add),
                    waits=[("V", p.at("V")), ("dG", ofree[ofc % 2])], inc=("V", 1))
                vout = p.at("V")
                gfree[ofc % 2] = vout
                yfree[ofc % 2] = vout
                p.dma("gpsimd", outT[ofc * P:(ofc + 1) * P, cs], ob,
                      waits=[("V", vout)], inc=("dG", 16))
                ofree[ofc % 2] = p.at("dG")
            hM_free = p.at("T")
            prev3 = p.snap()

        p.op("gpsimd", lambda e: e.memset(s_g3[0:1, 0:1], 0.0),
             waits=[("dG", p.at("dG"))])
        p.emit()

    es.close()
    p._es.close()
    return nc


def _host_layout(inputs):
    f32 = np.float32
    shared = {}
    shared["wq"] = np.ascontiguousarray(inputs["L0_attn_wq"], dtype=f32)
    shared["wk"] = np.ascontiguousarray(inputs["L0_attn_wk"], dtype=f32)
    shared["wv"] = np.ascontiguousarray(inputs["L0_attn_wv"], dtype=f32)
    shared["wo"] = np.ascontiguousarray(inputs["L0_attn_wo"], dtype=f32)
    shared["w1T"] = np.ascontiguousarray(np.asarray(inputs["L0_ffn_w1"], dtype=f32).T)
    shared["w2T"] = np.ascontiguousarray(np.asarray(inputs["L0_ffn_w2"], dtype=f32).T)
    shared["wr"] = np.ascontiguousarray(inputs["L0_ffn_w_router"], dtype=f32)
    shared["brow"] = np.ascontiguousarray(np.asarray(inputs["L0_ffn_b_router"], dtype=f32).reshape(1, NE))
    for name, key in [("bq", "L0_attn_bq"), ("bk", "L0_attn_bk"), ("bv", "L0_attn_bv"),
                      ("bo", "L0_attn_bo"), ("b2c", "L0_ffn_b2")]:
        shared[name] = np.ascontiguousarray(np.asarray(inputs[key], dtype=f32).reshape(8, P).T)
    shared["b1c"] = np.ascontiguousarray(np.asarray(inputs["L0_ffn_b1"], dtype=f32).reshape(32, P).T)
    shared["alpha"] = np.asarray(inputs["L0_attn_alpha"], dtype=f32).reshape(1, 1)
    shared["cIbf"] = np.eye(P, dtype=f32).astype(ml_dtypes.bfloat16)
    shared["cIf"] = np.eye(P, dtype=f32)
    shared["cones"] = np.ones((1, P), dtype=f32)
    shared["conesbf"] = np.ones((P, QG), dtype=f32).astype(ml_dtypes.bfloat16)
    x = np.asarray(inputs["L0_attn_x"], dtype=f32)
    in_maps = []
    for b in range(NCORE):
        m = dict(shared)
        m["xT"] = np.ascontiguousarray(x[b].T)
        in_maps.append(m)
    return in_maps


def kernel(**inputs):
    in_maps = _host_layout(inputs)
    nc = build_program()
    res = run_bass_kernel_spmd(nc, in_maps, core_ids=list(range(NCORE)))
    out = np.empty((NCORE, T, DM), dtype=np.float32)
    for b in range(NCORE):
        out[b] = np.asarray(res.results[b]["outT"], dtype=np.float32).T
    return out


# version 20
# speedup vs baseline: 1.0580x; 1.0103x over previous
import math
import contextlib
from collections import defaultdict

import numpy as np
import ml_dtypes

import concourse.bass as bass
import concourse.mybir as mybir
from concourse.bass_utils import run_bass_kernel_spmd

F32 = mybir.dt.float32
BF16 = mybir.dt.bfloat16
AF = mybir.ActivationFunctionType
ALU = mybir.AluOpType
AX = mybir.AxisListType

P = 128
DM = 1024
T = 4096
H = 16
D = 64
HID = 4096
NE = 16
QG = 512
NQG = T // QG
QG3 = 256
NQG3 = T // QG3
N_PAIR = 8
N_QT = 32
CLN = math.log(8.0 / math.log1p(64.0))
EPS_LN = 1e-5
NCORE = 8
PIPE_MODE = 2
EXP_SPLIT = False


class Prog:

    ENGS = ("sync", "gpsimd", "tensor", "scalar", "vector")

    def __init__(self, nc):
        self.nc = nc
        self.ops = {e: [] for e in self.ENGS}
        self.cnt = defaultdict(int)
        self.sems = {}
        self._es = contextlib.ExitStack()

    def sem(self, name):
        if name not in self.sems:
            self.sems[name] = self._es.enter_context(self.nc.semaphore(name))
        return self.sems[name]

    def op(self, eng, fn, waits=(), inc=None):
        incs = []
        if inc:
            for (s, k) in (inc if isinstance(inc, list) else [inc]):
                self.cnt[s] += k
                incs.append((s, k))
        self.ops[eng].append((fn, [(s, v) for (s, v) in waits if v > 0], incs))

    def dma(self, eng, out, in_, waits=(), inc=None, **kw):
        def fn(e, out=out, in_=in_, kw=kw):
            return e.dma_start(out=out, in_=in_, **kw)
        self.op(eng, fn, waits, inc)

    def at(self, s):
        return self.cnt[s]

    def snap(self):
        return dict(self.cnt)

    def emit(self):
        with self.nc.Block() as block:
            for eng in self.ENGS:
                ops = self.ops[eng]
                if not ops:
                    continue
                def body(e, ops=ops):
                    seen = defaultdict(int)
                    for fn, waits, incs in ops:
                        for (s, v) in waits:
                            if v > seen[s]:
                                e.wait_ge(self.sems[s], v)
                                seen[s] = v
                        inst = fn(e)
                        for (s, k) in incs:
                            inst.then_inc(self.sems[s], k)
                getattr(block, eng)(body)
        self.ops = {e: [] for e in self.ENGS}


def build_program():
    nc = bass.Bass("TRN2", target_bir_lowering=False)

    def din(name, shape, dt=F32):
        return nc.dram_tensor(name, shape, dt, kind="ExternalInput").ap()

    xT = din("xT", [DM, T])
    wq = din("wq", [DM, DM]); wk = din("wk", [DM, DM])
    wv = din("wv", [DM, DM]); wo = din("wo", [DM, DM])
    w1T = din("w1T", [DM, HID]); w2T = din("w2T", [HID, DM])
    wr = din("wr", [DM, NE])
    brow = din("brow", [1, NE])
    bq = din("bq", [P, 8]); bk = din("bk", [P, 8]); bv = din("bv", [P, 8])
    bo = din("bo", [P, 8]); b2c = din("b2c", [P, 8])
    b1c = din("b1c", [P, 32])
    alpha = din("alpha", [1, 1])
    cIbf = din("cIbf", [P, P], BF16)
    cIf = din("cIf", [P, P])
    cones = din("cones", [1, P])
    conesbf = din("conesbf", [P, QG], BF16)

    outT = nc.dram_tensor("outT", [DM, T], F32, kind="ExternalOutput").ap()

    def scr(name, shape, dt):
        return nc.dram_tensor(name, shape, dt, kind="Internal").ap()

    qD = scr("qD", [DM, T], BF16)
    kD = scr("kD", [DM, T], BF16)
    vD = scr("vD", [DM, T], BF16)
    xnD = scr("xnD", [DM, T], BF16)
    qnD = scr("qnD", [H, T], BF16)
    knD = scr("knD", [H, T], BF16)
    maskD = scr("maskD", [NE, T], BF16)
    ctxD = scr("ctxD", [DM, T], BF16)
    wqb = scr("wqb", [DM, DM], BF16); wkb = scr("wkb", [DM, DM], BF16)
    wvb = scr("wvb", [DM, DM], BF16); wob = scr("wob", [DM, DM], BF16)
    w1b = scr("w1b", [DM, HID], BF16); w2b = scr("w2b", [HID, DM], BF16)
    wrb = scr("wrb", [DM, NE], BF16)
    browb = scr("browb", [1, NE], BF16)

    es = contextlib.ExitStack()
    ec = es.enter_context

    s_bq = ec(nc.sbuf_tensor("s_bq", [P, 8], F32))
    s_m2bq = ec(nc.sbuf_tensor("s_m2bq", [P, 8], F32))
    s_bk = ec(nc.sbuf_tensor("s_bk", [P, 8], F32))
    s_bv = ec(nc.sbuf_tensor("s_bv", [P, 8], F32))
    s_bo = ec(nc.sbuf_tensor("s_bo", [P, 8], F32))
    s_b2 = ec(nc.sbuf_tensor("s_b2", [P, 8], F32))
    s_b1 = ec(nc.sbuf_tensor("s_b1", [P, 32], F32))
    s_alpha = ec(nc.sbuf_tensor("s_alpha", [1, 1], F32))
    s_slh = ec(nc.sbuf_tensor("s_slh", [1, 1], F32))
    s_slhB = ec(nc.sbuf_tensor("s_slhB", [P, 1], F32))
    s_s4B = ec(nc.sbuf_tensor("s_s4B", [P, 1], F32))
    s_Ibf = ec(nc.sbuf_tensor("s_Ibf", [P, P], BF16))
    s_If = ec(nc.sbuf_tensor("s_If", [P, P], F32))
    s_ones = ec(nc.sbuf_tensor("s_ones", [1, P], F32))
    s_eps = ec(nc.sbuf_tensor("s_eps", [1, 1], F32))
    s_onesbf = ec(nc.sbuf_tensor("s_onesbf", [P, QG], BF16))

    p = Prog(nc)
    for s in ("dS", "dG", "T", "A", "V"):
        p.sem(s)

    for ap_, dst in [(alpha, s_alpha), (bq, s_bq), (bk, s_bk), (bv, s_bv),
                     (bo, s_bo), (b2c, s_b2), (b1c, s_b1),
                     (cIbf, s_Ibf), (cIf, s_If), (cones, s_ones), (conesbf, s_onesbf)]:
        p.dma("sync", dst[:], ap_, inc=("dS", 16))
    ld0 = p.at("dS")

    p.op("vector", lambda e: e.memset(s_eps[:, :], EPS_LN), inc=("V", 1))
    p.op("vector", lambda e: e.tensor_scalar(out=s_m2bq[:, :], in0=s_bq[:, :],
                                             scalar1=-2.0, scalar2=None, op0=ALU.mult),
         waits=[("dS", ld0)], inc=("V", 1))
    p.op("scalar", lambda e: e.activation(s_slh[:, :], s_alpha[:, :], AF.Exp, scale=CLN / 2.0),
         waits=[("dS", ld0)], inc=("A", 1))
    p.op("vector", lambda e: e.tensor_scalar(out=s_slh[:, :], in0=s_slh[:, :],
                                             scalar1=0.5, scalar2=None, op0=ALU.mult),
         waits=[("A", p.at("A"))], inc=("V", 1))

    with nc.psum_tensor("pS0", [P, 512], F32) as pS0, \
         nc.sbuf_tensor("s_wst", [P, 2 * HID], F32) as s_wst, \
         nc.sbuf_tensor("s_wbo", [P, 2 * HID], BF16) as s_wbo:
        p.op("tensor", lambda e: e.matmul(pS0[:, 0:1], s_ones[0:1, :], s_slh[0:1, :],
                                          start=True, stop=True),
             waits=[("V", p.at("V"))], inc=("T", 1))
        p.op("vector", lambda e: e.tensor_copy(out=s_slhB[:, :], in_=pS0[:, 0:1]),
             waits=[("T", p.at("T"))], inc=("V", 1))
        p.op("vector", lambda e: e.tensor_tensor(out=s_s4B[:, :], in0=s_slhB[:, :],
                                                 in1=s_slhB[:, :], op=ALU.mult),
             waits=[("V", p.at("V"))], inc=("V", 1))

        jobs = []
        for (src, dst, nch, w) in [(wq, wqb, 8, DM), (wk, wkb, 8, DM),
                                   (wv, wvb, 8, DM), (wo, wob, 8, DM),
                                   (w1T, w1b, 8, HID), (w2T, w2b, 32, DM),
                                   (wr, wrb, 8, NE)]:
            for ci in range(nch):
                jobs.append((src, dst, ci, P, w))
        jobs.append((brow, browb, 0, 1, NE))
        store_done = []
        for idx, (src, dst, ci, rows, w) in enumerate(jobs):
            b = idx % 2
            st = s_wst[:rows, b * HID: b * HID + w]
            ob = s_wbo[:rows, b * HID: b * HID + w]
            waits = [("dG", store_done[idx - 2])] if idx >= 2 else []
            p.dma("sync", st, src[ci * P: ci * P + rows, :], waits=waits, inc=("dS", 16))
            p.op("vector", lambda e, ob=ob, st=st: e.tensor_copy(out=ob, in_=st),
                 waits=[("dS", p.at("dS"))], inc=("V", 1))
            p.dma("gpsimd", dst[ci * P: ci * P + rows, :], ob,
                  waits=[("V", p.at("V"))], inc=("dG", 16))
            store_done.append(p.at("dG"))
        s0 = p.snap()
        p.emit()

    with contextlib.ExitStack() as st1:
        e1 = st1.enter_context
        s_xf = e1(nc.sbuf_tensor("s_xf", [P, 8 * QG], F32))
        s_xb = e1(nc.sbuf_tensor("s_xb", [P, 8 * QG], BF16))
        s_xsq = e1(nc.sbuf_tensor("s_xsq", [P, 8 * QG], BF16))
        s_xn = e1(nc.sbuf_tensor("s_xn", [P, 8 * QG], BF16))
        s_wqb = e1(nc.sbuf_tensor("s_wqb", [P, 8 * DM], BF16))
        s_wkb = e1(nc.sbuf_tensor("s_wkb", [P, 8 * DM], BF16))
        s_wvb = e1(nc.sbuf_tensor("s_wvb", [P, 8 * DM], BF16))
        s_wrb = e1(nc.sbuf_tensor("s_wrb", [P, 8 * NE], BF16))
        s_brb = e1(nc.sbuf_tensor("s_brb", [1, NE], BF16))
        s_stats = e1(nc.sbuf_tensor("s_stats", [1, 6 * QG], F32))
        s_tmp = e1(nc.sbuf_tensor("s_tmp", [P, QG], F32))
        s_ev = e1(nc.sbuf_tensor("s_ev", [P, 2 * QG], BF16))
        s_sq2 = e1(nc.sbuf_tensor("s_sq2", [P, QG], BF16))
        s_nr = e1(nc.sbuf_tensor("s_nr", [1, QG], BF16))
        s_lgF = e1(nc.sbuf_tensor("s_lgF", [NE, QG], F32))
        s_lgA = e1(nc.sbuf_tensor("s_lgA", [P, QG], F32))
        s_srt = e1(nc.sbuf_tensor("s_srt", [P, 2 * QG], F32))
        s_thr = e1(nc.sbuf_tensor("s_thr", [P, QG], F32))
        s_mskF = e1(nc.sbuf_tensor("s_mskF", [NE, T], BF16))
        pStat = e1(nc.psum_tensor("pStat", [P, 512], F32))
        pStat2 = e1(nc.psum_tensor("pStat2", [P, 512], F32))
        pAB0 = e1(nc.psum_tensor("pAB0", [P, 512], F32))
        pAB1 = e1(nc.psum_tensor("pAB1", [P, 512], F32))
        pM0 = e1(nc.psum_tensor("pM0", [P, 512], F32))
        pM1 = e1(nc.psum_tensor("pM1", [P, 512], F32))
        pN1 = e1(nc.psum_tensor("pN1", [P, 512], F32))
        pN2 = e1(nc.psum_tensor("pN2", [P, 512], F32))
        pMm = [pM0, pM1]

        for ci in range(8):
            p.dma("sync", s_wqb[:, ci * DM:(ci + 1) * DM], wqb[ci * P:(ci + 1) * P, :],
                  waits=[("dG", s0["dG"])], inc=("dS", 16))
            p.dma("sync", s_wkb[:, ci * DM:(ci + 1) * DM], wkb[ci * P:(ci + 1) * P, :], inc=("dS", 16))
            p.dma("sync", s_wvb[:, ci * DM:(ci + 1) * DM], wvb[ci * P:(ci + 1) * P, :], inc=("dS", 16))
            p.dma("sync", s_wrb[:, ci * NE:(ci + 1) * NE], wrb[ci * P:(ci + 1) * P, :], inc=("dS", 16))
        p.dma("sync", s_brb[:, :], browb[:, :], inc=("dS", 16))
        s1w = p.at("dS")

        lgfree = 0
        prev = dict(s0)
        for qg in range(NQG):
            cs = slice(qg * QG, (qg + 1) * QG)
            wprev = [("T", prev["T"]), ("A", prev["A"]), ("V", prev["V"]), ("dS", s1w)]
            for ci in range(8):
                p.dma("sync", s_xf[:, ci * QG:(ci + 1) * QG], xT[ci * P:(ci + 1) * P, cs],
                      waits=wprev if ci == 0 else [], inc=("dS", 16))
            ldx = p.at("dS")
            for ci in range(8):
                sl = slice(ci * QG, (ci + 1) * QG)
                p.op("vector", lambda e, sl=sl: e.tensor_copy(out=s_xb[:, sl], in_=s_xf[:, sl]),
                     waits=[("dS", ldx)], inc=("V", 1))
                p.op("scalar", lambda e, sl=sl: e.activation(s_xsq[:, sl], s_xf[:, sl], AF.Square),
                     waits=[("dS", ldx)], inc=("A", 1))
            vcast, acast = p.at("V"), p.at("A")
            for ci in range(8):
                sl = slice(ci * QG, (ci + 1) * QG)
                p.op("tensor", lambda e, sl=sl, ci=ci: e.matmul(
                    pStat[0:1, :], s_onesbf[:, 0:1], s_xb[:, sl],
                    start=(ci == 0), stop=(ci == 7)),
                    waits=[("V", vcast)] if ci == 0 else [])
            for ci in range(8):
                sl = slice(ci * QG, (ci + 1) * QG)
                p.op("tensor", lambda e, sl=sl, ci=ci: e.matmul(
                    pStat2[0:1, :], s_onesbf[:, 0:1], s_xsq[:, sl],
                    start=(ci == 0), stop=(ci == 7)),
                    waits=[("A", acast)] if ci == 0 else [],
                    inc=("T", 1) if ci == 7 else None)
            tstat = p.at("T")
            MEAN = s_stats[0:1, 0:QG]; MSQ = s_stats[0:1, QG:2 * QG]
            VAR = s_stats[0:1, 2 * QG:3 * QG]; RSTD = s_stats[0:1, 3 * QG:4 * QG]
            BB = s_stats[0:1, 4 * QG:5 * QG]; LNV = s_stats[0:1, 5 * QG:6 * QG]
            p.op("vector", lambda e: e.tensor_scalar(out=MEAN, in0=pStat[0:1, :],
                                                     scalar1=1.0 / DM, scalar2=None, op0=ALU.mult),
                 waits=[("T", tstat)], inc=("V", 1))
            p.op("vector", lambda e: e.tensor_scalar(out=MSQ, in0=pStat2[0:1, :],
                                                     scalar1=1.0 / DM, scalar2=None, op0=ALU.mult),
                 inc=("V", 1))
            p.op("vector", lambda e: e.tensor_tensor(out=VAR, in0=MEAN, in1=MEAN, op=ALU.mult),
                 waits=[("V", p.at("V"))], inc=("V", 1))
            p.op("vector", lambda e: e.tensor_tensor(out=VAR, in0=MSQ, in1=VAR, op=ALU.subtract),
                 waits=[("V", p.at("V"))], inc=("V", 1))
            p.op("scalar", lambda e: e.activation(LNV, VAR, AF.Ln, bias=s_eps[0:1, 0:1]),
                 waits=[("V", p.at("V"))], inc=("A", 1))
            p.op("scalar", lambda e: e.activation(RSTD, LNV, AF.Exp, scale=-0.5),
                 waits=[("A", p.at("A"))], inc=("A", 1))
            p.op("vector", lambda e: e.tensor_tensor(out=BB, in0=MEAN, in1=RSTD, op=ALU.mult),
                 waits=[("A", p.at("A"))], inc=("V", 1))
            p.op("vector", lambda e: e.tensor_scalar(out=BB, in0=BB, scalar1=-1.0,
                                                     scalar2=None, op0=ALU.mult),
                 waits=[("V", p.at("V"))], inc=("V", 1))
            p.op("tensor", lambda e: e.matmul(pAB0[:, :], s_ones[0:1, :], RSTD,
                                              start=True, stop=True),
                 waits=[("V", p.at("V"))])
            p.op("tensor", lambda e: e.matmul(pAB1[:, :], s_ones[0:1, :], BB,
                                              start=True, stop=True),
                 inc=("T", 1))
            tab = p.at("T")
            for ci in range(8):
                sl = slice(ci * QG, (ci + 1) * QG)
                p.op("vector", lambda e, sl=sl: e.scalar_tensor_tensor(
                    out=s_tmp[:, :], in0=s_xf[:, sl], scalar=1.0, in1=pAB0[:, :],
                    op0=ALU.mult, op1=ALU.mult),
                    waits=[("T", tab), ("V", p.at("V"))], inc=("V", 1))
                p.op("vector", lambda e, sl=sl: e.tensor_tensor(
                    out=s_xn[:, sl], in0=s_tmp[:, :], in1=pAB1[:, :], op=ALU.add),
                    waits=[("V", p.at("V"))], inc=("V", 1))
            vxn = p.at("V")
            for ci in range(8):
                p.dma("gpsimd", xnD[ci * P:(ci + 1) * P, cs], s_xn[:, ci * QG:(ci + 1) * QG],
                      waits=[("V", vxn)] if ci == 0 else [], inc=("dG", 16))

            G = {"bfree": [[("T", prev["T"])], [("T", prev["T"])]],
                 "efree": [prev["dG"], prev["dG"]],
                 "sqfree": prev["T"], "nrfree": prev["V"], "nrfree_g": prev["dG"]}

            def proj(kind, wsb, evac_bias, evac_scale, dstD, normD):
                for ofc in range(8):
                    bank = pMm[ofc % 2]
                    for ci in range(8):
                        sl = slice(ci * QG, (ci + 1) * QG)
                        p.op("tensor", lambda e, sl=sl, bank=bank, ofc=ofc, ci=ci, wsb=wsb: e.matmul(
                            bank[:, :], wsb[:, ci * DM + ofc * P: ci * DM + ofc * P + P],
                            s_xn[:, sl], start=(ci == 0), stop=(ci == 7)),
                            waits=([("V", vxn)] + G["bfree"][ofc % 2]) if ci == 0 else [],
                            inc=("T", 1) if ci == 7 else None)
                    tmm = p.at("T")
                    eb = s_ev[:, (ofc % 2) * QG:(ofc % 2 + 1) * QG]
                    p.op("scalar", lambda e, eb=eb, bank=bank, ofc=ofc, evac_bias=evac_bias,
                         evac_scale=evac_scale: e.activation(
                             eb, bank[:, :], AF.Identity, bias=evac_bias[:, ofc:ofc + 1],
                             scale=evac_scale),
                         waits=[("T", tmm), ("dG", G["efree"][ofc % 2])], inc=("A", 1))
                    aev = p.at("A")
                    p.dma("gpsimd", dstD[ofc * P:(ofc + 1) * P, cs], eb,
                          waits=[("A", aev)], inc=("dG", 16))
                    G["efree"][ofc % 2] = p.at("dG")
                    if normD is not None:
                        nbias = s_bq if kind == "q" else s_bk
                        p.op("scalar", lambda e, bank=bank, ofc=ofc, nbias=nbias: e.activation(
                            s_sq2[:, :], bank[:, :], AF.Square, bias=nbias[:, ofc:ofc + 1]),
                            waits=[("T", G["sqfree"])], inc=("A", 1))
                        asq = p.at("A")
                        G["bfree"][ofc % 2] = [("A", asq)]
                        p.op("tensor", lambda e: e.matmul(
                            pN1[0:1, :], s_onesbf[0:64, 0:1], s_sq2[0:64, :],
                            start=True, stop=True, tile_position=(0, 0)),
                            waits=[("A", asq), ("V", G["nrfree"])])
                        p.op("tensor", lambda e: e.matmul(
                            pN1[64:65, :], s_onesbf[64:128, 0:1], s_sq2[64:128, :],
                            start=True, stop=True, tile_position=(64, 64)),
                            inc=("T", 1))
                        tn = p.at("T")
                        G["sqfree"] = tn
                        p.op("vector", lambda e: e.tensor_copy(out=s_nr[:, :], in_=pN1[0:1, :]),
                             waits=[("T", tn), ("dG", G["nrfree_g"])], inc=("V", 1))
                        p.dma("gpsimd", normD[2 * ofc: 2 * ofc + 1, cs], s_nr[:, :],
                              waits=[("V", p.at("V"))], inc=("dG", 16))
                        p.op("vector", lambda e: e.tensor_copy(out=s_nr[:, :], in_=pN1[64:65, :]),
                             waits=[("dG", p.at("dG"))], inc=("V", 1))
                        G["nrfree"] = p.at("V")
                        p.dma("gpsimd", normD[2 * ofc + 1: 2 * ofc + 2, cs], s_nr[:, :],
                              waits=[("V", p.at("V"))], inc=("dG", 16))
                        G["nrfree_g"] = p.at("dG")
                    else:
                        G["bfree"][ofc % 2] = [("A", p.at("A"))]

            proj("q", s_wqb, s_m2bq, -2.0, qD, qnD)
            proj("k", s_wkb, s_bk, 1.0, kD, knD)
            proj("v", s_wvb, s_bv, 1.0, vD, None)

            for ci in range(8):
                sl = slice(ci * QG, (ci + 1) * QG)
                p.op("tensor", lambda e, sl=sl, ci=ci: e.matmul(
                    pN2[0:NE, :], s_wrb[:, ci * NE:(ci + 1) * NE], s_xn[:, sl],
                    start=(ci == 0), stop=False),
                    waits=[("V", lgfree)] if ci == 0 else [])
            p.op("tensor", lambda e: e.matmul(
                pN2[0:NE, :], s_brb[0:1, :], s_onesbf[0:1, 0:QG],
                start=False, stop=True), inc=("T", 1))
            trt = p.at("T")
            p.op("vector", lambda e: e.tensor_copy(out=s_lgF[:, :], in_=pN2[0:NE, :]),
                 waits=[("T", trt)], inc=("V", 1))
            vlg = p.at("V")
            for tt_ in range(4):
                p.op("tensor", lambda e, tt_=tt_: e.transpose(
                    pN2[:, tt_ * NE:(tt_ + 1) * NE], s_lgF[:, tt_ * P:(tt_ + 1) * P], s_If[0:NE, 0:NE]),
                    waits=[("V", vlg)] if tt_ == 0 else [],
                    inc=("T", 1) if tt_ == 3 else None)
            tlg = p.at("T")
            p.op("vector", lambda e, qg=qg: e.tensor_copy(
                out=s_lgA[:, qg * 64:(qg + 1) * 64], in_=pN2[:, 0:64]),
                waits=[("T", tlg)], inc=("V", 1))
            lgfree = p.at("V")
            prev = p.snap()

        def stages16():
            for k in [2, 4, 8, 16]:
                j = k // 2
                while j >= 1:
                    yield k, j
                    j //= 2

        s1a = p.snap()
        cur, nxt = s_srt[:, 0:QG], s_srt[:, QG:2 * QG]
        p.op("vector", lambda e, cur=cur: e.tensor_copy(out=cur, in_=s_lgA[:, :]),
             waits=[("V", s1a["V"])], inc=("V", 1))
        for (k, j) in stages16():
            kb, ab, c = 16 // k, k // (2 * j), j
            u = max(kb // 2, 1)
            v = 2 if kb > 1 else 1
            vsrc = cur.rearrange("p (t u v a h c) -> p t u v a h c", t=32, u=u, v=v, a=ab, h=2, c=c)
            vdst = nxt.rearrange("p (t u v a h c) -> p t u v a h c", t=32, u=u, v=v, a=ab, h=2, c=c)
            vw = p.at("V")
            if kb > 1:
                ops_ = [(0, ALU.max, 0), (0, ALU.min, 1), (1, ALU.min, 0), (1, ALU.max, 1)]
            else:
                ops_ = [(0, ALU.max, 0), (0, ALU.min, 1)]
            for i_, (par, alu, hv) in enumerate(ops_):
                p.op("vector", lambda e, vsrc=vsrc, vdst=vdst, par=par, alu=alu, hv=hv: e.tensor_tensor(
                    out=vdst[:, :, :, par, :, hv, :],
                    in0=vsrc[:, :, :, par, :, 0, :],
                    in1=vsrc[:, :, :, par, :, 1, :], op=alu),
                    waits=[("V", vw)] if i_ == 0 else [],
                    inc=("V", 1) if i_ == len(ops_) - 1 else None)
            cur, nxt = nxt, cur
        c3 = cur.rearrange("p (t e) -> p t e", e=16)
        th = s_thr.rearrange("p (t e) -> p t e", e=16)
        p.op("vector", lambda e: e.tensor_copy(out=th[:, :, 0:1], in_=c3[:, :, 3:4]),
             waits=[("V", p.at("V"))], inc=("V", 1))
        for (a, b) in [(1, 2), (2, 4), (4, 8), (8, 16)]:
            p.op("vector", lambda e, a=a, b=b: e.tensor_copy(out=th[:, :, a:b], in_=th[:, :, 0:b - a]),
                 waits=[("V", p.at("V"))], inc=("V", 1))
        p.op("vector", lambda e: e.tensor_tensor(out=s_thr[:, :], in0=s_lgA[:, :],
                                                 in1=s_thr[:, :], op=ALU.is_ge),
             waits=[("V", p.at("V"))], inc=("V", 1))
        vmask = p.at("V")
        mq = s_thr.rearrange("p (t e) -> p t e", e=16)
        for g in range(8):
            for tt_ in range(4):
                p.op("tensor", lambda e, g=g, tt_=tt_: e.transpose(
                    pN2[0:NE, tt_ * P:(tt_ + 1) * P], mq[:, 4 * g + tt_, :], s_If[:, :]),
                    waits=[("V", vmask if g == 0 else p.at("V"))] if tt_ == 0 else [],
                    inc=("T", 1) if tt_ == 3 else None)
            tmk = p.at("T")
            p.op("vector", lambda e, g=g: e.tensor_copy(
                out=s_mskF[:, g * QG:(g + 1) * QG], in_=pN2[0:NE, 0:QG]),
                waits=[("T", tmk)], inc=("V", 1))
        p.dma("gpsimd", maskD[:, :], s_mskF[:, :], waits=[("V", p.at("V"))], inc=("dG", 16))
        s1b = p.snap()
        p.emit()

    with contextlib.ExitStack() as st2:
        e2 = st2.enter_context
        s_qp = [e2(nc.sbuf_tensor(f"s_qp{i}", [P, T], BF16)) for i in range(2)]
        s_kp = [e2(nc.sbuf_tensor(f"s_kp{i}", [P, T], BF16)) for i in range(2)]
        s_vF = e2(nc.sbuf_tensor("s_vF", [D, T], BF16))
        s_vb = [e2(nc.sbuf_tensor(f"s_vb{i}", [P, N_QT * D], BF16)) for i in range(2)]
        s_qnw = [e2(nc.sbuf_tensor(f"s_qnw{i}", [2, T], BF16)) for i in range(2)]
        s_knw = [e2(nc.sbuf_tensor(f"s_knw{i}", [2, T], BF16)) for i in range(2)]
        s_lnb = e2(nc.sbuf_tensor("s_lnb", [P, 1536], F32))
        s_rw2 = [e2(nc.sbuf_tensor(f"s_rw2{i}", [P, 2048], F32)) for i in range(2)]
        s_y2 = [e2(nc.sbuf_tensor(f"s_y2{i}", [P, 2048], F32)) for i in range(2)]
        s_z2 = [e2(nc.sbuf_tensor(f"s_z2{i}", [P, T], F32)) for i in range(2)]
        s_e2 = [e2(nc.sbuf_tensor(f"s_e2{i}", [P, T], BF16)) for i in range(2)]
        s_et = [e2(nc.sbuf_tensor(f"s_et{i}", [P, T], BF16)) for i in range(2)]
        s_ctx2 = [e2(nc.sbuf_tensor(f"s_ctx2{i}", [P, P], BF16)) for i in range(2)]
        s_uc2 = [e2(nc.sbuf_tensor(f"s_uc2{i}", [P, 512], F32)) for i in range(2)]
        s_sm = e2(nc.sbuf_tensor("s_sm", [P, 64], F32))
        s_lamT = [e2(nc.sbuf_tensor(f"s_lamT{i}", [1, P], F32)) for i in range(2)]
        s_lamB = e2(nc.sbuf_tensor("s_lamB", [P, P], F32))
        pA = [e2(nc.psum_tensor(f"pA{i}", [P, 512], F32)) for i in range(2)]
        pB = [e2(nc.psum_tensor(f"pB{i}", [P, 512], F32)) for i in range(3)]
        pT_ = e2(nc.psum_tensor("pT", [P, 1024], BF16))
        pAV = e2(nc.psum_tensor("pAV", [P, 128], F32))
        pSm = e2(nc.psum_tensor("pSm", [P, 512], F32))

        bar = [("V", s1b["V"]), ("A", s1b["A"]), ("T", s1b["T"]), ("dG", s1b["dG"])]

        for i in range(2):
            p.op("vector", lambda e, i=i: e.memset(s_qnw[i][:, :], 1.0),
                 waits=bar if i == 0 else [], inc=("V", 1))
            p.op("vector", lambda e, i=i: e.memset(s_knw[i][:, :], 1.0), inc=("V", 1))
        vmemset = p.at("V")

        head_ld = [0, 0]
        head_cons = [0, 0]
        vb_evac = [0, 0]
        vb_done_prev = 0
        pt_free = 0
        etA_free = [0, 0]
        z_freeh = [[0, 0], [0, 0]]
        e_freeh = [[0, 0], [0, 0]]
        rwhist = [[0, 0, 0, 0], [0, 0, 0, 0]]
        ctx_free2 = [0, 0]
        pAV_free = 0
        pSm_free = 0
        lamT_free = 0
        bankB_g = [0, 0, 0]
        bankA_t = [("A", 0), ("A", 0)]
        uhist = [[("A", 0)] * 4, [("A", 0)] * 4]
        gdot = 0

        for pair in range(N_PAIR):
            heads = (2 * pair, 2 * pair + 1)
            for i, h in enumerate(heads):
                w_ = list(bar) + [("T", head_cons[i]), ("V", vmemset)]
                p.dma("sync", s_qp[i][0:64, :], qD[h * D:(h + 1) * D, :], waits=w_, inc=("dS", 16))
                p.dma("sync", s_qp[i][64:128, :], qD[h * D:(h + 1) * D, :], inc=("dS", 16))
                p.dma("sync", s_kp[i][0:64, :], kD[h * D:(h + 1) * D, :], inc=("dS", 16))
                p.dma("sync", s_kp[i][64:128, :], kD[h * D:(h + 1) * D, :], inc=("dS", 16))
                p.dma("sync", s_qnw[i][0:1, :], qnD[h:h + 1, :], inc=("dS", 16))
                p.dma("sync", s_knw[i][1:2, :], knD[h:h + 1, :], inc=("dS", 16))
                head_ld[i] = p.at("dS")
                p.dma("sync", s_vF[:, :], vD[h * D:(h + 1) * D, :],
                      waits=[("T", vb_done_prev)], inc=("dS", 16))
                ldv = p.at("dS")
                for half in range(2):
                    for tr in range(16):
                        sc = half * 16 + tr
                        p.op("tensor", lambda e, sc=sc, tr=tr: e.transpose(
                            pT_[:, tr * D:(tr + 1) * D], s_vF[:, sc * P:(sc + 1) * P], s_Ibf[0:D, 0:D]),
                            waits=[("dS", ldv), ("V", pt_free)] if tr == 0 else [],
                            inc=("T", 1) if tr == 15 else None)
                    tvh = p.at("T")
                    p.op("vector", lambda e, i=i, half=half: e.tensor_copy(
                        out=s_vb[i][:, half * 1024:(half + 1) * 1024], in_=pT_[:, :]),
                        waits=[("T", tvh)], inc=("V", 1))
                    pt_free = p.at("V")
                vb_done_prev = p.at("T")
                vb_evac[i] = p.at("V")

            units = [(qt, i) for qt in range(N_QT) for i in range(2)]
            NU = len(units)
            pend = {}
            last_dots_T = [0, 0]

            def emit_headchunk(uidx, sc):
                nonlocal gdot
                qt, i = units[uidx]
                qs = slice(qt * P, (qt + 1) * P)
                s_rc = s_rw2[i]; s_u = s_y2[i]
                st_ = pend.setdefault(uidx, {"aRc": []})
                ss = slice(sc * 512, (sc + 1) * 512)
                rsl = slice((sc % 4) * 512, (sc % 4 + 1) * 512)
                g3 = gdot % 2
                g2 = gdot % 3
                gdot += 1
                lsl = slice(g2 * 512, (g2 + 1) * 512)
                rwh = rwhist[i]
                uh = uhist[i]
                w_ = [("dS", head_ld[i]),
                      ("A", bankB_g[g2]), bankA_t[g3]]
                p.op("tensor", lambda e, i=i, qs=qs, ss=ss, g3=g3: e.matmul(
                    pA[g3][:, :], s_qp[i][0:64, qs], s_kp[i][0:64, ss],
                    start=True, stop=True, tile_position=(0, 0)),
                    waits=w_)
                p.op("tensor", lambda e, i=i, qs=qs, ss=ss, g2=g2: e.matmul(
                    pB[g2][:, :], s_qp[i][64:128, qs], s_kp[i][64:128, ss],
                    start=True, stop=False, tile_position=(64, 0)))
                p.op("tensor", lambda e, i=i, qs=qs, ss=ss, g2=g2: e.matmul(
                    pB[g2][:, :], s_qnw[i][:, qs], s_knw[i][:, ss],
                    start=False, stop=True, tile_position=(0, 0)),
                    inc=("T", 1))
                tsc = p.at("T")
                if sc in (1, 3, 5):
                    s_uc = s_uc2[i]
                    p.op("vector", lambda e, g3=g3, s_uc=s_uc: e.tensor_copy(
                        out=s_uc[:, :], in_=pA[g3][:, :]),
                        waits=[("T", tsc), uh[sc % 4]], inc=("V", 1))
                    bankA_t[g3] = ("V", p.at("V"))
                    p.op("vector", lambda e, rsl=rsl, s_u=s_u, s_uc=s_uc: e.scalar_tensor_tensor(
                        out=s_u[:, rsl], in0=s_uc[:, :], scalar=s_s4B[:, :],
                        in1=s_uc[:, :], op0=ALU.mult, op1=ALU.mult),
                        inc=("V", 1))
                else:
                    p.op("scalar", lambda e, rsl=rsl, g3=g3, s_u=s_u: e.activation(
                        s_u[:, rsl], pA[g3][:, :], AF.Square, scale=s_slhB[:, :]),
                        waits=[("T", tsc), uh[sc % 4]], inc=("A", 1))
                    bankA_t[g3] = ("A", p.at("A"))
                p.op("scalar", lambda e, lsl=lsl, g2=g2: e.activation(
                    s_lnb[:, lsl], pB[g2][:, :], AF.Ln),
                    waits=[("T", tsc)], inc=("A", 1))
                bankB_g[g2] = p.at("A")
                p.op("scalar", lambda e, lsl=lsl, rsl=rsl, s_rc=s_rc: e.activation(
                    s_rc[:, rsl], s_lnb[:, lsl], AF.Exp, scale=-1.0),
                    waits=[("A", p.at("A")), ("V", rwh[sc % 4])], inc=("A", 1))
                st_["aRc"].append(p.at("A"))
                s_z = s_z2[i]
                smb = (qt % 2) * 32 + i * 16
                mpcol = s_sm[:, smb + sc:smb + sc + 1]
                p.op("vector", lambda e, ss=ss, rsl=rsl, s_u=s_u, s_rc=s_rc, s_z=s_z: e.tensor_tensor(
                    out=s_z[:, ss], in0=s_u[:, rsl], in1=s_rc[:, rsl], op=ALU.mult),
                    waits=[("A", st_["aRc"][-1]), ("A", z_freeh[i][sc // 4])], inc=("V", 1))
                zv = p.at("V")
                rwh[sc % 4] = zv
                uh[sc % 4] = ("V", zv)
                p.op("vector", lambda e, ss=ss, s_z=s_z, mpcol=mpcol: e.tensor_reduce(
                    out=mpcol, in_=s_z[:, ss], axis=AX.X, op=ALU.max),
                    inc=("V", 1))
                st_["vmp"] = p.at("V")
                if sc == 7:
                    last_dots_T[i] = p.at("T")
                    mcol = s_sm[:, smb + 8:smb + 9]
                    ncol = s_sm[:, smb + 9:smb + 10]
                    p.op("vector", lambda e, mcol=mcol, smb=smb: e.tensor_reduce(
                        out=mcol, in_=s_sm[:, smb:smb + 8], axis=AX.X, op=ALU.max),
                        waits=[("A", z_freeh[i][1])], inc=("V", 1))
                    p.op("vector", lambda e, mcol=mcol, ncol=ncol: e.tensor_scalar(
                        out=ncol, in0=mcol, scalar1=-1.0, scalar2=None, op0=ALU.mult),
                        inc=("V", 1))
                    st_["vneg"] = p.at("V")

            def emit_fill(uidx, half):
                nonlocal pt_free
                qt, i = units[uidx]
                s_e = s_e2[i]
                aexp = pend[uidx]["aexph"][half // 2]
                for tr in range(8):
                    sc16 = half * 8 + tr
                    p.op("tensor", lambda e, sc16=sc16, tr=tr, s_e=s_e: e.transpose(
                        pT_[:, tr * P:(tr + 1) * P], s_e[:, sc16 * P:(sc16 + 1) * P],
                        s_Ibf[:, :]),
                        waits=[("A", aexp), ("V", pt_free)] if tr == 0 else [],
                        inc=("T", 1) if tr == 7 else None)
                tfill = p.at("T")
                p.op("vector", lambda e, i=i, half=half: e.tensor_copy(
                    out=s_et[i][:, half * 1024:(half + 1) * 1024], in_=pT_[:, :]),
                    waits=[("T", tfill), ("T", etA_free[i])], inc=("V", 1))
                pt_free = p.at("V")
                if half == 1:
                    e_freeh[i][0] = tfill
                if half == 3:
                    e_freeh[i][1] = tfill
                    pend[uidx]["et_done"] = p.at("V")

            def emit_lamchain(uidx):
                nonlocal pSm_free, lamT_free
                qt, i = units[uidx]
                aexp = pend[uidx]["aexph"][1]
                EXP_SPLIT_ = EXP_SPLIT
                smb = (qt % 2) * 32 + i * 16
                e0 = s_sm[:, smb + 10:smb + 11]
                e1 = s_sm[:, smb + 11:smb + 12]
                etot = s_sm[:, smb + 12:smb + 13]
                lcol = s_sm[:, smb + 13:smb + 14]
                if EXP_SPLIT:
                    p.op("vector", lambda e, e0=e0, e1=e1, etot=etot: e.tensor_tensor(
                        out=etot, in0=e0, in1=e1, op=ALU.add),
                        waits=[("A", aexp)], inc=("V", 1))
                    p.op("vector", lambda e, etot=etot, lcol=lcol: e.reciprocal(out=lcol, in_=etot),
                         inc=("V", 1))
                else:
                    p.op("vector", lambda e, e0=e0, lcol=lcol: e.reciprocal(out=lcol, in_=e0),
                         waits=[("A", aexp)], inc=("V", 1))
                vlam = p.at("V")
                p.op("tensor", lambda e, lcol=lcol, i=i: e.transpose(
                    pSm[0:1, i * P:(i + 1) * P], lcol, s_If[:, :]),
                    waits=[("V", vlam), ("V", pSm_free)], inc=("T", 1))
                p.op("vector", lambda e, i=i: e.tensor_copy(
                    out=s_lamT[i][:, :], in_=pSm[0:1, i * P:(i + 1) * P]),
                    waits=[("T", p.at("T")), ("T", lamT_free)], inc=("V", 1))
                pend[uidx]["lamT_done"] = p.at("V")

            def emit_exp_h(uidx, h):
                qt, i = units[uidx]
                s_z = s_z2[i]
                smb = (qt % 2) * 32 + i * 16
                ncol = s_sm[:, smb + 9:smb + 10]
                s_e = s_e2[i]
                if not EXP_SPLIT:
                    if h == 1:
                        return
                    ecol = s_sm[:, smb + 10:smb + 11]
                    p.op("scalar", lambda e, ncol=ncol, ecol=ecol, s_z=s_z, s_e=s_e: e.activation(
                        s_e[:, :], s_z[:, :], AF.Exp, bias=ncol, accum_out=ecol),
                        waits=[("V", pend[uidx]["vneg"]), ("T", e_freeh[i][0]),
                               ("T", e_freeh[i][1])], inc=("A", 1))
                    c = p.at("A")
                    pend[uidx]["aexph"] = [c, c]
                    z_freeh[i][0] = c
                    z_freeh[i][1] = c
                    return
                ecol = s_sm[:, smb + 10 + h:smb + 11 + h]
                hs = slice(h * 2048, (h + 1) * 2048)
                p.op("scalar", lambda e, ncol=ncol, ecol=ecol, s_z=s_z, s_e=s_e, hs=hs: e.activation(
                    s_e[:, hs], s_z[:, hs], AF.Exp, bias=ncol, accum_out=ecol),
                    waits=[("V", pend[uidx]["vneg"]), ("T", e_freeh[i][h])], inc=("A", 1))
                pend[uidx].setdefault("aexph", [0, 0])[h] = p.at("A")
                z_freeh[i][h] = p.at("A")

            def emit_tailpair(qt):
                a = qt * 2
                b = a + 1
                nqt = qt + 1
                has_next = nqt < N_QT
                na, nb = nqt * 2, nqt * 2 + 1

                def hcpair(sc):
                    if has_next:
                        emit_headchunk(na, sc)
                        emit_headchunk(nb, sc)

                emit_exp_h(a, 0)
                emit_exp_h(a, 1)
                if has_next:
                    emit_headchunk(na, 0)
                emit_exp_h(b, 0)
                emit_exp_h(b, 1)
                if has_next:
                    emit_headchunk(nb, 0)
                hcpair(1)
                emit_lamchain(a)
                hcpair(2)
                emit_fill(a, 0)
                emit_lamchain(b)
                hcpair(3)
                emit_fill(a, 1)
                hcpair(4)
                emit_fill(a, 2)
                hcpair(5)
                emit_fill(a, 3)
                emit_fill(b, 0)
                hcpair(6)
                emit_fill(b, 1)
                emit_fill(b, 2)
                hcpair(7)
                emit_fill(b, 3)
                emit_avgroup(qt)

            def emit_avgroup(qt):
                nonlocal lamT_free, pSm_free, pAV_free, etA_free, ctx_free2
                u0 = qt * 2
                lamT_done = pend[u0 + 1]["lamT_done"]
                et_done = pend[u0 + 1]["et_done"]
                qs = slice(qt * P, (qt + 1) * P)
                p.op("tensor", lambda e: e.matmul(
                    pSm[0:64, 256:384], s_ones[0:1, 0:64], s_lamT[0][0:1, :],
                    start=True, stop=True, tile_position=(0, 0)),
                    waits=[("V", lamT_done), ("V", pSm_free)])
                p.op("tensor", lambda e: e.matmul(
                    pSm[64:128, 256:384], s_ones[0:1, 0:64], s_lamT[1][0:1, :],
                    start=True, stop=True, tile_position=(0, 64)),
                    inc=("T", 1))
                tlb = p.at("T")
                lamT_free = tlb
                p.op("vector", lambda e: e.tensor_copy(out=s_lamB[:, :], in_=pSm[:, 256:384]),
                     waits=[("T", tlb)], inc=("V", 1))
                vlamB = p.at("V")
                pSm_free = vlamB
                for sc in range(N_QT):
                    p.op("tensor", lambda e, sc=sc: e.matmul(
                        pAV[0:64, 0:128], s_vb[0][:, sc * D:(sc + 1) * D],
                        s_et[0][:, sc * P:(sc + 1) * P],
                        start=(sc == 0), stop=(sc == N_QT - 1), tile_position=(0, 0)),
                        waits=[("V", et_done), ("V", vb_evac[0]), ("V", vb_evac[1]),
                               ("V", pAV_free)] if sc == 0 else [])
                    p.op("tensor", lambda e, sc=sc: e.matmul(
                        pAV[64:128, 0:128], s_vb[1][:, sc * D:(sc + 1) * D],
                        s_et[1][:, sc * P:(sc + 1) * P],
                        start=(sc == 0), stop=(sc == N_QT - 1), tile_position=(0, 64)),
                        inc=("T", 1) if sc == N_QT - 1 else None)
                tav = p.at("T")
                etA_free = [tav, tav]
                cbuf = s_ctx2[qt % 2]
                p.op("vector", lambda e, cbuf=cbuf: e.tensor_tensor(
                    out=cbuf[:, :], in0=pAV[:, 0:128], in1=s_lamB[:, :], op=ALU.mult),
                    waits=[("T", tav), ("V", vlamB), ("dG", ctx_free2[qt % 2])], inc=("V", 1))
                pAV_free = p.at("V")
                p.dma("gpsimd", ctxD[pair * P:(pair + 1) * P, qs], cbuf[:, :],
                      waits=[("V", pAV_free)], inc=("dG", 16))
                ctx_free2[qt % 2] = p.at("dG")

            for sc in range(8):
                emit_headchunk(0, sc)
                emit_headchunk(1, sc)
            for qt in range(N_QT):
                emit_tailpair(qt)
                pend.pop(qt * 2 - 2, None)
                pend.pop(qt * 2 - 1, None)

            head_cons = [last_dots_T[0], last_dots_T[1]]

        s2 = p.snap()
        p.emit()

    with contextlib.ExitStack() as st3:
        e3 = st3.enter_context
        s_w1 = e3(nc.sbuf_tensor("s_w1", [P, 8 * HID], BF16))
        s_w2 = e3(nc.sbuf_tensor("s_w2", [P, 32 * DM], BF16))
        s_wo = e3(nc.sbuf_tensor("s_wo", [P, 8 * DM], BF16))
        s_xn3 = e3(nc.sbuf_tensor("s_xn3", [P, 8 * QG3], BF16))
        s_cx3 = e3(nc.sbuf_tensor("s_cx3", [P, 8 * QG3], BF16))
        s_x3 = e3(nc.sbuf_tensor("s_x3", [P, 8 * QG3], F32))
        s_hM = e3(nc.sbuf_tensor("s_hM", [P, 32 * QG3], BF16))
        s_hT = e3(nc.sbuf_tensor("s_hT", [P, 2 * QG3], BF16))
        s_mr = e3(nc.sbuf_tensor("s_mr", [1, NE * QG3], BF16))
        s_mB = [e3(nc.sbuf_tensor(f"s_mB{i}", [P, QG3], BF16)) for i in range(2)]
        s_sS = e3(nc.sbuf_tensor("s_sS", [P, 8 * QG3], F32))
        s_g3 = e3(nc.sbuf_tensor("s_g3", [P, 2 * QG3], F32))
        s_o3 = e3(nc.sbuf_tensor("s_o3", [P, 2 * QG3], F32))
        pS3 = [e3(nc.psum_tensor(f"pS3{i}", [P, 512], F32)) for i in range(2)]
        pH3 = [e3(nc.psum_tensor(f"pH3{i}", [P, 512], F32)) for i in range(2)]
        pY3 = [e3(nc.psum_tensor(f"pY3{i}", [P, 512], F32)) for i in range(2)]
        pM3 = e3(nc.psum_tensor("pM3", [P, 512], F32))

        bar2 = [("V", s2["V"]), ("A", s2["A"]), ("T", s2["T"]), ("dG", s2["dG"])]

        for ci in range(8):
            p.dma("sync", s_w1[:, ci * HID:(ci + 1) * HID], w1b[ci * P:(ci + 1) * P, :],
                  waits=bar2 if ci == 0 else [], inc=("dS", 16))
            p.dma("sync", s_wo[:, ci * DM:(ci + 1) * DM], wob[ci * P:(ci + 1) * P, :], inc=("dS", 16))
        for hc in range(32):
            p.dma("sync", s_w2[:, hc * DM:(hc + 1) * DM], w2b[hc * P:(hc + 1) * P, :], inc=("dS", 16))
        s3w = p.at("dS")

        sfree = [0, 0]
        mfree = [0, 0]
        mB_done = [0, 0]
        hfree = [0, 0]
        hTfree = [0, 0]
        hM_free = 0
        yfree = [0, 0]
        gfree = [0, 0]
        ofree = [0, 0]

        prev3 = dict(s2)
        for g3 in range(NQG3):
            cs = slice(g3 * QG3, (g3 + 1) * QG3)
            wprev = [("T", prev3["T"]), ("A", prev3["A"]), ("V", prev3["V"]),
                     ("dG", prev3["dG"]), ("dS", s3w)]
            for ci in range(8):
                p.dma("sync", s_xn3[:, ci * QG3:(ci + 1) * QG3], xnD[ci * P:(ci + 1) * P, cs],
                      waits=wprev if ci == 0 else [], inc=("dS", 16))
                p.dma("sync", s_cx3[:, ci * QG3:(ci + 1) * QG3], ctxD[ci * P:(ci + 1) * P, cs],
                      inc=("dS", 16))
                p.dma("sync", s_x3[:, ci * QG3:(ci + 1) * QG3], xT[ci * P:(ci + 1) * P, cs],
                      inc=("dS", 16))
            for e_ in range(NE):
                p.dma("sync", s_mr[0:1, e_ * QG3:(e_ + 1) * QG3], maskD[e_:e_ + 1, cs],
                      inc=("dS", 16))
            ld3 = p.at("dS")

            for ofc in range(8):
                for hd in range(8):
                    p.op("tensor", lambda e, ofc=ofc, hd=hd: e.matmul(
                        pS3[ofc % 2][:, 0:QG3],
                        s_wo[:, hd * DM + ofc * P: hd * DM + ofc * P + P],
                        s_cx3[:, hd * QG3:(hd + 1) * QG3],
                        start=(hd == 0), stop=(hd == 7)),
                        waits=[("dS", ld3), ("A", sfree[ofc % 2])] if hd == 0 else [],
                        inc=("T", 1) if hd == 7 else None)
                tpr = p.at("T")
                p.op("scalar", lambda e, ofc=ofc: e.activation(
                    s_sS[:, ofc * QG3:(ofc + 1) * QG3], pS3[ofc % 2][:, 0:QG3],
                    AF.Identity, bias=s_bo[:, ofc:ofc + 1]),
                    waits=[("T", tpr), ("V", gfree[0]), ("V", gfree[1])], inc=("A", 1))
                sfree[ofc % 2] = p.at("A")
            aS = p.at("A")

            for hc in range(32):
                ex = hc // 2
                if hc % 2 == 0:
                    p.op("tensor", lambda e, ex=ex: e.matmul(
                        pM3[:, 0:QG3], s_onesbf[0:1, 0:P],
                        s_mr[0:1, ex * QG3:(ex + 1) * QG3],
                        start=True, stop=True),
                        waits=[("dS", ld3), ("V", mfree[ex % 2])],
                        inc=("T", 1))
                    tmb = p.at("T")
                    p.op("vector", lambda e, ex=ex: e.tensor_copy(
                        out=s_mB[ex % 2][:, :], in_=pM3[:, 0:QG3]),
                        waits=[("T", tmb)], inc=("V", 1))
                    mB_done[ex % 2] = p.at("V")
                for ci in range(8):
                    p.op("tensor", lambda e, hc=hc, ci=ci: e.matmul(
                        pH3[hc % 2][:, 0:QG3],
                        s_w1[:, ci * HID + hc * P: ci * HID + hc * P + P],
                        s_xn3[:, ci * QG3:(ci + 1) * QG3],
                        start=(ci == 0), stop=(ci == 7)),
                        waits=[("dS", ld3), ("A", hfree[hc % 2])] if ci == 0 else [],
                        inc=("T", 1) if ci == 7 else None)
                thc = p.at("T")
                p.op("scalar", lambda e, hc=hc: e.activation(
                    s_hT[:, (hc % 2) * QG3:(hc % 2 + 1) * QG3], pH3[hc % 2][:, 0:QG3],
                    AF.Relu, bias=s_b1[:, hc:hc + 1]),
                    waits=[("T", thc), ("V", hTfree[hc % 2])], inc=("A", 1))
                hfree[hc % 2] = p.at("A")
                p.op("vector", lambda e, hc=hc, ex=ex: e.tensor_tensor(
                    out=s_hM[:, hc * QG3:(hc + 1) * QG3],
                    in0=s_hT[:, (hc % 2) * QG3:(hc % 2 + 1) * QG3],
                    in1=s_mB[ex % 2][:, :], op=ALU.mult),
                    waits=[("A", hfree[hc % 2]), ("V", mB_done[ex % 2]),
                           ("T", hM_free)], inc=("V", 1))
                hTfree[hc % 2] = p.at("V")
                mfree[ex % 2] = p.at("V")
            vH = p.at("V")

            for ofc in range(8):
                for hc in range(32):
                    p.op("tensor", lambda e, ofc=ofc, hc=hc: e.matmul(
                        pY3[ofc % 2][:, 0:QG3],
                        s_w2[:, hc * DM + ofc * P: hc * DM + ofc * P + P],
                        s_hM[:, hc * QG3:(hc + 1) * QG3],
                        start=(hc == 0), stop=(hc == 31)),
                        waits=[("V", vH), ("V", yfree[ofc % 2])] if hc == 0 else [],
                        inc=("T", 1) if hc == 31 else None)
                tyc = p.at("T")
                gb = s_g3[:, (ofc % 2) * QG3:(ofc % 2 + 1) * QG3]
                ob = s_o3[:, (ofc % 2) * QG3:(ofc % 2 + 1) * QG3]
                p.op("scalar", lambda e, gb=gb, ofc=ofc: e.activation(
                    gb, pY3[ofc % 2][:, 0:QG3], AF.Silu, bias=s_b2[:, ofc:ofc + 1]),
                    waits=[("T", tyc), ("V", gfree[ofc % 2])], inc=("A", 1))
                ag = p.at("A")
                p.op("vector", lambda e, gb=gb, ofc=ofc: e.tensor_tensor(
                    out=gb, in0=gb, in1=s_sS[:, ofc * QG3:(ofc + 1) * QG3], op=ALU.mult),
                    waits=[("A", ag), ("A", aS)], inc=("V", 1))
                p.op("vector", lambda e, gb=gb, ob=ob, ofc=ofc: e.tensor_tensor(
                    out=ob, in0=gb, in1=s_x3[:, ofc * QG3:(ofc + 1) * QG3], op=ALU.add),
                    waits=[("V", p.at("V")), ("dG", ofree[ofc % 2])], inc=("V", 1))
                vout = p.at("V")
                gfree[ofc % 2] = vout
                yfree[ofc % 2] = vout
                p.dma("gpsimd", outT[ofc * P:(ofc + 1) * P, cs], ob,
                      waits=[("V", vout)], inc=("dG", 16))
                ofree[ofc % 2] = p.at("dG")
            hM_free = p.at("T")
            prev3 = p.snap()

        p.op("gpsimd", lambda e: e.memset(s_g3[0:1, 0:1], 0.0),
             waits=[("dG", p.at("dG"))])
        p.emit()

    es.close()
    p._es.close()
    return nc


def _host_layout(inputs):
    f32 = np.float32
    shared = {}
    shared["wq"] = np.ascontiguousarray(inputs["L0_attn_wq"], dtype=f32)
    shared["wk"] = np.ascontiguousarray(inputs["L0_attn_wk"], dtype=f32)
    shared["wv"] = np.ascontiguousarray(inputs["L0_attn_wv"], dtype=f32)
    shared["wo"] = np.ascontiguousarray(inputs["L0_attn_wo"], dtype=f32)
    shared["w1T"] = np.ascontiguousarray(np.asarray(inputs["L0_ffn_w1"], dtype=f32).T)
    shared["w2T"] = np.ascontiguousarray(np.asarray(inputs["L0_ffn_w2"], dtype=f32).T)
    shared["wr"] = np.ascontiguousarray(inputs["L0_ffn_w_router"], dtype=f32)
    shared["brow"] = np.ascontiguousarray(np.asarray(inputs["L0_ffn_b_router"], dtype=f32).reshape(1, NE))
    for name, key in [("bq", "L0_attn_bq"), ("bk", "L0_attn_bk"), ("bv", "L0_attn_bv"),
                      ("bo", "L0_attn_bo"), ("b2c", "L0_ffn_b2")]:
        shared[name] = np.ascontiguousarray(np.asarray(inputs[key], dtype=f32).reshape(8, P).T)
    shared["b1c"] = np.ascontiguousarray(np.asarray(inputs["L0_ffn_b1"], dtype=f32).reshape(32, P).T)
    shared["alpha"] = np.asarray(inputs["L0_attn_alpha"], dtype=f32).reshape(1, 1)
    shared["cIbf"] = np.eye(P, dtype=f32).astype(ml_dtypes.bfloat16)
    shared["cIf"] = np.eye(P, dtype=f32)
    shared["cones"] = np.ones((1, P), dtype=f32)
    shared["conesbf"] = np.ones((P, QG), dtype=f32).astype(ml_dtypes.bfloat16)
    x = np.asarray(inputs["L0_attn_x"], dtype=f32)
    in_maps = []
    for b in range(NCORE):
        m = dict(shared)
        m["xT"] = np.ascontiguousarray(x[b].T)
        in_maps.append(m)
    return in_maps


def kernel(**inputs):
    in_maps = _host_layout(inputs)
    nc = build_program()
    res = run_bass_kernel_spmd(nc, in_maps, core_ids=list(range(NCORE)))
    out = np.empty((NCORE, T, DM), dtype=np.float32)
    for b in range(NCORE):
        out[b] = np.asarray(res.results[b]["outT"], dtype=np.float32).T
    return out


# version 21 (final)
# speedup vs baseline: 1.0930x; 1.0331x over previous
import math
import contextlib
from collections import defaultdict

import numpy as np
import ml_dtypes

import concourse.bass as bass
import concourse.mybir as mybir
from concourse.bass_utils import run_bass_kernel_spmd

F32 = mybir.dt.float32
BF16 = mybir.dt.bfloat16
AF = mybir.ActivationFunctionType
ALU = mybir.AluOpType
AX = mybir.AxisListType

P = 128
DM = 1024
T = 4096
H = 16
D = 64
HID = 4096
NE = 16
QG = 512
NQG = T // QG
QG3 = 256
NQG3 = T // QG3
N_PAIR = 8
N_QT = 32
CLN = math.log(8.0 / math.log1p(64.0))
EPS_LN = 1e-5
NCORE = 8
PIPE_MODE = 2


class Prog:

    ENGS = ("sync", "gpsimd", "tensor", "scalar", "vector")

    def __init__(self, nc):
        self.nc = nc
        self.ops = {e: [] for e in self.ENGS}
        self.cnt = defaultdict(int)
        self.sems = {}
        self._es = contextlib.ExitStack()

    def sem(self, name):
        if name not in self.sems:
            self.sems[name] = self._es.enter_context(self.nc.semaphore(name))
        return self.sems[name]

    def op(self, eng, fn, waits=(), inc=None):
        incs = []
        if inc:
            for (s, k) in (inc if isinstance(inc, list) else [inc]):
                self.cnt[s] += k
                incs.append((s, k))
        self.ops[eng].append((fn, [(s, v) for (s, v) in waits if v > 0], incs))

    def dma(self, eng, out, in_, waits=(), inc=None, **kw):
        def fn(e, out=out, in_=in_, kw=kw):
            return e.dma_start(out=out, in_=in_, **kw)
        self.op(eng, fn, waits, inc)

    def at(self, s):
        return self.cnt[s]

    def snap(self):
        return dict(self.cnt)

    def emit(self):
        with self.nc.Block() as block:
            for eng in self.ENGS:
                ops = self.ops[eng]
                if not ops:
                    continue
                def body(e, ops=ops):
                    seen = defaultdict(int)
                    for fn, waits, incs in ops:
                        for (s, v) in waits:
                            if v > seen[s]:
                                e.wait_ge(self.sems[s], v)
                                seen[s] = v
                        inst = fn(e)
                        for (s, k) in incs:
                            inst.then_inc(self.sems[s], k)
                getattr(block, eng)(body)
        self.ops = {e: [] for e in self.ENGS}


def build_program():
    nc = bass.Bass("TRN2", target_bir_lowering=False)

    def din(name, shape, dt=F32):
        return nc.dram_tensor(name, shape, dt, kind="ExternalInput").ap()

    xT = din("xT", [DM, T])
    wq = din("wq", [DM, DM]); wk = din("wk", [DM, DM])
    wv = din("wv", [DM, DM]); wo = din("wo", [DM, DM])
    w1T = din("w1T", [DM, HID]); w2T = din("w2T", [HID, DM])
    wr = din("wr", [DM, NE])
    brow = din("brow", [1, NE])
    bq = din("bq", [P, 8]); bk = din("bk", [P, 8]); bv = din("bv", [P, 8])
    bo = din("bo", [P, 8]); b2c = din("b2c", [P, 8])
    b1c = din("b1c", [P, 32])
    alpha = din("alpha", [1, 1])
    cIbf = din("cIbf", [P, P], BF16)
    cIf = din("cIf", [P, P])
    cones = din("cones", [1, P])
    conesbf = din("conesbf", [P, QG], BF16)

    outT = nc.dram_tensor("outT", [DM, T], F32, kind="ExternalOutput").ap()

    def scr(name, shape, dt):
        return nc.dram_tensor(name, shape, dt, kind="Internal").ap()

    qD = scr("qD", [DM, T], BF16)
    kD = scr("kD", [DM, T], BF16)
    vD = scr("vD", [DM, T], BF16)
    xnD = scr("xnD", [DM, T], BF16)
    qnD = scr("qnD", [H, T], BF16)
    knD = scr("knD", [H, T], BF16)
    maskD = scr("maskD", [NE, T], BF16)
    ctxD = scr("ctxD", [DM, T], BF16)
    wqb = scr("wqb", [DM, DM], BF16); wkb = scr("wkb", [DM, DM], BF16)
    wvb = scr("wvb", [DM, DM], BF16); wob = scr("wob", [DM, DM], BF16)
    w1b = scr("w1b", [DM, HID], BF16); w2b = scr("w2b", [HID, DM], BF16)
    wrb = scr("wrb", [DM, NE], BF16)
    browb = scr("browb", [1, NE], BF16)

    es = contextlib.ExitStack()
    ec = es.enter_context

    s_bq = ec(nc.sbuf_tensor("s_bq", [P, 8], F32))
    s_m2bq = ec(nc.sbuf_tensor("s_m2bq", [P, 8], F32))
    s_bk = ec(nc.sbuf_tensor("s_bk", [P, 8], F32))
    s_bv = ec(nc.sbuf_tensor("s_bv", [P, 8], F32))
    s_bo = ec(nc.sbuf_tensor("s_bo", [P, 8], F32))
    s_b2 = ec(nc.sbuf_tensor("s_b2", [P, 8], F32))
    s_b1 = ec(nc.sbuf_tensor("s_b1", [P, 32], F32))
    s_alpha = ec(nc.sbuf_tensor("s_alpha", [1, 1], F32))
    s_slh = ec(nc.sbuf_tensor("s_slh", [1, 1], F32))
    s_slhB = ec(nc.sbuf_tensor("s_slhB", [P, 1], F32))
    s_Ibf = ec(nc.sbuf_tensor("s_Ibf", [P, P], BF16))
    s_If = ec(nc.sbuf_tensor("s_If", [P, P], F32))
    s_ones = ec(nc.sbuf_tensor("s_ones", [1, P], F32))
    s_eps = ec(nc.sbuf_tensor("s_eps", [1, 1], F32))
    s_onesbf = ec(nc.sbuf_tensor("s_onesbf", [P, QG], BF16))

    p = Prog(nc)
    for s in ("dS", "dG", "T", "A", "V"):
        p.sem(s)

    for ap_, dst in [(alpha, s_alpha), (bq, s_bq), (bk, s_bk), (bv, s_bv),
                     (bo, s_bo), (b2c, s_b2), (b1c, s_b1),
                     (cIbf, s_Ibf), (cIf, s_If), (cones, s_ones), (conesbf, s_onesbf)]:
        p.dma("sync", dst[:], ap_, inc=("dS", 16))
    ld0 = p.at("dS")

    p.op("vector", lambda e: e.memset(s_eps[:, :], EPS_LN), inc=("V", 1))
    p.op("vector", lambda e: e.tensor_scalar(out=s_m2bq[:, :], in0=s_bq[:, :],
                                             scalar1=-2.0, scalar2=None, op0=ALU.mult),
         waits=[("dS", ld0)], inc=("V", 1))
    p.op("scalar", lambda e: e.activation(s_slh[:, :], s_alpha[:, :], AF.Exp, scale=CLN / 2.0),
         waits=[("dS", ld0)], inc=("A", 1))
    p.op("vector", lambda e: e.tensor_scalar(out=s_slh[:, :], in0=s_slh[:, :],
                                             scalar1=0.5, scalar2=None, op0=ALU.mult),
         waits=[("A", p.at("A"))], inc=("V", 1))

    with nc.psum_tensor("pS0", [P, 512], F32) as pS0, \
         nc.sbuf_tensor("s_wst", [P, 2 * HID], F32) as s_wst, \
         nc.sbuf_tensor("s_wbo", [P, 2 * HID], BF16) as s_wbo:
        p.op("tensor", lambda e: e.matmul(pS0[:, 0:1], s_ones[0:1, :], s_slh[0:1, :],
                                          start=True, stop=True),
             waits=[("V", p.at("V"))], inc=("T", 1))
        p.op("vector", lambda e: e.tensor_copy(out=s_slhB[:, :], in_=pS0[:, 0:1]),
             waits=[("T", p.at("T"))], inc=("V", 1))

        jobs = []
        for (src, dst, nch, w) in [(wq, wqb, 8, DM), (wk, wkb, 8, DM),
                                   (wv, wvb, 8, DM), (wo, wob, 8, DM),
                                   (w1T, w1b, 8, HID), (w2T, w2b, 32, DM),
                                   (wr, wrb, 8, NE)]:
            for ci in range(nch):
                jobs.append((src, dst, ci, P, w))
        jobs.append((brow, browb, 0, 1, NE))
        store_done = []
        for idx, (src, dst, ci, rows, w) in enumerate(jobs):
            b = idx % 2
            st = s_wst[:rows, b * HID: b * HID + w]
            ob = s_wbo[:rows, b * HID: b * HID + w]
            waits = [("dG", store_done[idx - 2])] if idx >= 2 else []
            p.dma("sync", st, src[ci * P: ci * P + rows, :], waits=waits, inc=("dS", 16))
            p.op("vector", lambda e, ob=ob, st=st: e.tensor_copy(out=ob, in_=st),
                 waits=[("dS", p.at("dS"))], inc=("V", 1))
            p.dma("gpsimd", dst[ci * P: ci * P + rows, :], ob,
                  waits=[("V", p.at("V"))], inc=("dG", 16))
            store_done.append(p.at("dG"))
        s0 = p.snap()
        p.emit()

    with contextlib.ExitStack() as st1:
        e1 = st1.enter_context
        s_xf = e1(nc.sbuf_tensor("s_xf", [P, 8 * QG], F32))
        s_xb = e1(nc.sbuf_tensor("s_xb", [P, 8 * QG], BF16))
        s_xsq = e1(nc.sbuf_tensor("s_xsq", [P, 8 * QG], BF16))
        s_xn = e1(nc.sbuf_tensor("s_xn", [P, 8 * QG], BF16))
        s_wqb = e1(nc.sbuf_tensor("s_wqb", [P, 8 * DM], BF16))
        s_wkb = e1(nc.sbuf_tensor("s_wkb", [P, 8 * DM], BF16))
        s_wvb = e1(nc.sbuf_tensor("s_wvb", [P, 8 * DM], BF16))
        s_wrb = e1(nc.sbuf_tensor("s_wrb", [P, 8 * NE], BF16))
        s_brb = e1(nc.sbuf_tensor("s_brb", [1, NE], BF16))
        s_stats = e1(nc.sbuf_tensor("s_stats", [1, 6 * QG], F32))
        s_tmp = e1(nc.sbuf_tensor("s_tmp", [P, QG], F32))
        s_ev = e1(nc.sbuf_tensor("s_ev", [P, 2 * QG], BF16))
        s_sq2 = e1(nc.sbuf_tensor("s_sq2", [P, QG], BF16))
        s_nr = e1(nc.sbuf_tensor("s_nr", [1, QG], BF16))
        s_lgF = e1(nc.sbuf_tensor("s_lgF", [NE, QG], F32))
        s_lgA = e1(nc.sbuf_tensor("s_lgA", [P, QG], F32))
        s_srt = e1(nc.sbuf_tensor("s_srt", [P, 2 * QG], F32))
        s_thr = e1(nc.sbuf_tensor("s_thr", [P, QG], F32))
        s_mskF = e1(nc.sbuf_tensor("s_mskF", [NE, T], BF16))
        pStat = e1(nc.psum_tensor("pStat", [P, 512], F32))
        pStat2 = e1(nc.psum_tensor("pStat2", [P, 512], F32))
        pAB0 = e1(nc.psum_tensor("pAB0", [P, 512], F32))
        pAB1 = e1(nc.psum_tensor("pAB1", [P, 512], F32))
        pM0 = e1(nc.psum_tensor("pM0", [P, 512], F32))
        pM1 = e1(nc.psum_tensor("pM1", [P, 512], F32))
        pN1 = e1(nc.psum_tensor("pN1", [P, 512], F32))
        pN2 = e1(nc.psum_tensor("pN2", [P, 512], F32))
        pMm = [pM0, pM1]

        for ci in range(8):
            p.dma("sync", s_wqb[:, ci * DM:(ci + 1) * DM], wqb[ci * P:(ci + 1) * P, :],
                  waits=[("dG", s0["dG"])], inc=("dS", 16))
            p.dma("sync", s_wkb[:, ci * DM:(ci + 1) * DM], wkb[ci * P:(ci + 1) * P, :], inc=("dS", 16))
            p.dma("sync", s_wvb[:, ci * DM:(ci + 1) * DM], wvb[ci * P:(ci + 1) * P, :], inc=("dS", 16))
            p.dma("sync", s_wrb[:, ci * NE:(ci + 1) * NE], wrb[ci * P:(ci + 1) * P, :], inc=("dS", 16))
        p.dma("sync", s_brb[:, :], browb[:, :], inc=("dS", 16))
        s1w = p.at("dS")

        lgfree = 0
        prev = dict(s0)
        for qg in range(NQG):
            cs = slice(qg * QG, (qg + 1) * QG)
            wprev = [("T", prev["T"]), ("A", prev["A"]), ("V", prev["V"]), ("dS", s1w)]
            for ci in range(8):
                p.dma("sync", s_xf[:, ci * QG:(ci + 1) * QG], xT[ci * P:(ci + 1) * P, cs],
                      waits=wprev if ci == 0 else [], inc=("dS", 16))
            ldx = p.at("dS")
            for ci in range(8):
                sl = slice(ci * QG, (ci + 1) * QG)
                p.op("vector", lambda e, sl=sl: e.tensor_copy(out=s_xb[:, sl], in_=s_xf[:, sl]),
                     waits=[("dS", ldx)], inc=("V", 1))
                p.op("scalar", lambda e, sl=sl: e.activation(s_xsq[:, sl], s_xf[:, sl], AF.Square),
                     waits=[("dS", ldx)], inc=("A", 1))
            vcast, acast = p.at("V"), p.at("A")
            for ci in range(8):
                sl = slice(ci * QG, (ci + 1) * QG)
                p.op("tensor", lambda e, sl=sl, ci=ci: e.matmul(
                    pStat[0:1, :], s_onesbf[:, 0:1], s_xb[:, sl],
                    start=(ci == 0), stop=(ci == 7)),
                    waits=[("V", vcast)] if ci == 0 else [])
            for ci in range(8):
                sl = slice(ci * QG, (ci + 1) * QG)
                p.op("tensor", lambda e, sl=sl, ci=ci: e.matmul(
                    pStat2[0:1, :], s_onesbf[:, 0:1], s_xsq[:, sl],
                    start=(ci == 0), stop=(ci == 7)),
                    waits=[("A", acast)] if ci == 0 else [],
                    inc=("T", 1) if ci == 7 else None)
            tstat = p.at("T")
            MEAN = s_stats[0:1, 0:QG]; MSQ = s_stats[0:1, QG:2 * QG]
            VAR = s_stats[0:1, 2 * QG:3 * QG]; RSTD = s_stats[0:1, 3 * QG:4 * QG]
            BB = s_stats[0:1, 4 * QG:5 * QG]; LNV = s_stats[0:1, 5 * QG:6 * QG]
            p.op("vector", lambda e: e.tensor_scalar(out=MEAN, in0=pStat[0:1, :],
                                                     scalar1=1.0 / DM, scalar2=None, op0=ALU.mult),
                 waits=[("T", tstat)], inc=("V", 1))
            p.op("vector", lambda e: e.tensor_scalar(out=MSQ, in0=pStat2[0:1, :],
                                                     scalar1=1.0 / DM, scalar2=None, op0=ALU.mult),
                 inc=("V", 1))
            p.op("vector", lambda e: e.tensor_tensor(out=VAR, in0=MEAN, in1=MEAN, op=ALU.mult),
                 waits=[("V", p.at("V"))], inc=("V", 1))
            p.op("vector", lambda e: e.tensor_tensor(out=VAR, in0=MSQ, in1=VAR, op=ALU.subtract),
                 waits=[("V", p.at("V"))], inc=("V", 1))
            p.op("scalar", lambda e: e.activation(LNV, VAR, AF.Ln, bias=s_eps[0:1, 0:1]),
                 waits=[("V", p.at("V"))], inc=("A", 1))
            p.op("scalar", lambda e: e.activation(RSTD, LNV, AF.Exp, scale=-0.5),
                 waits=[("A", p.at("A"))], inc=("A", 1))
            p.op("vector", lambda e: e.tensor_tensor(out=BB, in0=MEAN, in1=RSTD, op=ALU.mult),
                 waits=[("A", p.at("A"))], inc=("V", 1))
            p.op("vector", lambda e: e.tensor_scalar(out=BB, in0=BB, scalar1=-1.0,
                                                     scalar2=None, op0=ALU.mult),
                 waits=[("V", p.at("V"))], inc=("V", 1))
            p.op("tensor", lambda e: e.matmul(pAB0[:, :], s_ones[0:1, :], RSTD,
                                              start=True, stop=True),
                 waits=[("V", p.at("V"))])
            p.op("tensor", lambda e: e.matmul(pAB1[:, :], s_ones[0:1, :], BB,
                                              start=True, stop=True),
                 inc=("T", 1))
            tab = p.at("T")
            for ci in range(8):
                sl = slice(ci * QG, (ci + 1) * QG)
                p.op("vector", lambda e, sl=sl: e.scalar_tensor_tensor(
                    out=s_tmp[:, :], in0=s_xf[:, sl], scalar=1.0, in1=pAB0[:, :],
                    op0=ALU.mult, op1=ALU.mult),
                    waits=[("T", tab), ("V", p.at("V"))], inc=("V", 1))
                p.op("vector", lambda e, sl=sl: e.tensor_tensor(
                    out=s_xn[:, sl], in0=s_tmp[:, :], in1=pAB1[:, :], op=ALU.add),
                    waits=[("V", p.at("V"))], inc=("V", 1))
            vxn = p.at("V")
            for ci in range(8):
                p.dma("gpsimd", xnD[ci * P:(ci + 1) * P, cs], s_xn[:, ci * QG:(ci + 1) * QG],
                      waits=[("V", vxn)] if ci == 0 else [], inc=("dG", 16))

            G = {"bfree": [[("T", prev["T"])], [("T", prev["T"])]],
                 "efree": [prev["dG"], prev["dG"]],
                 "sqfree": prev["T"], "nrfree": prev["V"], "nrfree_g": prev["dG"]}

            def proj(kind, wsb, evac_bias, evac_scale, dstD, normD):
                for ofc in range(8):
                    bank = pMm[ofc % 2]
                    for ci in range(8):
                        sl = slice(ci * QG, (ci + 1) * QG)
                        p.op("tensor", lambda e, sl=sl, bank=bank, ofc=ofc, ci=ci, wsb=wsb: e.matmul(
                            bank[:, :], wsb[:, ci * DM + ofc * P: ci * DM + ofc * P + P],
                            s_xn[:, sl], start=(ci == 0), stop=(ci == 7)),
                            waits=([("V", vxn)] + G["bfree"][ofc % 2]) if ci == 0 else [],
                            inc=("T", 1) if ci == 7 else None)
                    tmm = p.at("T")
                    eb = s_ev[:, (ofc % 2) * QG:(ofc % 2 + 1) * QG]
                    p.op("scalar", lambda e, eb=eb, bank=bank, ofc=ofc, evac_bias=evac_bias,
                         evac_scale=evac_scale: e.activation(
                             eb, bank[:, :], AF.Identity, bias=evac_bias[:, ofc:ofc + 1],
                             scale=evac_scale),
                         waits=[("T", tmm), ("dG", G["efree"][ofc % 2])], inc=("A", 1))
                    aev = p.at("A")
                    p.dma("gpsimd", dstD[ofc * P:(ofc + 1) * P, cs], eb,
                          waits=[("A", aev)], inc=("dG", 16))
                    G["efree"][ofc % 2] = p.at("dG")
                    if normD is not None:
                        nbias = s_bq if kind == "q" else s_bk
                        p.op("scalar", lambda e, bank=bank, ofc=ofc, nbias=nbias: e.activation(
                            s_sq2[:, :], bank[:, :], AF.Square, bias=nbias[:, ofc:ofc + 1]),
                            waits=[("T", G["sqfree"])], inc=("A", 1))
                        asq = p.at("A")
                        G["bfree"][ofc % 2] = [("A", asq)]
                        p.op("tensor", lambda e: e.matmul(
                            pN1[0:1, :], s_onesbf[0:64, 0:1], s_sq2[0:64, :],
                            start=True, stop=True, tile_position=(0, 0)),
                            waits=[("A", asq), ("V", G["nrfree"])])
                        p.op("tensor", lambda e: e.matmul(
                            pN1[64:65, :], s_onesbf[64:128, 0:1], s_sq2[64:128, :],
                            start=True, stop=True, tile_position=(64, 64)),
                            inc=("T", 1))
                        tn = p.at("T")
                        G["sqfree"] = tn
                        p.op("vector", lambda e: e.tensor_copy(out=s_nr[:, :], in_=pN1[0:1, :]),
                             waits=[("T", tn), ("dG", G["nrfree_g"])], inc=("V", 1))
                        p.dma("gpsimd", normD[2 * ofc: 2 * ofc + 1, cs], s_nr[:, :],
                              waits=[("V", p.at("V"))], inc=("dG", 16))
                        p.op("vector", lambda e: e.tensor_copy(out=s_nr[:, :], in_=pN1[64:65, :]),
                             waits=[("dG", p.at("dG"))], inc=("V", 1))
                        G["nrfree"] = p.at("V")
                        p.dma("gpsimd", normD[2 * ofc + 1: 2 * ofc + 2, cs], s_nr[:, :],
                              waits=[("V", p.at("V"))], inc=("dG", 16))
                        G["nrfree_g"] = p.at("dG")
                    else:
                        G["bfree"][ofc % 2] = [("A", p.at("A"))]

            proj("q", s_wqb, s_m2bq, -2.0, qD, qnD)
            proj("k", s_wkb, s_bk, 1.0, kD, knD)
            proj("v", s_wvb, s_bv, 1.0, vD, None)

            for ci in range(8):
                sl = slice(ci * QG, (ci + 1) * QG)
                p.op("tensor", lambda e, sl=sl, ci=ci: e.matmul(
                    pN2[0:NE, :], s_wrb[:, ci * NE:(ci + 1) * NE], s_xn[:, sl],
                    start=(ci == 0), stop=False),
                    waits=[("V", lgfree)] if ci == 0 else [])
            p.op("tensor", lambda e: e.matmul(
                pN2[0:NE, :], s_brb[0:1, :], s_onesbf[0:1, 0:QG],
                start=False, stop=True), inc=("T", 1))
            trt = p.at("T")
            p.op("vector", lambda e: e.tensor_copy(out=s_lgF[:, :], in_=pN2[0:NE, :]),
                 waits=[("T", trt)], inc=("V", 1))
            vlg = p.at("V")
            for tt_ in range(4):
                p.op("tensor", lambda e, tt_=tt_: e.transpose(
                    pN2[:, tt_ * NE:(tt_ + 1) * NE], s_lgF[:, tt_ * P:(tt_ + 1) * P], s_If[0:NE, 0:NE]),
                    waits=[("V", vlg)] if tt_ == 0 else [],
                    inc=("T", 1) if tt_ == 3 else None)
            tlg = p.at("T")
            p.op("vector", lambda e, qg=qg: e.tensor_copy(
                out=s_lgA[:, qg * 64:(qg + 1) * 64], in_=pN2[:, 0:64]),
                waits=[("T", tlg)], inc=("V", 1))
            lgfree = p.at("V")
            prev = p.snap()

        def stages16():
            for k in [2, 4, 8, 16]:
                j = k // 2
                while j >= 1:
                    yield k, j
                    j //= 2

        s1a = p.snap()
        cur, nxt = s_srt[:, 0:QG], s_srt[:, QG:2 * QG]
        p.op("vector", lambda e, cur=cur: e.tensor_copy(out=cur, in_=s_lgA[:, :]),
             waits=[("V", s1a["V"])], inc=("V", 1))
        for (k, j) in stages16():
            kb, ab, c = 16 // k, k // (2 * j), j
            u = max(kb // 2, 1)
            v = 2 if kb > 1 else 1
            vsrc = cur.rearrange("p (t u v a h c) -> p t u v a h c", t=32, u=u, v=v, a=ab, h=2, c=c)
            vdst = nxt.rearrange("p (t u v a h c) -> p t u v a h c", t=32, u=u, v=v, a=ab, h=2, c=c)
            vw = p.at("V")
            if kb > 1:
                ops_ = [(0, ALU.max, 0), (0, ALU.min, 1), (1, ALU.min, 0), (1, ALU.max, 1)]
            else:
                ops_ = [(0, ALU.max, 0), (0, ALU.min, 1)]
            for i_, (par, alu, hv) in enumerate(ops_):
                p.op("vector", lambda e, vsrc=vsrc, vdst=vdst, par=par, alu=alu, hv=hv: e.tensor_tensor(
                    out=vdst[:, :, :, par, :, hv, :],
                    in0=vsrc[:, :, :, par, :, 0, :],
                    in1=vsrc[:, :, :, par, :, 1, :], op=alu),
                    waits=[("V", vw)] if i_ == 0 else [],
                    inc=("V", 1) if i_ == len(ops_) - 1 else None)
            cur, nxt = nxt, cur
        c3 = cur.rearrange("p (t e) -> p t e", e=16)
        th = s_thr.rearrange("p (t e) -> p t e", e=16)
        p.op("vector", lambda e: e.tensor_copy(out=th[:, :, 0:1], in_=c3[:, :, 3:4]),
             waits=[("V", p.at("V"))], inc=("V", 1))
        for (a, b) in [(1, 2), (2, 4), (4, 8), (8, 16)]:
            p.op("vector", lambda e, a=a, b=b: e.tensor_copy(out=th[:, :, a:b], in_=th[:, :, 0:b - a]),
                 waits=[("V", p.at("V"))], inc=("V", 1))
        p.op("vector", lambda e: e.tensor_tensor(out=s_thr[:, :], in0=s_lgA[:, :],
                                                 in1=s_thr[:, :], op=ALU.is_ge),
             waits=[("V", p.at("V"))], inc=("V", 1))
        vmask = p.at("V")
        mq = s_thr.rearrange("p (t e) -> p t e", e=16)
        for g in range(8):
            for tt_ in range(4):
                p.op("tensor", lambda e, g=g, tt_=tt_: e.transpose(
                    pN2[0:NE, tt_ * P:(tt_ + 1) * P], mq[:, 4 * g + tt_, :], s_If[:, :]),
                    waits=[("V", vmask if g == 0 else p.at("V"))] if tt_ == 0 else [],
                    inc=("T", 1) if tt_ == 3 else None)
            tmk = p.at("T")
            p.op("vector", lambda e, g=g: e.tensor_copy(
                out=s_mskF[:, g * QG:(g + 1) * QG], in_=pN2[0:NE, 0:QG]),
                waits=[("T", tmk)], inc=("V", 1))
        p.dma("gpsimd", maskD[:, :], s_mskF[:, :], waits=[("V", p.at("V"))], inc=("dG", 16))
        s1b = p.snap()
        p.emit()

    with contextlib.ExitStack() as st2:
        e2 = st2.enter_context
        s_qp = [e2(nc.sbuf_tensor(f"s_qp{i}", [P, T], BF16)) for i in range(2)]
        s_kp = [e2(nc.sbuf_tensor(f"s_kp{i}", [P, T], BF16)) for i in range(2)]
        s_vF = e2(nc.sbuf_tensor("s_vF", [D, T], BF16))
        s_vb = [e2(nc.sbuf_tensor(f"s_vb{i}", [P, N_QT * D], BF16)) for i in range(2)]
        s_qnw = [e2(nc.sbuf_tensor(f"s_qnw{i}", [2, T], BF16)) for i in range(2)]
        s_knw = [e2(nc.sbuf_tensor(f"s_knw{i}", [2, T], BF16)) for i in range(2)]
        s_lnb = e2(nc.sbuf_tensor("s_lnb", [P, 1024], F32))
        s_rw2 = [e2(nc.sbuf_tensor(f"s_rw2{i}", [P, 2048], F32)) for i in range(2)]
        s_y2 = [e2(nc.sbuf_tensor(f"s_y2{i}", [P, T], F32)) for i in range(2)]
        s_z2 = [e2(nc.sbuf_tensor(f"s_z2{i}", [P, T], F32)) for i in range(2)]
        s_e2 = [e2(nc.sbuf_tensor(f"s_e2{i}", [P, T], BF16)) for i in range(2)]
        s_et = [e2(nc.sbuf_tensor(f"s_et{i}", [P, T], BF16)) for i in range(2)]
        s_ctx2 = [e2(nc.sbuf_tensor(f"s_ctx2{i}", [P, P], BF16)) for i in range(2)]
        s_sm = e2(nc.sbuf_tensor("s_sm", [P, 16], F32))
        s_lamT = [e2(nc.sbuf_tensor(f"s_lamT{i}", [1, P], F32)) for i in range(2)]
        s_lamB = e2(nc.sbuf_tensor("s_lamB", [P, P], F32))
        pA = [e2(nc.psum_tensor(f"pA{i}", [P, 512], F32)) for i in range(3)]
        pB = [e2(nc.psum_tensor(f"pB{i}", [P, 512], F32)) for i in range(2)]
        pT_ = e2(nc.psum_tensor("pT", [P, 1024], BF16))
        pAV = e2(nc.psum_tensor("pAV", [P, 128], F32))
        pSm = e2(nc.psum_tensor("pSm", [P, 512], F32))

        bar = [("V", s1b["V"]), ("A", s1b["A"]), ("T", s1b["T"]), ("dG", s1b["dG"])]

        for i in range(2):
            p.op("vector", lambda e, i=i: e.memset(s_qnw[i][:, :], 1.0),
                 waits=bar if i == 0 else [], inc=("V", 1))
            p.op("vector", lambda e, i=i: e.memset(s_knw[i][:, :], 1.0), inc=("V", 1))
        vmemset = p.at("V")

        head_ld = [0, 0]
        head_cons = [0, 0]
        vb_evac = [0, 0]
        vb_done_prev = 0
        pt_free = 0
        etA_free = [0, 0]
        y_free2 = [0, 0]
        z_free2 = [0, 0]
        e_free2 = [0, 0]
        rwhist = [[0, 0, 0, 0], [0, 0, 0, 0]]
        ctx_free2 = [0, 0]
        pAV_free = 0
        pSm_free = 0
        lamT_free = 0
        bankA_g = [0, 0, 0]
        bankB_g = [0, 0]

        for pair in range(N_PAIR):
            heads = (2 * pair, 2 * pair + 1)
            for i, h in enumerate(heads):
                w_ = list(bar) + [("T", head_cons[i]), ("V", vmemset)]
                p.dma("sync", s_qp[i][0:64, :], qD[h * D:(h + 1) * D, :], waits=w_, inc=("dS", 16))
                p.dma("sync", s_qp[i][64:128, :], qD[h * D:(h + 1) * D, :], inc=("dS", 16))
                p.dma("sync", s_kp[i][0:64, :], kD[h * D:(h + 1) * D, :], inc=("dS", 16))
                p.dma("sync", s_kp[i][64:128, :], kD[h * D:(h + 1) * D, :], inc=("dS", 16))
                p.dma("sync", s_qnw[i][0:1, :], qnD[h:h + 1, :], inc=("dS", 16))
                p.dma("sync", s_knw[i][1:2, :], knD[h:h + 1, :], inc=("dS", 16))
                head_ld[i] = p.at("dS")
                p.dma("sync", s_vF[:, :], vD[h * D:(h + 1) * D, :],
                      waits=[("T", vb_done_prev)], inc=("dS", 16))
                ldv = p.at("dS")
                for half in range(2):
                    for tr in range(16):
                        sc = half * 16 + tr
                        p.op("tensor", lambda e, sc=sc, tr=tr: e.transpose(
                            pT_[:, tr * D:(tr + 1) * D], s_vF[:, sc * P:(sc + 1) * P], s_Ibf[0:D, 0:D]),
                            waits=[("dS", ldv), ("V", pt_free)] if tr == 0 else [],
                            inc=("T", 1) if tr == 15 else None)
                    tvh = p.at("T")
                    p.op("vector", lambda e, i=i, half=half: e.tensor_copy(
                        out=s_vb[i][:, half * 1024:(half + 1) * 1024], in_=pT_[:, :]),
                        waits=[("T", tvh)], inc=("V", 1))
                    pt_free = p.at("V")
                vb_done_prev = p.at("T")
                vb_evac[i] = p.at("V")

            units = [(qt, i) for qt in range(N_QT) for i in range(2)]
            NU = len(units)
            pend = {}
            last_dots_T = [0, 0]

            def emit_head(uidx):
                qt, i = units[uidx]
                qs = slice(qt * P, (qt + 1) * P)
                s_rw = s_rw2[i]; s_y = s_y2[i]
                aLn = []
                vY = []
                rwh = rwhist[i]
                for sc in range(8):
                    ss = slice(sc * 512, (sc + 1) * 512)
                    rsl = slice((sc % 4) * 512, (sc % 4 + 1) * 512)
                    lsl = slice((sc % 2) * 512, (sc % 2 + 1) * 512)
                    w_ = [("dS", head_ld[i]),
                          ("A", bankB_g[sc % 2]), ("V", bankA_g[sc % 3])]
                    p.op("tensor", lambda e, i=i, qs=qs, ss=ss, sc=sc: e.matmul(
                        pA[sc % 3][:, :], s_qp[i][0:64, qs], s_kp[i][0:64, ss],
                        start=True, stop=True, tile_position=(0, 0)),
                        waits=w_)
                    p.op("tensor", lambda e, i=i, qs=qs, ss=ss, sc=sc: e.matmul(
                        pB[sc % 2][:, :], s_qp[i][64:128, qs], s_kp[i][64:128, ss],
                        start=True, stop=False, tile_position=(64, 0)))
                    p.op("tensor", lambda e, i=i, qs=qs, ss=ss, sc=sc: e.matmul(
                        pB[sc % 2][:, :], s_qnw[i][:, qs], s_knw[i][:, ss],
                        start=False, stop=True, tile_position=(0, 0)),
                        inc=("T", 1))
                    tsc = p.at("T")
                    p.op("scalar", lambda e, lsl=lsl, sc=sc: e.activation(
                        s_lnb[:, lsl], pB[sc % 2][:, :], AF.Ln),
                        waits=[("T", tsc)], inc=("A", 1))
                    bankB_g[sc % 2] = p.at("A")
                    p.op("scalar", lambda e, lsl=lsl, rsl=rsl, s_rw=s_rw: e.activation(
                        s_rw[:, rsl], s_lnb[:, lsl], AF.Exp, scale=-0.5),
                        waits=[("A", p.at("A")), ("V", rwh[sc % 4])], inc=("A", 1))
                    aLn.append(p.at("A"))
                    p.op("vector", lambda e, ss=ss, rsl=rsl, sc=sc, s_rw=s_rw, s_y=s_y: e.scalar_tensor_tensor(
                        out=s_y[:, ss], in0=s_rw[:, rsl], scalar=s_slhB[:, :],
                        in1=pA[sc % 3][:, :], op0=ALU.mult, op1=ALU.mult),
                        waits=[("A", aLn[-1]), ("A", y_free2[i])], inc=("V", 1))
                    vY.append(p.at("V"))
                    rwh[sc % 4] = vY[-1]
                    bankA_g[sc % 3] = vY[-1]
                last_dots_T[i] = p.at("T")
                pend[uidx] = {"vY": vY}

            def emit_zparts(uidx):
                qt, i = units[uidx]
                s_y = s_y2[i]; s_z = s_z2[i]
                vY = pend[uidx]["vY"]
                p.op("scalar", lambda e, s_y=s_y, s_z=s_z: e.activation(
                    s_z[:, 2048:4096], s_y[:, 2048:4096], AF.Square),
                    waits=[("V", vY[7]), ("A", z_free2[i])], inc=("A", 1))
                azhi = p.at("A")
                pend[uidx]["azhi"] = azhi
                p.op("vector", lambda e, s_y=s_y, s_z=s_z: e.tensor_tensor(
                    out=s_z[:, 0:2048], in0=s_y[:, 0:2048], in1=s_y[:, 0:2048], op=ALU.mult),
                    waits=[("V", vY[3]), ("A", z_free2[i])], inc=("V", 1))
                vzlo = p.at("V")
                smb = (qt % 2) * 8 + i * 4
                mcol = s_sm[:, smb:smb + 1]
                ncol = s_sm[:, smb + 1:smb + 2]
                p.op("vector", lambda e, mcol=mcol, s_z=s_z: e.tensor_reduce(
                    out=mcol, in_=s_z[:, :], axis=AX.X, op=ALU.max),
                    waits=[("V", vzlo), ("A", azhi)], inc=("V", 1))
                p.op("vector", lambda e, mcol=mcol, ncol=ncol: e.tensor_scalar(
                    out=ncol, in0=mcol, scalar1=-1.0, scalar2=None, op0=ALU.mult),
                    waits=[("V", p.at("V"))], inc=("V", 1))
                pend[uidx]["vneg"] = p.at("V")
                y_free2[i] = azhi

            def emit_exp(uidx):
                qt, i = units[uidx]
                s_z = s_z2[i]; s_e = s_e2[i]
                smb = (qt % 2) * 8 + i * 4
                ncol = s_sm[:, smb + 1:smb + 2]
                ecol = s_sm[:, smb + 2:smb + 3]
                p.op("scalar", lambda e, ncol=ncol, ecol=ecol, s_z=s_z, s_e=s_e: e.activation(
                    s_e[:, :], s_z[:, :], AF.Exp, bias=ncol, accum_out=ecol),
                    waits=[("V", pend[uidx]["vneg"]), ("T", e_free2[i])], inc=("A", 1))
                pend[uidx]["aexp"] = p.at("A")
                z_free2[i] = p.at("A")

            def emit_rest(uidx):
                nonlocal pt_free, pSm_free, lamT_free
                qt, i = units[uidx]
                s_e = s_e2[i]
                aexp = pend[uidx]["aexp"]
                smb = (qt % 2) * 8 + i * 4
                ecol = s_sm[:, smb + 2:smb + 3]
                lcol = s_sm[:, smb + 3:smb + 4]
                p.op("vector", lambda e, ecol=ecol, lcol=lcol: e.reciprocal(out=lcol, in_=ecol),
                     waits=[("A", aexp)], inc=("V", 1))
                vlam = p.at("V")
                p.op("tensor", lambda e, lcol=lcol, i=i: e.transpose(
                    pSm[0:1, i * P:(i + 1) * P], lcol, s_If[:, :]),
                    waits=[("V", vlam), ("V", pSm_free)], inc=("T", 1))
                p.op("vector", lambda e, i=i: e.tensor_copy(
                    out=s_lamT[i][:, :], in_=pSm[0:1, i * P:(i + 1) * P]),
                    waits=[("T", p.at("T")), ("T", lamT_free)], inc=("V", 1))
                lamT_done = p.at("V")
                tfill = 0
                for half in range(4):
                    for tr in range(8):
                        sc16 = half * 8 + tr
                        p.op("tensor", lambda e, sc16=sc16, tr=tr, s_e=s_e: e.transpose(
                            pT_[:, tr * P:(tr + 1) * P], s_e[:, sc16 * P:(sc16 + 1) * P],
                            s_Ibf[:, :]),
                            waits=[("A", aexp), ("V", pt_free)] if tr == 0 else [],
                            inc=("T", 1) if tr == 7 else None)
                    tfill = p.at("T")
                    p.op("vector", lambda e, i=i, half=half: e.tensor_copy(
                        out=s_et[i][:, half * 1024:(half + 1) * 1024], in_=pT_[:, :]),
                        waits=[("T", tfill), ("T", etA_free[i])], inc=("V", 1))
                    pt_free = p.at("V")
                e_free2[i] = tfill
                return lamT_done, p.at("V")

            def emit_avgroup(qt, lamT_done, et_done):
                nonlocal lamT_free, pSm_free, pAV_free, etA_free, ctx_free2
                qs = slice(qt * P, (qt + 1) * P)
                p.op("tensor", lambda e: e.matmul(
                    pSm[0:64, 256:384], s_ones[0:1, 0:64], s_lamT[0][0:1, :],
                    start=True, stop=True, tile_position=(0, 0)),
                    waits=[("V", lamT_done), ("V", pSm_free)])
                p.op("tensor", lambda e: e.matmul(
                    pSm[64:128, 256:384], s_ones[0:1, 0:64], s_lamT[1][0:1, :],
                    start=True, stop=True, tile_position=(0, 64)),
                    inc=("T", 1))
                tlb = p.at("T")
                lamT_free = tlb
                p.op("vector", lambda e: e.tensor_copy(out=s_lamB[:, :], in_=pSm[:, 256:384]),
                     waits=[("T", tlb)], inc=("V", 1))
                vlamB = p.at("V")
                pSm_free = vlamB
                for sc in range(N_QT):
                    p.op("tensor", lambda e, sc=sc: e.matmul(
                        pAV[0:64, :], s_vb[0][:, sc * D:(sc + 1) * D],
                        s_et[0][:, sc * P:(sc + 1) * P],
                        start=(sc == 0), stop=(sc == N_QT - 1), tile_position=(0, 0)),
                        waits=[("V", et_done), ("V", vb_evac[0]), ("V", vb_evac[1]),
                               ("V", pAV_free)] if sc == 0 else [])
                    p.op("tensor", lambda e, sc=sc: e.matmul(
                        pAV[64:128, :], s_vb[1][:, sc * D:(sc + 1) * D],
                        s_et[1][:, sc * P:(sc + 1) * P],
                        start=(sc == 0), stop=(sc == N_QT - 1), tile_position=(0, 64)),
                        inc=("T", 1) if sc == N_QT - 1 else None)
                tav = p.at("T")
                etA_free = [tav, tav]
                cbuf = s_ctx2[qt % 2]
                p.op("vector", lambda e, cbuf=cbuf: e.tensor_tensor(
                    out=cbuf[:, :], in0=pAV[:, :], in1=s_lamB[:, :], op=ALU.mult),
                    waits=[("T", tav), ("V", vlamB), ("dG", ctx_free2[qt % 2])], inc=("V", 1))
                pAV_free = p.at("V")
                p.dma("gpsimd", ctxD[pair * P:(pair + 1) * P, qs], cbuf[:, :],
                      waits=[("V", pAV_free)], inc=("dG", 16))
                ctx_free2[qt % 2] = p.at("dG")

            def emit_headchunk(uidx, sc):
                qt, i = units[uidx]
                qs = slice(qt * P, (qt + 1) * P)
                s_rw = s_rw2[i]; s_y = s_y2[i]
                st_ = pend.setdefault(uidx, {"aLn": [], "vY": []})
                ss = slice(sc * 512, (sc + 1) * 512)
                rsl = slice((sc % 4) * 512, (sc % 4 + 1) * 512)
                lsl = slice((sc % 2) * 512, (sc % 2 + 1) * 512)
                rwh = rwhist[i]
                w_ = [("dS", head_ld[i]),
                      ("A", bankB_g[sc % 2]), ("V", bankA_g[sc % 3])]
                p.op("tensor", lambda e, i=i, qs=qs, ss=ss, sc=sc: e.matmul(
                    pA[sc % 3][:, :], s_qp[i][0:64, qs], s_kp[i][0:64, ss],
                    start=True, stop=True, tile_position=(0, 0)),
                    waits=w_)
                p.op("tensor", lambda e, i=i, qs=qs, ss=ss, sc=sc: e.matmul(
                    pB[sc % 2][:, :], s_qp[i][64:128, qs], s_kp[i][64:128, ss],
                    start=True, stop=False, tile_position=(64, 0)))
                p.op("tensor", lambda e, i=i, qs=qs, ss=ss, sc=sc: e.matmul(
                    pB[sc % 2][:, :], s_qnw[i][:, qs], s_knw[i][:, ss],
                    start=False, stop=True, tile_position=(0, 0)),
                    inc=("T", 1))
                tsc = p.at("T")
                p.op("scalar", lambda e, lsl=lsl, sc=sc: e.activation(
                    s_lnb[:, lsl], pB[sc % 2][:, :], AF.Ln),
                    waits=[("T", tsc)], inc=("A", 1))
                bankB_g[sc % 2] = p.at("A")
                p.op("scalar", lambda e, lsl=lsl, rsl=rsl, s_rw=s_rw: e.activation(
                    s_rw[:, rsl], s_lnb[:, lsl], AF.Exp, scale=-0.5),
                    waits=[("A", p.at("A")), ("V", rwh[sc % 4])], inc=("A", 1))
                st_["aLn"].append(p.at("A"))
                p.op("vector", lambda e, ss=ss, rsl=rsl, sc=sc, s_rw=s_rw, s_y=s_y: e.scalar_tensor_tensor(
                    out=s_y[:, ss], in0=s_rw[:, rsl], scalar=s_slhB[:, :],
                    in1=pA[sc % 3][:, :], op0=ALU.mult, op1=ALU.mult),
                    waits=[("A", st_["aLn"][-1]), ("A", y_free2[i])], inc=("V", 1))
                st_["vY"].append(p.at("V"))
                rwh[sc % 4] = st_["vY"][-1]
                bankA_g[sc % 3] = st_["vY"][-1]
                if sc == 7:
                    last_dots_T[i] = p.at("T")

            def emit_fill(uidx, half):
                nonlocal pt_free
                qt, i = units[uidx]
                s_e = s_e2[i]
                aexp = pend[uidx]["aexp"]
                for tr in range(8):
                    sc16 = half * 8 + tr
                    p.op("tensor", lambda e, sc16=sc16, tr=tr, s_e=s_e: e.transpose(
                        pT_[:, tr * P:(tr + 1) * P], s_e[:, sc16 * P:(sc16 + 1) * P],
                        s_Ibf[:, :]),
                        waits=[("A", aexp), ("V", pt_free)] if tr == 0 else [],
                        inc=("T", 1) if tr == 7 else None)
                tfill = p.at("T")
                p.op("vector", lambda e, i=i, half=half: e.tensor_copy(
                    out=s_et[i][:, half * 1024:(half + 1) * 1024], in_=pT_[:, :]),
                    waits=[("T", tfill), ("T", etA_free[i])], inc=("V", 1))
                pt_free = p.at("V")
                if half == 3:
                    e_free2[i] = tfill
                    pend[uidx]["et_done"] = p.at("V")

            def emit_lamchain(uidx):
                nonlocal pSm_free, lamT_free
                qt, i = units[uidx]
                aexp = pend[uidx]["aexp"]
                smb = (qt % 2) * 8 + i * 4
                ecol = s_sm[:, smb + 2:smb + 3]
                lcol = s_sm[:, smb + 3:smb + 4]
                p.op("vector", lambda e, ecol=ecol, lcol=lcol: e.reciprocal(out=lcol, in_=ecol),
                     waits=[("A", aexp)], inc=("V", 1))
                vlam = p.at("V")
                p.op("tensor", lambda e, lcol=lcol, i=i: e.transpose(
                    pSm[0:1, i * P:(i + 1) * P], lcol, s_If[:, :]),
                    waits=[("V", vlam), ("V", pSm_free)], inc=("T", 1))
                p.op("vector", lambda e, i=i: e.tensor_copy(
                    out=s_lamT[i][:, :], in_=pSm[0:1, i * P:(i + 1) * P]),
                    waits=[("T", p.at("T")), ("T", lamT_free)], inc=("V", 1))
                pend[uidx]["lamT_done"] = p.at("V")

            def emit_tailzip(uidx):
                qt, i = units[uidx]
                s_y = s_y2[i]; s_z = s_z2[i]
                vY = pend[uidx]["vY"]
                has_next = uidx + 2 < NU

                p.op("scalar", lambda e, s_y=s_y, s_z=s_z: e.activation(
                    s_z[:, 2048:4096], s_y[:, 2048:4096], AF.Square),
                    waits=[("V", vY[7]), ("A", z_free2[i])], inc=("A", 1))
                azhi = p.at("A")
                pend[uidx]["azhi"] = azhi
                p.op("vector", lambda e, s_y=s_y, s_z=s_z: e.tensor_tensor(
                    out=s_z[:, 0:2048], in0=s_y[:, 0:2048], in1=s_y[:, 0:2048], op=ALU.mult),
                    waits=[("V", vY[3]), ("A", z_free2[i])], inc=("V", 1))
                vzlo = p.at("V")
                y_free2[i] = azhi

                smb = (qt % 2) * 8 + i * 4
                mcol = s_sm[:, smb:smb + 1]
                ncol = s_sm[:, smb + 1:smb + 2]
                ecol = s_sm[:, smb + 2:smb + 3]

                if has_next:
                    emit_headchunk(uidx + 2, 0)
                p.op("vector", lambda e, mcol=mcol, s_z=s_z: e.tensor_reduce(
                    out=mcol, in_=s_z[:, :], axis=AX.X, op=ALU.max),
                    waits=[("V", vzlo), ("A", azhi)], inc=("V", 1))
                p.op("vector", lambda e, mcol=mcol, ncol=ncol: e.tensor_scalar(
                    out=ncol, in0=mcol, scalar1=-1.0, scalar2=None, op0=ALU.mult),
                    waits=[("V", p.at("V"))], inc=("V", 1))
                vneg = p.at("V")
                if has_next:
                    emit_headchunk(uidx + 2, 1)
                s_e = s_e2[i]
                p.op("scalar", lambda e, ncol=ncol, ecol=ecol, s_z=s_z, s_e=s_e: e.activation(
                    s_e[:, :], s_z[:, :], AF.Exp, bias=ncol, accum_out=ecol),
                    waits=[("V", vneg), ("T", e_free2[i])], inc=("A", 1))
                pend[uidx]["aexp"] = p.at("A")
                z_free2[i] = p.at("A")
                if has_next:
                    emit_headchunk(uidx + 2, 2)
                emit_lamchain(uidx)
                if has_next:
                    emit_headchunk(uidx + 2, 3)
                emit_fill(uidx, 0)
                if has_next:
                    emit_headchunk(uidx + 2, 4)
                emit_fill(uidx, 1)
                if has_next:
                    emit_headchunk(uidx + 2, 5)
                emit_fill(uidx, 2)
                if has_next:
                    emit_headchunk(uidx + 2, 6)
                emit_fill(uidx, 3)
                if has_next:
                    emit_headchunk(uidx + 2, 7)

            def emit_avgroup(qt):
                nonlocal lamT_free, pSm_free, pAV_free, etA_free, ctx_free2
                u0 = qt * 2
                lamT_done = pend[u0 + 1]["lamT_done"]
                et_done = pend[u0 + 1]["et_done"]
                qs = slice(qt * P, (qt + 1) * P)
                p.op("tensor", lambda e: e.matmul(
                    pSm[0:64, 256:384], s_ones[0:1, 0:64], s_lamT[0][0:1, :],
                    start=True, stop=True, tile_position=(0, 0)),
                    waits=[("V", lamT_done), ("V", pSm_free)])
                p.op("tensor", lambda e: e.matmul(
                    pSm[64:128, 256:384], s_ones[0:1, 0:64], s_lamT[1][0:1, :],
                    start=True, stop=True, tile_position=(0, 64)),
                    inc=("T", 1))
                tlb = p.at("T")
                lamT_free = tlb
                p.op("vector", lambda e: e.tensor_copy(out=s_lamB[:, :], in_=pSm[:, 256:384]),
                     waits=[("T", tlb)], inc=("V", 1))
                vlamB = p.at("V")
                pSm_free = vlamB
                for sc in range(N_QT):
                    p.op("tensor", lambda e, sc=sc: e.matmul(
                        pAV[0:64, :], s_vb[0][:, sc * D:(sc + 1) * D],
                        s_et[0][:, sc * P:(sc + 1) * P],
                        start=(sc == 0), stop=(sc == N_QT - 1), tile_position=(0, 0)),
                        waits=[("V", et_done), ("V", vb_evac[0]), ("V", vb_evac[1]),
                               ("V", pAV_free)] if sc == 0 else [])
                    p.op("tensor", lambda e, sc=sc: e.matmul(
                        pAV[64:128, :], s_vb[1][:, sc * D:(sc + 1) * D],
                        s_et[1][:, sc * P:(sc + 1) * P],
                        start=(sc == 0), stop=(sc == N_QT - 1), tile_position=(0, 64)),
                        inc=("T", 1) if sc == N_QT - 1 else None)
                tav = p.at("T")
                etA_free = [tav, tav]
                cbuf = s_ctx2[qt % 2]
                p.op("vector", lambda e, cbuf=cbuf: e.tensor_tensor(
                    out=cbuf[:, :], in0=pAV[:, :], in1=s_lamB[:, :], op=ALU.mult),
                    waits=[("T", tav), ("V", vlamB), ("dG", ctx_free2[qt % 2])], inc=("V", 1))
                pAV_free = p.at("V")
                p.dma("gpsimd", ctxD[pair * P:(pair + 1) * P, qs], cbuf[:, :],
                      waits=[("V", pAV_free)], inc=("dG", 16))
                ctx_free2[qt % 2] = p.at("dG")

            for sc in range(8):
                emit_headchunk(0, sc)
            for sc in range(8):
                emit_headchunk(1, sc)
            for uidx in range(NU):
                emit_tailzip(uidx)
                qt, i = units[uidx]
                if i == 1:
                    emit_avgroup(qt)
                if uidx - 2 >= 0:
                    pend.pop(uidx - 2, None)

            head_cons = [last_dots_T[0], last_dots_T[1]]

        s2 = p.snap()
        p.emit()

    with contextlib.ExitStack() as st3:
        e3 = st3.enter_context
        s_w1 = e3(nc.sbuf_tensor("s_w1", [P, 8 * HID], BF16))
        s_w2 = e3(nc.sbuf_tensor("s_w2", [P, 32 * DM], BF16))
        s_wo = e3(nc.sbuf_tensor("s_wo", [P, 8 * DM], BF16))
        s_xn3 = e3(nc.sbuf_tensor("s_xn3", [P, 8 * QG3], BF16))
        s_cx3 = e3(nc.sbuf_tensor("s_cx3", [P, 16 * QG3], BF16))
        s_x3 = e3(nc.sbuf_tensor("s_x3", [P, 8 * QG3], F32))
        s_hM = e3(nc.sbuf_tensor("s_hM", [P, 32 * QG3], BF16))
        s_hT = e3(nc.sbuf_tensor("s_hT", [P, 2 * QG3], BF16))
        s_mr = e3(nc.sbuf_tensor("s_mr", [1, NE * QG3], BF16))
        s_mB = [e3(nc.sbuf_tensor(f"s_mB{i}", [P, QG3], BF16)) for i in range(2)]
        s_sS = e3(nc.sbuf_tensor("s_sS", [P, 8 * QG3], F32))
        s_g3 = e3(nc.sbuf_tensor("s_g3", [P, 2 * QG3], F32))
        s_o3 = e3(nc.sbuf_tensor("s_o3", [P, 2 * QG3], F32))
        pS3 = [e3(nc.psum_tensor(f"pS3{i}", [P, 512], F32)) for i in range(2)]
        pH3 = [e3(nc.psum_tensor(f"pH3{i}", [P, 512], F32)) for i in range(2)]
        pY3 = [e3(nc.psum_tensor(f"pY3{i}", [P, 512], F32)) for i in range(2)]
        pM3 = e3(nc.psum_tensor("pM3", [P, 512], F32))

        bar2 = [("V", s2["V"]), ("A", s2["A"]), ("T", s2["T"]), ("dG", s2["dG"])]

        for ci in range(8):
            p.dma("sync", s_w1[:, ci * HID:(ci + 1) * HID], w1b[ci * P:(ci + 1) * P, :],
                  waits=bar2 if ci == 0 else [], inc=("dS", 16))
            p.dma("sync", s_wo[:, ci * DM:(ci + 1) * DM], wob[ci * P:(ci + 1) * P, :], inc=("dS", 16))
        for hc in range(32):
            p.dma("sync", s_w2[:, hc * DM:(hc + 1) * DM], w2b[hc * P:(hc + 1) * P, :], inc=("dS", 16))
        s3w = p.at("dS")

        sfree = [0, 0]
        mfree = [0, 0]
        mB_done = [0, 0]
        hfree = [0, 0]
        hTfree = [0, 0]
        hM_free = 0
        yfree = [0, 0]
        gfree = [0, 0]
        ofree = [0, 0]

        prev3 = dict(s2)
        pp3 = dict(s2)
        hdone_prev = s2["T"]
        for g3 in range(NQG3):
            cs = slice(g3 * QG3, (g3 + 1) * QG3)
            lo = (g3 % 2) * 8 * QG3
            wprev = [("T", pp3["T"]), ("A", pp3["A"]), ("V", pp3["V"]),
                     ("dG", pp3["dG"]), ("dS", s3w)]
            for ci in range(8):
                p.dma("sync", s_cx3[:, lo + ci * QG3:lo + (ci + 1) * QG3], ctxD[ci * P:(ci + 1) * P, cs],
                      waits=wprev if ci == 0 else [], inc=("dS", 16))
            for ci in range(8):
                p.dma("sync", s_xn3[:, ci * QG3:(ci + 1) * QG3], xnD[ci * P:(ci + 1) * P, cs],
                      waits=[("T", hdone_prev)] if ci == 0 else [], inc=("dS", 16))
            for e_ in range(NE):
                p.dma("sync", s_mr[0:1, e_ * QG3:(e_ + 1) * QG3], maskD[e_:e_ + 1, cs],
                      inc=("dS", 16))
            for ci in range(8):
                p.dma("sync", s_x3[:, ci * QG3:(ci + 1) * QG3], xT[ci * P:(ci + 1) * P, cs],
                      waits=[("V", prev3["V"])] if ci == 0 else [], inc=("dS", 16))
            ld3 = p.at("dS")

            for ofc in range(8):
                for hd in range(8):
                    p.op("tensor", lambda e, ofc=ofc, hd=hd, lo=lo: e.matmul(
                        pS3[ofc % 2][:, 0:QG3],
                        s_wo[:, hd * DM + ofc * P: hd * DM + ofc * P + P],
                        s_cx3[:, lo + hd * QG3:lo + (hd + 1) * QG3],
                        start=(hd == 0), stop=(hd == 7)),
                        waits=[("dS", ld3), ("A", sfree[ofc % 2])] if hd == 0 else [],
                        inc=("T", 1) if hd == 7 else None)
                tpr = p.at("T")
                p.op("scalar", lambda e, ofc=ofc: e.activation(
                    s_sS[:, ofc * QG3:(ofc + 1) * QG3], pS3[ofc % 2][:, 0:QG3],
                    AF.Identity, bias=s_bo[:, ofc:ofc + 1]),
                    waits=[("T", tpr), ("V", gfree[0]), ("V", gfree[1])], inc=("A", 1))
                sfree[ofc % 2] = p.at("A")
            aS = p.at("A")

            for hc in range(32):
                ex = hc // 2
                if hc % 2 == 0:
                    p.op("tensor", lambda e, ex=ex: e.matmul(
                        pM3[:, 0:QG3], s_onesbf[0:1, 0:P],
                        s_mr[0:1, ex * QG3:(ex + 1) * QG3],
                        start=True, stop=True),
                        waits=[("dS", ld3), ("V", mfree[ex % 2])],
                        inc=("T", 1))
                    tmb = p.at("T")
                    p.op("vector", lambda e, ex=ex: e.tensor_copy(
                        out=s_mB[ex % 2][:, :], in_=pM3[:, 0:QG3]),
                        waits=[("T", tmb)], inc=("V", 1))
                    mB_done[ex % 2] = p.at("V")
                for ci in range(8):
                    p.op("tensor", lambda e, hc=hc, ci=ci: e.matmul(
                        pH3[hc % 2][:, 0:QG3],
                        s_w1[:, ci * HID + hc * P: ci * HID + hc * P + P],
                        s_xn3[:, ci * QG3:(ci + 1) * QG3],
                        start=(ci == 0), stop=(ci == 7)),
                        waits=[("dS", ld3), ("A", hfree[hc % 2])] if ci == 0 else [],
                        inc=("T", 1) if ci == 7 else None)
                thc = p.at("T")
                p.op("scalar", lambda e, hc=hc: e.activation(
                    s_hT[:, (hc % 2) * QG3:(hc % 2 + 1) * QG3], pH3[hc % 2][:, 0:QG3],
                    AF.Relu, bias=s_b1[:, hc:hc + 1]),
                    waits=[("T", thc), ("V", hTfree[hc % 2])], inc=("A", 1))
                hfree[hc % 2] = p.at("A")
                p.op("vector", lambda e, hc=hc, ex=ex: e.tensor_tensor(
                    out=s_hM[:, hc * QG3:(hc + 1) * QG3],
                    in0=s_hT[:, (hc % 2) * QG3:(hc % 2 + 1) * QG3],
                    in1=s_mB[ex % 2][:, :], op=ALU.mult),
                    waits=[("A", hfree[hc % 2]), ("V", mB_done[ex % 2]),
                           ("T", hM_free)], inc=("V", 1))
                hTfree[hc % 2] = p.at("V")
                mfree[ex % 2] = p.at("V")
            vH = p.at("V")
            hdone_prev = p.at("T")

            for ofc in range(8):
                for hc in range(32):
                    p.op("tensor", lambda e, ofc=ofc, hc=hc: e.matmul(
                        pY3[ofc % 2][:, 0:QG3],
                        s_w2[:, hc * DM + ofc * P: hc * DM + ofc * P + P],
                        s_hM[:, hc * QG3:(hc + 1) * QG3],
                        start=(hc == 0), stop=(hc == 31)),
                        waits=[("V", vH), ("V", yfree[ofc % 2])] if hc == 0 else [],
                        inc=("T", 1) if hc == 31 else None)
                tyc = p.at("T")
                gb = s_g3[:, (ofc % 2) * QG3:(ofc % 2 + 1) * QG3]
                ob = s_o3[:, (ofc % 2) * QG3:(ofc % 2 + 1) * QG3]
                p.op("scalar", lambda e, gb=gb, ofc=ofc: e.activation(
                    gb, pY3[ofc % 2][:, 0:QG3], AF.Silu, bias=s_b2[:, ofc:ofc + 1]),
                    waits=[("T", tyc), ("V", gfree[ofc % 2])], inc=("A", 1))
                ag = p.at("A")
                p.op("vector", lambda e, gb=gb, ofc=ofc: e.tensor_tensor(
                    out=gb, in0=gb, in1=s_sS[:, ofc * QG3:(ofc + 1) * QG3], op=ALU.mult),
                    waits=[("A", ag), ("A", aS)], inc=("V", 1))
                p.op("vector", lambda e, gb=gb, ob=ob, ofc=ofc: e.tensor_tensor(
                    out=ob, in0=gb, in1=s_x3[:, ofc * QG3:(ofc + 1) * QG3], op=ALU.add),
                    waits=[("V", p.at("V")), ("dG", ofree[ofc % 2])], inc=("V", 1))
                vout = p.at("V")
                gfree[ofc % 2] = vout
                yfree[ofc % 2] = vout
                p.dma("gpsimd", outT[ofc * P:(ofc + 1) * P, cs], ob,
                      waits=[("V", vout)], inc=("dG", 16))
                ofree[ofc % 2] = p.at("dG")
            hM_free = p.at("T")
            pp3 = prev3
            prev3 = p.snap()

        p.op("gpsimd", lambda e: e.memset(s_g3[0:1, 0:1], 0.0),
             waits=[("dG", p.at("dG"))])
        p.emit()

    es.close()
    p._es.close()
    return nc


def _host_layout(inputs):
    f32 = np.float32
    shared = {}
    shared["wq"] = np.ascontiguousarray(inputs["L0_attn_wq"], dtype=f32)
    shared["wk"] = np.ascontiguousarray(inputs["L0_attn_wk"], dtype=f32)
    shared["wv"] = np.ascontiguousarray(inputs["L0_attn_wv"], dtype=f32)
    shared["wo"] = np.ascontiguousarray(inputs["L0_attn_wo"], dtype=f32)
    shared["w1T"] = np.ascontiguousarray(np.asarray(inputs["L0_ffn_w1"], dtype=f32).T)
    shared["w2T"] = np.ascontiguousarray(np.asarray(inputs["L0_ffn_w2"], dtype=f32).T)
    shared["wr"] = np.ascontiguousarray(inputs["L0_ffn_w_router"], dtype=f32)
    shared["brow"] = np.ascontiguousarray(np.asarray(inputs["L0_ffn_b_router"], dtype=f32).reshape(1, NE))
    for name, key in [("bq", "L0_attn_bq"), ("bk", "L0_attn_bk"), ("bv", "L0_attn_bv"),
                      ("bo", "L0_attn_bo"), ("b2c", "L0_ffn_b2")]:
        shared[name] = np.ascontiguousarray(np.asarray(inputs[key], dtype=f32).reshape(8, P).T)
    shared["b1c"] = np.ascontiguousarray(np.asarray(inputs["L0_ffn_b1"], dtype=f32).reshape(32, P).T)
    shared["alpha"] = np.asarray(inputs["L0_attn_alpha"], dtype=f32).reshape(1, 1)
    shared["cIbf"] = np.eye(P, dtype=f32).astype(ml_dtypes.bfloat16)
    shared["cIf"] = np.eye(P, dtype=f32)
    shared["cones"] = np.ones((1, P), dtype=f32)
    shared["conesbf"] = np.ones((P, QG), dtype=f32).astype(ml_dtypes.bfloat16)
    x = np.asarray(inputs["L0_attn_x"], dtype=f32)
    in_maps = []
    for b in range(NCORE):
        m = dict(shared)
        m["xT"] = np.ascontiguousarray(x[b].T)
        in_maps.append(m)
    return in_maps


def kernel(**inputs):
    in_maps = _host_layout(inputs)
    nc = build_program()
    res = run_bass_kernel_spmd(nc, in_maps, core_ids=list(range(NCORE)))
    out = np.empty((NCORE, T, DM), dtype=np.float32)
    for b in range(NCORE):
        out[b] = np.asarray(res.results[b]["outT"], dtype=np.float32).T
    return out
